# version 10
# baseline (speedup 1.0000x reference)
import os
import numpy as np
import concourse.bass as bass
import concourse.mybir as mybir
from concourse.bass_utils import run_bass_kernel_spmd
from contextlib import ExitStack

F32 = mybir.dt.float32
BF16 = mybir.dt.bfloat16
AF = mybir.ActivationFunctionType
ALU = mybir.AluOpType
AX = mybir.AxisListType

NCORE = 8
T = 8192
D = 4096
TC = 1024
BLK = 512
NB = 16
NH = 32
HD = 128
NHL = 4
NE = 8
LR = 16
NEG = -1e30
EPS_LN = 1e-5
EPS_RMS = 1e-6
TT = 256
NTT = T // TT
DCH = 32
STAGES = int(os.environ.get("MK_STAGES", "4"))
TAPS = [t for t in os.environ.get("MK_TAPS", "").split(",") if t]


def blk_owner(b):
    return (b, 0) if b < 8 else (15 - b, 1)


def gcol(b):
    r, h = blk_owner(b)
    return 1024 * r + 512 * h



class Sem:
    def __init__(self, handle, name):
        self.h = handle
        self.name = name
        self.count = 0


class Buf:
    __slots__ = ("name", "writer", "readers")

    def __init__(self, name):
        self.name = name
        self.writer = None
        self.readers = []


class Qu:
    def __init__(self, name, sem, definc):
        self.name = name
        self.sem = sem
        self.definc = definc
        self.items = []
        self.waited = {}
        self.pending = []

    def _wait(self, sem, val):
        if val is None:
            raise AssertionError(f"{self.name}: unresolved pending event")
        if val <= 0:
            return
        if val > sem.count:
            raise AssertionError(
                f"{self.name}: wait on future count of {sem.name}")
        self._wait_emit(sem, val)

    def _wait_emit(self, sem, val):
        if self.waited.get(id(sem), -1) >= val:
            return
        self.waited[id(sem)] = val
        self.items.append(("w", sem, val))

    def _ev(self, ev):
        if ev is None:
            return
        sem, val = ev[0], ev[1]
        if isinstance(ev, list) and val is None:
            if sem is self.sem:
                return
            self.items.append(("wd", sem, ev))
            return
        self._wait(sem, val)

    def op(self, fn, reads=(), writes=(), inc=None, slot=None):
        for b in reads:
            if b is not None and b.writer is not None:
                self._ev(b.writer)
        for b in writes:
            if b is None:
                continue
            if b.writer is not None:
                self._ev(b.writer)
            for ev in b.readers:
                self._ev(ev)
        inc = self.definc if inc is None else inc
        if self.definc == 16:
            assert slot is not None, f"{self.name}: DMA op needs slot sem"
            sem = slot
        else:
            sem = self.sem
        if inc:
            sem.count += inc
            ev = (sem, sem.count)
            if sem is self.sem:
                for pb in self.pending:
                    pb[1] = self.sem.count
                self.pending = []
        else:
            ev = [self.sem, None]
            self.pending.append(ev)
        self.items.append(("o", fn, inc, sem))
        for b in writes:
            if b is not None:
                b.writer = ev
                b.readers = []
        for b in reads:
            if b is not None:
                b.readers.append(ev)
        return ev

    def flush(self, eng):
        for it in self.items:
            if it[0] == "w":
                _, sem, val = it
                eng.wait_ge(sem.h, val)
            elif it[0] == "wd":
                _, sem, ev = it
                assert ev[1] is not None, f"{self.name}: unresolved deferred wait"
                eng.wait_ge(sem.h, ev[1])
            else:
                _, fn, inc, sem = it
                inst = fn(eng)
                if inc:
                    inst.then_inc(sem.h, inc)


class BK:
    def __init__(self):
        nc = bass.Bass("TRN2", target_bir_lowering=False)
        self.nc = nc
        self.es = ExitStack()
        self.queues = {}
        for qn, definc in [("pe", 1), ("dve", 1), ("act", 1),
                           ("qs", 16), ("qg", 16)]:
            s = Sem(self.es.enter_context(nc.semaphore("s_" + qn)), qn)
            self.queues[qn] = Qu(qn, s, definc)
        self.pe = self.queues["pe"]
        self.dve = self.queues["dve"]
        self.act = self.queues["act"]
        self.qs = self.queues["qs"]
        self.qg = self.queues["qg"]
        self.ccs = []
        self.slotsems = {}
        self.np_inputs = {}

    def slot(self, buf):
        s = self.slotsems.get(id(buf))
        if s is None:
            s = Sem(self.es.enter_context(
                self.nc.semaphore(f"sl_{buf.name}")), f"sl_{buf.name}")
            self.slotsems[id(buf)] = s
        return s

    def sbuf(self, name, shape, dtype):
        h = self.es.enter_context(self.nc.sbuf_tensor(name, list(shape), dtype))
        return h, Buf(name)

    def dram_in(self, name, shape, dtype=F32):
        h = self.nc.dram_tensor(name, list(shape), dtype, kind="ExternalInput")
        return h, Buf(name)

    def dram_out(self, name, shape, dtype=F32):
        h = self.nc.dram_tensor(name, list(shape), dtype, kind="ExternalOutput")
        return h, Buf(name)

    def dram(self, name, shape, dtype):
        h = self.nc.dram_tensor(name, list(shape), dtype, kind="Internal")
        return h, Buf(name)

    def psum(self, name):
        h = self.es.enter_context(self.nc.psum_tensor(name, [128, 512], F32))
        return h, Buf(name)

    def ccsem(self):
        s = Sem(self.es.enter_context(
            self.nc.semaphore(f"cc{len(self.ccs)}")), f"cc{len(self.ccs)}")
        self.ccs.append(s)
        return s

    def finish(self):
        with self.nc.Block() as block:
            block.tensor(lambda e: self.queues["pe"].flush(e))
            block.vector(lambda e: self.queues["dve"].flush(e))
            block.scalar(lambda e: self.queues["act"].flush(e))
            block.sync(lambda e: self.queues["qs"].flush(e))
            block.gpsimd(lambda e: self.queues["qg"].flush(e))
        self.es.close()
        return self.nc



def host_consts():
    c = {}
    gpos = np.zeros(T, np.int64)
    for b in range(NB):
        gpos[gcol(b):gcol(b) + BLK] = np.arange(b * BLK, (b + 1) * BLK)
    pos = gpos.astype(np.float32)
    c["cosrow"] = np.cos(pos)[None, :]
    c["sinrow"] = np.sin(pos)[None, :]
    c["colones"] = np.ones((1, 128), np.float32)
    c["colscale"] = np.full((1, 128), float(HD) ** -0.5, np.float32)
    c["colneg"] = np.full((1, 128), -1.0, np.float32)
    c["iotacol"] = np.arange(128, dtype=np.float32)[None, :]
    c["idn"] = np.eye(128, dtype=np.float32)
    J = np.zeros((128, 128), np.float32)
    for i in range(64):
        J[2 * i, 2 * i + 1] = -1.0
        J[2 * i + 1, 2 * i] = 1.0
    c["jT"] = np.ascontiguousarray(J.T)
    i_ = np.arange(128)[:, None]
    j_ = np.arange(512)[None, :]
    jj = np.arange(128)[None, :]
    c["AM"] = np.where(jj <= i_, 0.0, NEG).astype(np.float32)
    c["PM"] = np.concatenate(
        [(128 * o + i_ <= j_).astype(np.float32) for o in range(4)],
        axis=1)
    c["NEGJ"] = np.broadcast_to(-j_.astype(np.float32), (128, 512)).copy()
    c["NEGJM"] = np.concatenate(
        [np.where(128 * o + i_ <= j_, -j_.astype(np.float32), NEG)
         for o in range(4)], axis=1).astype(np.float32)
    c["onescol"] = np.ones((128, 1), np.float32)
    return c


_CONSTS = host_consts()


def host_inputs(core, ins):
    hs = NHL * core
    d = {}
    x = np.asarray(ins["L0_attn_a_x"])[0]
    xT = np.ascontiguousarray(x.T)
    d["xT"] = np.ascontiguousarray(
        np.concatenate([xT[:, core * BLK:(core + 1) * BLK],
                        xT[:, (15 - core) * BLK:(16 - core) * BLK]], axis=1))
    tr = lambda a: np.ascontiguousarray(np.asarray(a).T.astype(np.float32))
    d["wqTs"] = tr(ins["L0_attn_a_wq"])[:, hs * HD:(hs + NHL) * HD]
    d["wkTs"] = tr(ins["L0_attn_a_wk"])[:, hs * HD:(hs + NHL) * HD]
    d["wvTs"] = tr(ins["L0_attn_a_wv"])[:, hs * HD:(hs + NHL) * HD]
    d["woT"] = tr(ins["L0_attn_a_wo"])
    d["bWqTs"] = tr(ins["L0_attn_b_Wq"])[:, hs * HD:(hs + NHL) * HD]
    d["bWkTs"] = tr(ins["L0_attn_b_Wk"])[:, core * HD:(core + 1) * HD]
    d["bWvTs"] = tr(ins["L0_attn_b_Wv"])[:, core * HD:(core + 1) * HD]
    d["bWoT"] = tr(ins["L0_attn_b_Wo"])
    d["WbT"] = tr(ins["L0_ffn_W_base"])
    d["WrT"] = tr(ins["L0_ffn_W_router"])
    d["AallT"] = np.ascontiguousarray(np.asarray(
        ins["L0_ffn_lora_A"]).transpose(2, 0, 1).reshape(D, NE * LR).astype(np.float32))
    d["Ball"] = np.ascontiguousarray(np.asarray(
        ins["L0_ffn_lora_B"]).transpose(0, 2, 1).reshape(NE * LR, D).astype(np.float32))
    d["bbase"] = np.ascontiguousarray(np.asarray(
        ins["L0_ffn_b_base"]).astype(np.float32).reshape(32, 128).T)
    d["qnw"] = np.asarray(ins["L0_attn_b_q_norm_w"]).astype(np.float32).reshape(128, 1)
    d["knw"] = np.asarray(ins["L0_attn_b_k_norm_w"]).astype(np.float32).reshape(128, 1)
    start = 2.0 ** (-2.0 ** (-(np.log2(NH) - 3)))
    slopes = np.array([start * start ** (h + 1 - 1) for h in range(NH)], np.float32)
    slopes = np.array([start * start ** i for i in range(NH)], np.float32)
    d["slopesrow"] = slopes[hs:hs + NHL][None, :].copy()
    sl8 = np.zeros((1, 16), np.float32)
    slk8 = np.zeros((1, 16), np.float32)
    for hp in range(2):
        for ko in range(4):
            for h in range(2):
                sl8[0, 8 * hp + 2 * ko + h] = slopes[hs + 2 * hp + h]
                slk8[0, 8 * hp + 2 * ko + h] = slopes[hs + 2 * hp + h] * 128.0 * ko
    d["SL8"] = sl8
    d["SLK8"] = slk8
    d["slopecols"] = np.broadcast_to(slopes[hs:hs + NHL][None, :],
                                     (128, NHL)).copy()
    d.update(_CONSTS)
    return {k: np.ascontiguousarray(v) for k, v in d.items()}


def reassemble(outs):
    full = np.empty((D, T), np.float32)
    for c in range(NCORE):
        y = outs[c]
        full[:, c * BLK:(c + 1) * BLK] = y[:, :BLK]
        full[:, (15 - c) * BLK:(16 - c) * BLK] = y[:, BLK:]
    return np.ascontiguousarray(full.T).reshape(1, T, D)



class Prog:
    def __init__(self, k: BK):
        self.k = k
        nc = k.nc
        di = k.dram_in
        self.xT = di("xT", [D, TC])
        self.wqTs = di("wqTs", [D, NHL * HD])
        self.wkTs = di("wkTs", [D, NHL * HD])
        self.wvTs = di("wvTs", [D, NHL * HD])
        self.woT = di("woT", [D, D])
        self.bWqTs = di("bWqTs", [D, NHL * HD])
        self.bWkTs = di("bWkTs", [D, HD])
        self.bWvTs = di("bWvTs", [D, HD])
        self.bWoT = di("bWoT", [D, D])
        self.WbT = di("WbT", [D, D])
        self.WrT = di("WrT", [D, NE])
        self.AallT = di("AallT", [D, NE * LR])
        self.Ball = di("Ball", [NE * LR, D])
        self.bbase = di("bbase", [128, 32])
        self.qnw = di("qnw", [128, 1])
        self.knw = di("knw", [128, 1])
        self.slopesrow = di("slopesrow", [1, NHL])
        self.SL8_in = di("SL8_in", [1, 16])
        self.SLK8_in = di("SLK8_in", [1, 16])
        self.slopecols = di("slopecols", [128, NHL])
        self.cosbc = di("cosbc", [128, T])
        self.sinbc = di("sinbc", [128, T])
        self.colones = di("colones", [1, 128])
        self.colneg = di("colneg", [1, 128])
        self.iotacol = di("iotacol", [1, 128])
        self.idn_in = di("idn_in", [128, 128])
        self.jT_in = di("jT_in", [128, 128])
        self.AM_in = di("AM_in", [128, 128])
        self.PM_in = di("PM_in", [128, 4 * 512])
        self.NEGJ_in = di("NEGJ_in", [128, 512])
        self.NEGJM_in = di("NEGJM_in", [128, 4 * 512])
        self.onescol_in = di("onescol_in", [128, 1])
        self.REP_in = di("REP_in", [NE, 128])
        self.yout = k.dram_out("yout", [D, TC], F32)
        dr = k.dram
        self.xnsh1 = dr("xnsh1", [D, TC], BF16)
        self.XN1 = dr("XN1", [NCORE * D, TC], BF16)
        self.xnsh3 = dr("xnsh3", [D, TC], BF16)
        self.XN3 = dr("XN3", [NCORE * D, TC], BF16)
        self.qa = dr("qa", [NHL * HD, T], BF16)
        self.ka = dr("ka", [NHL * HD, T], BF16)
        self.va = dr("va", [T, NHL * HD], BF16)
        self.osh_a = [dr(f"osh_a{h}", [NCORE * BLK, BLK], BF16) for h in range(2)]
        self.OG_a = [dr(f"OG_a{h}", [NCORE * BLK, BLK], BF16) for h in range(2)]
        self.qb = dr("qb", [NHL * HD, T], BF16)
        self.kb = dr("kb", [HD, T], BF16)
        self.vb = dr("vb", [T, HD], BF16)
        self.osh_b = [dr(f"osh_b{h}", [NCORE * BLK, BLK], BF16) for h in range(2)]
        self.OG_b = [dr(f"OG_b{h}", [NCORE * BLK, BLK], BF16) for h in range(2)]
        self.y1 = dr("y1", [D, TC], F32)
        self.y2 = dr("y2", [D, TC], F32)
        self.y3 = dr("y3", [D, TC], F32)
        sb = k.sbuf
        self.xng = [sb(f"xng{i}", [128, DCH * TT], BF16) for i in range(2)]
        self.wsl = [sb(f"wsl{i}", [128, DCH * 256], BF16) for i in range(2)]
        self.rhs2 = [sb(f"rhs2{i}", [128, DCH * 256], BF16) for i in range(2)]
        self.hwT = sb("hwT", [128, TC], BF16)
        self.wt16sb = sb("wt16sb", [128, 512], F32)
        self.Ballsb = sb("Ballsb", [128, D], BF16)
        self.wrssb = sb("wrssb", [128, DCH * NE], BF16)
        self.wTar = sb("wTar", [NE, 512], F32)
        self.lg = sb("lg", [NE, 512], F32)
        self.rsm = [sb(f"rsm{i}", [128, NE], F32) for i in range(6)]
        self.rcol = [sb(f"rcol{i}", [128, 1], F32) for i in range(4)]
        self.qt = [sb(f"qt{i}", [128, 2 * 512], BF16) for i in range(1)]
        self.kt = [sb(f"kt{i}", [128, 2 * 512], BF16) for i in range(2)]
        self.vt = [sb(f"vt{i}", [128, 4 * 128 * 2], BF16) for i in range(2)]
        self.pbuf = [sb(f"pbuf{i}", [128, 512], BF16) for i in range(4)]
        self.sscr = [sb(f"sscr{i}", [128, 512], F32) for i in range(3)]
        self.kna = sb("kna", [1, NHL * 32], F32)
        self.qna = sb("qna", [1, NHL * 32], F32)
        self.crow = sb("crow", [1, 2], F32)
        self.cbneg = sb("cbneg", [128, 2], F32)
        self.bmsb = sb("bmsb", [128, 2], F32)
        self.SL8 = sb("SL8", [1, 16], F32)
        self.SLK8 = sb("SLK8", [1, 16], F32)
        self.crow8 = sb("crow8", [1, 8], F32)
        self.bm8 = [sb(f"bm8{i}", [128, 8], F32) for i in range(2)]
        self.rowtmp = sb("rowtmp", [1, 8], F32)
        self.invlb = sb("invlb", [128, 512], F32)
        self.otst = [sb(f"otst{i}", [128, 512], BF16) for i in range(2)]
        self.cbt = [sb(f"cbt{i}", [128, TT], F32) for i in range(2)]
        self.sbt = [sb(f"sbt{i}", [128, TT], F32) for i in range(2)]
        self.ust = [sb(f"ust{i}", [128, TT], BF16) for i in range(1)]
        self.vscr = [sb(f"vscr{i}", [128, TT], F32) for i in range(2)]
        self.qkst = [sb(f"qkst{i}", [128, 512], BF16) for i in range(2)]
        self.sq256 = [sb(f"sq256{i}", [128, TT], F32) for i in range(1)]
        self.rmrow = sb("rmrow", [1, 2 * TT], F32)
        self.invbc = sb("invbc", [128, TT], F32)
        self.ybf = [sb(f"ybf{i}", [128, TC], BF16) for i in range(2)]
        self.sqt = [sb(f"sqt{i}", [128, TC], BF16) for i in range(1)]
        self.rows = sb("rows", [1, 3 * TC], F32)
        self.invb = sb("invb", [128, TC], BF16)
        self.minvb = sb("minvb", [128, TC], BF16)
        self.xnst = [sb(f"xnst{i}", [128, TC], BF16) for i in range(2)]
        self.rtile = [sb(f"rtile{i}", [128, 512], F32) for i in range(2)]
        self.ytile = [sb(f"ytile{i}", [128, 512], F32) for i in range(2)]
        self.yc = [sb(f"yc{i}", [128, 512], BF16) for i in range(2)]
        self.sqc = [sb(f"sqc{i}", [128, 512], BF16) for i in range(2)]
        self.idn = sb("idn", [128, 128], F32)
        self.jTb = sb("jTb", [128, 128], BF16)
        self.AMt = sb("AMt", [128, 128], F32)
        self.PMt = sb("PMt", [128, 4 * 512], BF16)
        self.NEGJt = sb("NEGJt", [128, 512], F32)
        self.NEGJMt = sb("NEGJMt", [128, 4 * 512], F32)
        self.slc = sb("slc", [128, NHL], F32)
        self.slr = sb("slr", [1, NHL], F32)
        self.c1 = sb("c1", [1, 128], F32)
        self.cn1 = sb("cn1", [1, 128], F32)
        self.iotar = sb("iotar", [1, 128], F32)
        self.ones_f = sb("ones_f", [128, 1], F32)
        self.ones_b = sb("ones_b", [128, 1], BF16)
        self.bb = sb("bb", [128, 32], F32)
        self.qnwsb = sb("qnwsb", [128, 1], F32)
        self.knwsb = sb("knwsb", [128, 1], F32)
        self.REPsb = sb("REPsb", [NE, 128], F32)
        self.B = [k.psum(f"bank{i}") for i in range(8)]

    def mm(self, bank, out_ap, lhsT, rhs, reads, start, stop, inc=True):
        self.k.pe.op(
            lambda e, o=out_ap, l=lhsT, r=rhs, s=start, t=stop: e.matmul(
                o, l, r, start=s, stop=t, skip_group_check=True),
            reads=reads, writes=[bank[1]], inc=1 if inc else 0)

    def transpose(self, bank, out_ap, in_ap, reads, p=128):
        self.k.pe.op(
            lambda e, o=out_ap, i=in_ap, p=p: e.transpose(o, i, self.idn[0][0:p, 0:p]),
            reads=list(reads) + [self.idn[1]], writes=[bank[1]], inc=1)

    def sqrt_row(self, t, n=128):
        self.k.act.op(lambda e, o=t[0][0:1, 0:n], x=t[0][0:1, 0:n]:
                      e.activation(o, x, AF.Sqrt), reads=[t[1]], writes=[t[1]])

    def load_consts(self):
        k = self.k
        qs, qg = k.qs, k.qg

        def ld(dst, dstbuf, src, srcbuf, cast=False):
            q = qg if cast else qs
            q.op(lambda e, d=dst, s=src: e.dma_start(out=d, in_=s),
                 reads=[srcbuf], writes=[dstbuf], slot=self.k.slot(dstbuf))

        ld(self.idn[0][:, :], self.idn[1], self.idn_in[0].ap(), self.idn_in[1])
        ld(self.jTb[0][:, :], self.jTb[1], self.jT_in[0].ap(), self.jT_in[1], cast=True)
        ld(self.AMt[0][:, :], self.AMt[1], self.AM_in[0].ap(), self.AM_in[1])
        ld(self.PMt[0][:, :], self.PMt[1], self.PM_in[0].ap(), self.PM_in[1], cast=True)
        ld(self.NEGJt[0][:, :], self.NEGJt[1], self.NEGJ_in[0].ap(), self.NEGJ_in[1])
        ld(self.NEGJMt[0][:, :], self.NEGJMt[1], self.NEGJM_in[0].ap(), self.NEGJM_in[1])
        ld(self.slc[0][:, :], self.slc[1], self.slopecols[0].ap(), self.slopecols[1])
        ld(self.slr[0][:, :], self.slr[1], self.slopesrow[0].ap(), self.slopesrow[1])
        ld(self.SL8[0][:, :], self.SL8[1], self.SL8_in[0].ap(), self.SL8_in[1])
        ld(self.SLK8[0][:, :], self.SLK8[1], self.SLK8_in[0].ap(), self.SLK8_in[1])
        ld(self.c1[0][:, :], self.c1[1], self.colones[0].ap(), self.colones[1])
        ld(self.cn1[0][:, :], self.cn1[1], self.colneg[0].ap(), self.colneg[1])
        ld(self.iotar[0][:, :], self.iotar[1], self.iotacol[0].ap(), self.iotacol[1])
        ld(self.ones_f[0][:, :], self.ones_f[1], self.onescol_in[0].ap(), self.onescol_in[1])
        ld(self.ones_b[0][:, :], self.ones_b[1], self.onescol_in[0].ap(), self.onescol_in[1], cast=True)
        ld(self.bb[0][:, :], self.bb[1], self.bbase[0].ap(), self.bbase[1])
        ld(self.qnwsb[0][:, :], self.qnwsb[1], self.qnw[0].ap(), self.qnw[1])
        ld(self.knwsb[0][:, :], self.knwsb[1], self.knw[0].ap(), self.knw[1])
        ld(self.REPsb[0][:, :], self.REPsb[1], self.REP_in[0].ap(), self.REP_in[1])
        ld(self.wrssb[0][:, :].rearrange("p (c e) -> p c e", c=DCH), self.wrssb[1],
           self.WrT[0].ap().rearrange("(c p) e -> p c e", p=128), self.WrT[1], cast=True)
        ld(self.Ballsb[0][:, :], self.Ballsb[1], self.Ball[0].ap(), self.Ball[1], cast=True)

    def ln_stats_scan(self, src):
        k = self.k

        def loads(c):
            yb = self.ybf[c % 2]
            k.qg.op(lambda e, d=yb[0][:, :], s=src[0].ap()[128 * c:128 * (c + 1), :]:
                    e.dma_start(out=d, in_=s), reads=[src[1]], writes=[yb[1]],
                    slot=k.slot(yb[1]))

        loads(0)
        for c in range(DCH):
            i = c % 2
            yb = self.ybf[i]
            if c + 1 < DCH:
                loads(c + 1)
            sq = self.sqt[0]
            k.act.op(lambda e, o=sq[0][:, :], x=yb[0][:, :]:
                     e.activation(o, x, AF.Square), reads=[yb[1]], writes=[sq[1]])
            for g in range(2):
                bS, bQ = self.B[2 + g], self.B[4 + g]
                self.mm(bS, bS[0][0:1, 0:512], self.ones_b[0][:, :],
                        yb[0][:, 512 * g:512 * (g + 1)],
                        [self.ones_b[1], yb[1]], start=(c == 0), stop=(c == DCH - 1))
                self.mm(bQ, bQ[0][0:1, 0:512], self.ones_b[0][:, :],
                        sq[0][:, 512 * g:512 * (g + 1)],
                        [self.ones_b[1], sq[1]], start=(c == 0), stop=(c == DCH - 1))

    def ln_rows_bcast(self, eps=EPS_LN):
        k = self.k
        R = self.rows[0]
        rb = self.rows[1]
        for g in range(2):
            M = R[0:1, 512 * g:512 * (g + 1)]
            A = R[0:1, TC + 512 * g: TC + 512 * (g + 1)]
            Bs = R[0:1, 2 * TC + 512 * g: 2 * TC + 512 * (g + 1)]
            bS, bQ = self.B[2 + g], self.B[4 + g]
            k.dve.op(lambda e, o=M, x=bS[0][0:1, 0:512]: e.tensor_scalar(
                o, x, 1.0 / D, None, ALU.mult), reads=[bS[1]], writes=[rb])
            k.dve.op(lambda e, o=A, x=bQ[0][0:1, 0:512]: e.tensor_scalar(
                o, x, 1.0 / D, None, ALU.mult), reads=[bQ[1]], writes=[rb])
            k.dve.op(lambda e, o=Bs, a=M, b=M: e.tensor_tensor(o, a, b, ALU.mult),
                     reads=[rb], writes=[rb])
            k.dve.op(lambda e, o=A, a=A, b=Bs: e.tensor_tensor(o, a, b, ALU.subtract),
                     reads=[rb], writes=[rb])
            k.dve.op(lambda e, o=Bs, x=A: e.tensor_scalar(
                o, x, float(eps), None, ALU.add), reads=[rb], writes=[rb])
            k.act.op(lambda e, o=A, x=Bs: e.activation(o, x, AF.Ln),
                     reads=[rb], writes=[rb])
            k.dve.op(lambda e, o=Bs, x=A: e.tensor_scalar(
                o, x, -0.5, None, ALU.mult), reads=[rb], writes=[rb])
            k.act.op(lambda e, o=A, x=Bs: e.activation(o, x, AF.Exp),
                     reads=[rb], writes=[rb])
            k.dve.op(lambda e, o=Bs, a=M, b=A: e.tensor_tensor(o, a, b, ALU.mult),
                     reads=[rb], writes=[rb])
            INV, MINV = A, Bs
            b6, b7 = self.B[6], self.B[7]
            self.mm(b6, b6[0][:, 0:512], self.c1[0][:, :], INV,
                    [self.c1[1], rb], start=True, stop=True)
            k.act.op(lambda e, o=self.invb[0][:, 512 * g:512 * (g + 1)],
                     x=b6[0][:, 0:512]: e.activation(o, x, AF.Copy),
                     reads=[b6[1]], writes=[self.invb[1]])
            self.mm(b7, b7[0][:, 0:512], self.c1[0][:, :], MINV,
                    [self.c1[1], rb], start=True, stop=True)
            k.act.op(lambda e, o=self.minvb[0][:, 512 * g:512 * (g + 1)],
                     x=b7[0][:, 0:512]: e.activation(o, x, AF.Copy),
                     reads=[b7[1]], writes=[self.minvb[1]])

    def ln_norm_to_shard(self, src, dstsh):
        k = self.k

        def loads(c):
            yb = self.ybf[c % 2]
            k.qg.op(lambda e, d=yb[0][:, :], s=src[0].ap()[128 * c:128 * (c + 1), :]:
                    e.dma_start(out=d, in_=s), reads=[src[1]], writes=[yb[1]],
                    slot=k.slot(yb[1]))

        loads(0)
        for c in range(DCH):
            i = c % 2
            yb = self.ybf[i]
            if c + 1 < DCH:
                loads(c + 1)
            sq = self.sqt[0]
            k.dve.op(lambda e, o=sq[0][:, :], a=yb[0][:, :], b=self.invb[0][:, :]:
                     e.tensor_tensor(o, a, b, ALU.mult),
                     reads=[yb[1], self.invb[1]], writes=[sq[1]])
            xn = self.xnst[i]
            k.dve.op(lambda e, o=xn[0][:, :], a=sq[0][:, :], b=self.minvb[0][:, :]:
                     e.tensor_tensor(o, a, b, ALU.subtract),
                     reads=[sq[1], self.minvb[1]], writes=[xn[1]])
            k.qs.op(lambda e, d=dstsh[0].ap()[128 * c:128 * (c + 1), :], s=xn[0][:, :]:
                    e.dma_start(out=d, in_=s), reads=[xn[1]], writes=[dstsh[1]],
                    slot=k.slot(xn[1]))

    def ln_norm_to_half(self, src, g):
        k = self.k

        def loads(c):
            yb = self.yc[c % 2]
            k.qg.op(lambda e, d=yb[0][:, :],
                    s=src[0].ap()[128 * c:128 * (c + 1), 512 * g:512 * (g + 1)]:
                    e.dma_start(out=d, in_=s), reads=[src[1]], writes=[yb[1]],
                    slot=k.slot(yb[1]))

        loads(0)
        for c in range(DCH):
            i = c % 2
            yb = self.yc[i]
            if c + 1 < DCH:
                loads(c + 1)
            sq = self.sqc[i]
            k.dve.op(lambda e, o=sq[0][:, :], a=yb[0][:, :],
                     b=self.invb[0][:, 512 * g:512 * (g + 1)]:
                     e.tensor_tensor(o, a, b, ALU.mult),
                     reads=[yb[1], self.invb[1]], writes=[sq[1]])
            xa, xbuf = self.xn_chunk(g, c)
            k.dve.op(lambda e, o=xa,
                     a=sq[0][:, :], b=self.minvb[0][:, 512 * g:512 * (g + 1)]:
                     e.tensor_tensor(o, a, b, ALU.subtract),
                     reads=[sq[1], self.minvb[1]], writes=[xbuf])

    def collective(self, kind, shard, full, in_ap=None, out_ap=None):
        k = self.k
        cs = k.ccsem()
        q = k.qg
        if shard[1].writer is not None:
            q._ev(shard[1].writer)
        for ev in full[1].readers:
            q._ev(ev)
        if full[1].writer is not None:
            q._ev(full[1].writer)
        in_ap = shard[0].ap() if in_ap is None else in_ap
        out_ap = full[0].ap() if out_ap is None else out_ap

        def fn(e, kind=kind, si=in_ap, fo=out_ap, cs=cs):
            inst = e.collective_compute(
                kind, ALU.bypass, replica_groups=[list(range(NCORE))],
                ins=[si.opt()], outs=[fo.opt()])
            inst.then_inc(cs.h, 1)
            return inst
        q.items.append(("o", fn, 0, None))
        cs.count = 1
        full[1].writer = (cs, 1)
        full[1].readers = []
        shard[1].readers.append((cs, 1))
        return cs

    def load_wslice(self, wt, cols0, ncols, bufs, cast=True):
        k = self.k
        src = wt[0].ap()[:, cols0:cols0 + ncols].rearrange("(c p) o -> p c o", p=128)
        npieces = (ncols + 255) // 256
        q = k.qg if cast else k.qs
        nj = 4 if cast else 2
        for pc in range(npieces):
            w = min(256, ncols - 256 * pc)
            b_ = bufs[pc]
            bi = self.wsl[b_] if isinstance(b_, int) else b_
            for j in range(nj):
                sp = DCH // nj
                q.op(lambda e,
                     d=bi[0][:, :].rearrange("p (c o) -> p c o", c=DCH)[:, sp * j:sp * (j + 1), 0:w],
                     s=src[:, sp * j:sp * (j + 1), 256 * pc:256 * pc + w]:
                     e.dma_start(out=d, in_=s), reads=[wt[1]], writes=[bi[1]],
                     slot=k.slot(bi[1]))

    def qkv_scan(self, XN, streams, vstream=None):
        k = self.k
        for s_ in streams:
            self.load_wslice(s_["wt"], 0, s_["ncols"], s_["wb"])
        if vstream is not None:
            self.load_wslice(vstream["wt"], 0, vstream["ncols"], vstream["wb"])
        any_rope = any(s_["kind"] == "rope" for s_ in streams)

        def wbuf(s_, j):
            b_ = s_["wb"][j]
            return self.wsl[b_] if isinstance(b_, int) else b_

        def loads(t_):
            t0 = t_ * TT
            r = t0 // TC
            wc = t0 % TC
            i = t_ % 2
            xg = self.xng[i]
            for j in range(4):
                k.qs.op(lambda e,
                        d=xg[0][:, :].rearrange("p (c t) -> p c t", c=DCH)[:, 8 * j:8 * (j + 1), :],
                        s=XN[0].ap()[D * r + 1024 * j:D * r + 1024 * (j + 1),
                                     wc:wc + TT].rearrange("(c p) t -> p c t", p=128):
                        e.dma_start(out=d, in_=s), reads=[XN[1]], writes=[xg[1]],
                        slot=k.slot(xg[1]))
            if any_rope:
                cb, sb_ = self.cbt[i], self.sbt[i]
                k.qs.op(lambda e, d=cb[0][:, :], s=self.cosbc[0].ap()[:, t0:t0 + TT]:
                        e.dma_start(out=d, in_=s), reads=[self.cosbc[1]], writes=[cb[1]],
                        slot=k.slot(cb[1]))
                k.qs.op(lambda e, d=sb_[0][:, :], s=self.sinbc[0].ap()[:, t0:t0 + TT]:
                        e.dma_start(out=d, in_=s), reads=[self.sinbc[1]], writes=[sb_[1]],
                        slot=k.slot(sb_[1]))

        def compute(t_):
            t0 = t_ * TT
            i = t_ % 2
            xg = self.xng[i]
            cb, sb_ = self.cbt[i], self.sbt[i]
            mmctr = t_
            for s_ in streams:
                kind = s_["kind"]
                rms_qscale = s_["rms_qscale"]
                nrm_w = s_["nrm_w"]
                knorm = s_["knorm"]
                dst = s_["dst"]
                for oc in range(s_["ncols"] // 128):
                    bk = self.B[mmctr % 2]
                    mmctr += 1
                    ps = bk[0][:, 0:TT]
                    wb = wbuf(s_, oc // 2)
                    off = (oc % 2) * 128
                    for c in range(DCH):
                        self.mm(bk, ps, wb[0][:, 256 * c + off:256 * c + off + 128],
                                xg[0][:, TT * c:TT * (c + 1)], [wb[1], xg[1]],
                                start=(c == 0), stop=(c == DCH - 1),
                                inc=(c == DCH - 1))
                    if kind == "rope":
                        if rms_qscale:
                            pq = self.sq256[0]
                            k.dve.op(lambda e, o=pq[0][:, :], x=ps:
                                     e.tensor_scalar(o, x, float(HD) ** -0.5, None, ALU.mult),
                                     reads=[bk[1]], writes=[pq[1]])
                            qsrc_ap, qsrc_b = pq[0][:, :], pq[1]
                        else:
                            qsrc_ap, qsrc_b = ps, bk[1]
                        us = self.ust[0]
                        k.dve.op(lambda e, o=us[0][:, :], a=qsrc_ap, b=sb_[0][:, :]:
                                 e.tensor_tensor(o, a, b, ALU.mult),
                                 reads=[qsrc_b, sb_[1]], writes=[us[1]])
                        bj = self.B[6 + (oc % 2)]
                        self.mm(bj, bj[0][:, 0:TT], self.jTb[0][:, :], us[0][:, :],
                                [self.jTb[1], us[1]], start=True, stop=True)
                        vs = self.vscr[i]
                        k.dve.op(lambda e, o=vs[0][:, :], a=qsrc_ap, b=cb[0][:, :]:
                                 e.tensor_tensor(o, a, b, ALU.mult),
                                 reads=[qsrc_b, cb[1]], writes=[vs[1]])
                        st = self.qkst[i]
                        k.dve.op(lambda e, o=st[0][:, 0:TT], a=vs[0][:, :], b=bj[0][:, 0:TT]:
                                 e.tensor_tensor(o, a, b, ALU.add),
                                 reads=[vs[1], bj[1]], writes=[st[1]])
                    elif kind == "rms":
                        sq = self.sq256[0]
                        k.act.op(lambda e, o=sq[0][:, :], x=ps:
                                 e.activation(o, x, AF.Square), reads=[bk[1]], writes=[sq[1]])
                        b6 = self.B[6]
                        self.mm(b6, b6[0][0:1, 0:TT], self.ones_f[0][:, :], sq[0][:, :],
                                [self.ones_f[1], sq[1]], start=True, stop=True)
                        rm = self.rmrow
                        ra = rm[0][0:1, 0:TT]
                        rb2 = rm[0][0:1, TT:2 * TT]
                        k.dve.op(lambda e, o=ra, x=b6[0][0:1, 0:TT]:
                                 e.tensor_scalar(o, x, 1.0 / HD, float(EPS_RMS),
                                                 ALU.mult, ALU.add),
                                 reads=[b6[1]], writes=[rm[1]])
                        k.act.op(lambda e, o=rb2, x=ra: e.activation(o, x, AF.Ln),
                                 reads=[rm[1]], writes=[rm[1]])
                        qsc = float(np.log(HD ** -0.5)) if rms_qscale else 0.0
                        k.dve.op(lambda e, o=ra, x=rb2, q=qsc:
                                 e.tensor_scalar(o, x, -0.5, q, ALU.mult, ALU.add),
                                 reads=[rm[1]], writes=[rm[1]])
                        k.act.op(lambda e, o=rb2, x=ra: e.activation(o, x, AF.Exp),
                                 reads=[rm[1]], writes=[rm[1]])
                        b7 = self.B[7]
                        self.mm(b7, b7[0][:, 0:TT], self.c1[0][:, :], rb2,
                                [self.c1[1], rm[1]], start=True, stop=True)
                        ib = self.invbc
                        k.act.op(lambda e, o=ib[0][:, :], x=b7[0][:, 0:TT]:
                                 e.activation(o, x, AF.Copy), reads=[b7[1]], writes=[ib[1]])
                        vs = self.vscr[i]
                        k.dve.op(lambda e, o=vs[0][:, :], a=ps, b=ib[0][:, :]:
                                 e.tensor_tensor(o, a, b, ALU.mult),
                                 reads=[bk[1], ib[1]], writes=[vs[1]])
                        st = self.qkst[i]
                        k.dve.op(lambda e, o=st[0][:, 0:TT], x=vs[0][:, :], s=nrm_w[0][:, :]:
                                 e.tensor_scalar(o, x, s, None, ALU.mult),
                                 reads=[vs[1], nrm_w[1]], writes=[st[1]])
                    st = self.qkst[i]
                    k.qs.op(lambda e, d=dst[0].ap()[128 * oc:128 * (oc + 1), t0:t0 + TT],
                            s=st[0][:, 0:TT]: e.dma_start(out=d, in_=s),
                            reads=[st[1]], writes=[dst[1]], slot=k.slot(st[1]))
                    if knorm is not None:
                        r_ = t0 // TC
                        gh = (t0 % TC) // BLK
                        true_b = r_ if gh == 0 else 15 - r_
                        col = oc * 32 + true_b * 2 + (t0 % BLK) // TT
                        usq = self.ust[0]
                        k.act.op(lambda e, o=usq[0][:, :], x=st[0][:, 0:TT]:
                                 e.activation(o, x, AF.Square), reads=[st[1]], writes=[usq[1]])
                        b5 = self.B[5]
                        self.mm(b5, b5[0][0:1, 0:TT], self.ones_b[0][:, :], usq[0][:, :],
                                [self.ones_b[1], usq[1]], start=True, stop=True)
                        k.dve.op(lambda e, o=knorm[0][0:1, col:col + 1], x=b5[0][0:1, 0:TT]:
                                 e.reduce_max(o, x, AX.X), reads=[b5[1]], writes=[knorm[1]])
            if vstream is not None:
                ncols = vstream["ncols"]
                dst = vstream["dst"]
                w0 = wbuf(vstream, 0)
                w1 = wbuf(vstream, 1) if ncols > 256 else None
                for ts_ in range(2):
                    bk = self.B[mmctr % 2]
                    mmctr += 1
                    ps = bk[0][:, 0:ncols]
                    for c in range(DCH):
                        first = (c == 0)
                        if ncols > 256:
                            self.mm(bk, bk[0][:, 0:256],
                                    xg[0][:, TT * c + 128 * ts_:TT * c + 128 * ts_ + 128],
                                    w0[0][:, 256 * c:256 * (c + 1)],
                                    [xg[1], w0[1]], start=first, stop=False,
                                    inc=False)
                            self.mm(bk, bk[0][:, 256:512],
                                    xg[0][:, TT * c + 128 * ts_:TT * c + 128 * ts_ + 128],
                                    w1[0][:, 256 * c:256 * (c + 1)],
                                    [xg[1], w1[1]], start=False,
                                    stop=(c == DCH - 1), inc=(c == DCH - 1))
                        else:
                            self.mm(bk, ps,
                                    xg[0][:, TT * c + 128 * ts_:TT * c + 128 * ts_ + 128],
                                    w0[0][:, 256 * c:256 * c + ncols],
                                    [xg[1], w0[1]], start=first,
                                    stop=(c == DCH - 1), inc=(c == DCH - 1))
                    st = self.qkst[i]
                    k.act.op(lambda e, o=st[0][:, 0:ncols], x=ps:
                             e.activation(o, x, AF.Copy), reads=[bk[1]], writes=[st[1]])
                    k.qs.op(lambda e, d=dst[0].ap()[t0 + 128 * ts_:t0 + 128 * (ts_ + 1), 0:ncols],
                            s=st[0][:, 0:ncols]: e.dma_start(out=d, in_=s),
                            reads=[st[1]], writes=[dst[1]], slot=k.slot(st[1]))

        loads(0)
        for t_ in range(NTT):
            if t_ + 1 < NTT:
                loads(t_ + 1)
            compute(t_)

    def flash(self, which):
        k = self.k
        isa = which == "a"
        qten = self.qa if isa else self.qb
        kten = self.ka if isa else self.kb
        vten = self.va if isa else self.vb
        osh = self.osh_a if isa else self.osh_b
        og = self.OG_a if isa else self.OG_b
        ew = 256 if isa else 128
        for b in range(NB):
            q0 = gcol(b)
            dst_d, half_b = blk_owner(b)
            for hp in range(2):
                qt = self.qt[0]
                k.qs.op(lambda e, d=qt[0][:, :].rearrange("p (h q) -> p h q", h=2),
                        s=qten[0].ap()[256 * hp:256 * (hp + 1), q0:q0 + 512]
                        .rearrange("(h p) q -> p h q", p=128):
                        e.dma_start(out=d, in_=s), reads=[qten[1]], writes=[qt[1]],
                        slot=k.slot(qt[1]))
                b6, b7 = self.B[6], self.B[7]
                cr = self.crow
                rm = self.rmrow
                for h in range(2):
                    c0q = 32 * (2 * hp + h)
                    c0k = 32 * (2 * hp + h) if isa else 0
                    k.dve.op(lambda e, o=rm[0][0:1, h:h + 1],
                             x=self.qna[0][0:1, c0q + 2 * b:c0q + 2 * b + 2]:
                             e.reduce_max(o, x, AX.X), reads=[self.qna[1]], writes=[rm[1]])
                    k.dve.op(lambda e, o=rm[0][0:1, 2 + h:3 + h],
                             x=self.kna[0][0:1, c0k:c0k + 2 * b + 2]:
                             e.reduce_max(o, x, AX.X), reads=[self.kna[1]], writes=[rm[1]])
                    k.dve.op(lambda e, o=cr[0][0:1, h:h + 1], a=rm[0][0:1, h:h + 1],
                             b_=rm[0][0:1, 2 + h:3 + h]:
                             e.tensor_tensor(o, a, b_, ALU.mult),
                             reads=[rm[1]], writes=[cr[1]])
                if not isa:
                    for ko in range(4):
                        k.dve.op(lambda e, o=self.crow8[0][0:1, 2 * ko:2 * ko + 2],
                                 x=cr[0][0:1, 0:2]: e.tensor_copy(o, x),
                                 reads=[cr[1]], writes=[self.crow8[1]])
                if isa:
                    cbn = self.cbneg
                    for h in range(2):
                        self.mm(b6, b6[0][:, 0:1], self.cn1[0][:, :],
                                cr[0][0:1, h:h + 1], [self.cn1[1], cr[1]],
                                start=True, stop=True)
                        k.act.op(lambda e, o=cbn[0][:, h:h + 1], x=b6[0][:, 0:1]:
                                 e.activation(o, x, AF.Copy),
                                 reads=[b6[1]], writes=[cbn[1]])
                OT = (self.B[4], self.B[5])
                BL = (self.B[6], self.B[7])
                STB = ((self.B[2], self.B[3], self.B[1], self.B[0]) if isa
                       else (self.B[2], self.B[3], self.B[1]))
                nstb = len(STB)
                B0 = self.B[0]
                nks = 4 * (b + 1)
                steps = [(kb, ko, h) for kb in range(b + 1)
                         for ko in range(4) for h in range(2)]
                nstep = len(steps)

                def load_kb(kb):
                    kt = self.kt[kb % 2]
                    vt = self.vt[kb % 2]
                    kk0 = gcol(kb)
                    if isa:
                        k.qs.op(lambda e, d=kt[0][:, :].rearrange("p (h q) -> p h q", h=2),
                                s=kten[0].ap()[256 * hp:256 * (hp + 1), kk0:kk0 + 512]
                                .rearrange("(h p) q -> p h q", p=128):
                                e.dma_start(out=d, in_=s), reads=[kten[1]], writes=[kt[1]],
                                slot=k.slot(kt[1]))
                        k.qs.op(lambda e,
                                d=vt[0][:, 0:4 * ew].rearrange("p (c x) -> p c x", c=4),
                                s=vten[0].ap()[kk0:kk0 + 512, 256 * hp:256 * (hp + 1)]
                                .rearrange("(c p) x -> p c x", p=128):
                                e.dma_start(out=d, in_=s), reads=[vten[1]], writes=[vt[1]],
                                slot=k.slot(vt[1]))
                    else:
                        k.qs.op(lambda e, d=kt[0][:, 0:512],
                                s=kten[0].ap()[:, kk0:kk0 + 512]:
                                e.dma_start(out=d, in_=s), reads=[kten[1]], writes=[kt[1]],
                                slot=k.slot(kt[1]))
                        k.qs.op(lambda e,
                                d=vt[0][:, 0:4 * ew].rearrange("p (c x) -> p c x", c=4),
                                s=vten[0].ap()[kk0:kk0 + 512, :]
                                .rearrange("(c p) x -> p c x", p=128):
                                e.dma_start(out=d, in_=s), reads=[vten[1]], writes=[vt[1]],
                                slot=k.slot(vt[1]))

                def bias_kb(kb):
                    if isa:
                        return
                    rt = self.rowtmp
                    v0 = float(512 * (kb - b))
                    k.dve.op(lambda e, o=rt[0][0:1, 0:8],
                             x=self.SL8[0][0:1, 8 * hp:8 * hp + 8], v=v0:
                             e.tensor_scalar(o, x, v, None, ALU.mult),
                             reads=[self.SL8[1]], writes=[rt[1]])
                    k.dve.op(lambda e, o=rt[0][0:1, 0:8], a=rt[0][0:1, 0:8],
                             b_=self.SLK8[0][0:1, 8 * hp:8 * hp + 8]:
                             e.tensor_tensor(o, a, b_, ALU.add),
                             reads=[self.SLK8[1], rt[1]], writes=[rt[1]])
                    k.dve.op(lambda e, o=rt[0][0:1, 0:8], a=rt[0][0:1, 0:8],
                             b_=self.crow8[0][0:1, 0:8]:
                             e.tensor_tensor(o, a, b_, ALU.subtract),
                             reads=[self.crow8[1], rt[1]], writes=[rt[1]])
                    self.mm(B0, B0[0][:, 0:8], self.iotar[0][:, :],
                            self.SL8[0][0:1, 8 * hp:8 * hp + 8],
                            [self.iotar[1], self.SL8[1]], start=True, stop=False)
                    self.mm(B0, B0[0][:, 0:8], self.c1[0][:, :], rt[0][0:1, 0:8],
                            [self.c1[1], rt[1]], start=False, stop=True)
                    bm = self.bm8[kb % 2]
                    k.act.op(lambda e, o=bm[0][:, :], x=B0[0][:, 0:8]:
                             e.activation(o, x, AF.Copy), reads=[B0[1]], writes=[bm[1]])

                def emit_sT(i):
                    kb, ko, h = steps[i]
                    kt = self.kt[kb % 2]
                    bk = STB[i % nstb]
                    ps = bk[0][:, 0:512]
                    klh = (kt[0][:, 512 * h + 128 * ko:512 * h + 128 * ko + 128]
                           if isa else kt[0][:, 128 * ko:128 * ko + 128])
                    self.mm(bk, ps, klh, qt[0][:, 512 * h:512 * (h + 1)],
                            [kt[1], qt[1]], start=True, stop=True)
                    pb = self.pbuf[i % nstb]
                    if isa:
                        k.act.op(lambda e, o=pb[0][:, :], x=ps,
                                 bb_=self.cbneg[0][:, h:h + 1]:
                                 e.activation(o, x, AF.Exp, bias=bb_),
                                 reads=[bk[1], self.cbneg[1]], writes=[pb[1]])
                    else:
                        sc = self.sscr[i % 3]
                        nj, njb = ((self.NEGJMt[0][:, 512 * ko:512 * (ko + 1)],
                                    self.NEGJMt[1]) if kb == b else
                                   (self.NEGJt[0][:, :], self.NEGJt[1]))
                        k.dve.op(lambda e, o=sc[0][:, :], i0=nj,
                                 s=self.slc[0][:, 2 * hp + h:2 * hp + h + 1], i1=ps:
                                 e.scalar_tensor_tensor(o, i0, s, i1,
                                                        ALU.mult, ALU.add),
                                 reads=[njb, self.slc[1], bk[1]],
                                 writes=[sc[1]])
                        bm = self.bm8[kb % 2]
                        k.act.op(lambda e, o=pb[0][:, :], x=sc[0][:, :],
                                 bb_=bm[0][:, 2 * ko + h:2 * ko + h + 1]:
                                 e.activation(o, x, AF.Exp, bias=bb_),
                                 reads=[sc[1], bm[1]], writes=[pb[1]])

                def consume(i):
                    kb, ko, h = steps[i]
                    vt = self.vt[kb % 2]
                    pb = self.pbuf[i % nstb]
                    if kb == b and isa:
                        pb2 = self.qkst[0]
                        k.dve.op(lambda e, o=pb2[0][:, :], a=pb[0][:, :],
                                 b_=self.PMt[0][:, 512 * ko:512 * (ko + 1)]:
                                 e.tensor_tensor(o, a, b_, ALU.mult),
                                 reads=[pb[1], self.PMt[1]], writes=[pb2[1]])
                        pu = pb2
                    else:
                        pu = pb
                    first = (i == 0)
                    last = (i >= nstep - 2)
                    self.mm(BL[h], BL[h][0][0:1, 0:512],
                            self.ones_b[0][:, :], pu[0][:, :],
                            [self.ones_b[1], pu[1]], start=first or i == 1,
                            stop=last)
                    vlh = vt[0][:, ew * ko + 128 * h:ew * ko + 128 * h + 128] \
                        if isa else vt[0][:, 128 * ko:128 * ko + 128]
                    self.mm(OT[h], OT[h][0][:, 0:512], vlh, pu[0][:, :],
                            [vt[1], pu[1]], start=first or i == 1, stop=last)

                def prep(j):
                    kb = steps[j][0]
                    if j == 0 or steps[j - 1][0] != kb:
                        load_kb(kb)
                        bias_kb(kb)
                    emit_sT(j)

                la = nstb - 1 if isa else 2
                for j in range(min(la, nstep)):
                    prep(j)
                for i in range(nstep):
                    if i + la < nstep:
                        prep(i + la)
                    consume(i)
                for h in range(2):
                    lr_ap = self.rows[0][0:1, 0:512]
                    rb_ = self.rows[1]
                    k.dve.op(lambda e, o=lr_ap, x=BL[h][0][0:1, 0:512]:
                             e.tensor_scalar(o, x, 1e-38, None, ALU.max),
                             reads=[BL[h][1]], writes=[rb_])
                    il_ap = self.rows[0][0:1, 512:1024]
                    k.dve.op(lambda e, o=il_ap, x=lr_ap:
                             e.reciprocal(o, x), reads=[rb_], writes=[rb_])
                    b1 = self.B[1]
                    self.mm(b1, b1[0][:, 0:512], self.c1[0][:, :], il_ap,
                            [self.c1[1], rb_], start=True, stop=True)
                    ib = self.invlb
                    k.act.op(lambda e, o=ib[0][:, :], x=b1[0][:, 0:512]:
                             e.activation(o, x, AF.Copy), reads=[b1[1]], writes=[ib[1]])
                    ot = self.otst[h]
                    k.dve.op(lambda e, o=ot[0][:, :], a=OT[h][0][:, 0:512],
                             b_=ib[0][:, :]: e.tensor_tensor(o, a, b_, ALU.mult),
                             reads=[OT[h][1], ib[1]], writes=[ot[1]])
                    row0 = 512 * dst_d + 128 * (2 * hp + h)
                    oshh = osh[half_b]
                    k.qs.op(lambda e, d=oshh[0].ap()[row0:row0 + 128, :],
                            s=ot[0][:, :]: e.dma_start(out=d, in_=s),
                            reads=[ot[1]], writes=[oshh[1]], slot=k.slot(ot[1]))
            if b in (7, 15):
                hb = 0 if b == 7 else 1
                for h_ in range(2):
                    sl = k.slot(self.otst[h_][1])
                    if sl.count:
                        k.qg._wait_emit(sl, sl.count)
                self.collective("AllToAll", osh[hb], og[hb])

    def load_OG(self, OG, g):
        k = self.k
        for j in range(4):
            if g == 0:
                xb = self.rhs2[j // 2]
            else:
                xb = self.xng[j // 2]
            d_ap = (xb[0][:, :].rearrange("p (c t) -> p c t", c=16)
                    [:, 8 * (j % 2):8 * (j % 2) + 8, :])
            dbuf = xb[1]
            OGh = OG[g]
            k.qs.op(lambda e, d=d_ap,
                    s=OGh[0].ap()[1024 * j:1024 * (j + 1), :]
                    .rearrange("(c p) t -> p c t", p=128):
                    e.dma_start(out=d, in_=s), reads=[OGh[1]], writes=[dbuf],
                    slot=k.slot(dbuf))

    def xn_chunk(self, g, c):
        xb = (self.rhs2 if g == 0 else self.xng)[c // 16]
        cc = c % 16
        return xb[0][:, 512 * cc:512 * (cc + 1)], xb[1]

    def router_and_h(self, g):
        k = self.k
        b6, b7 = self.B[6], self.B[7]
        for c in range(DCH):
            xa, xbuf = self.xn_chunk(g, c)
            self.mm(b6, b6[0][0:8, 0:512], self.wrssb[0][:, 8 * c:8 * (c + 1)],
                    xa, [self.wrssb[1], xbuf],
                    start=(c == 0), stop=(c == DCH - 1), inc=(c == DCH - 1))
        lg = self.lg
        k.dve.op(lambda e, o=lg[0][:, :], x=b6[0][0:8, 0:512]:
                 e.tensor_copy(o, x), reads=[b6[1]], writes=[lg[1]])
        for qs_ in range(4):
            self.transpose(b7, b7[0][0:128, 0:8], lg[0][0:8, 128 * qs_:128 * (qs_ + 1)],
                           [lg[1]], p=8)
            r0, r1, r2, r3, r4, r5 = [self.rsm[j] for j in range(6)]
            c0, c1_, c2, c3 = [self.rcol[j] for j in range(4)]
            k.dve.op(lambda e, o=r0[0][:, :], x=b7[0][0:128, 0:8]:
                     e.tensor_copy(o, x), reads=[b7[1]], writes=[r0[1]])
            k.dve.op(lambda e, o=c0[0][:, :], x=r0[0][:, :]:
                     e.reduce_max(o, x, AX.X, negate=True), reads=[r0[1]], writes=[c0[1]])
            k.act.op(lambda e, o=r1[0][:, :], x=r0[0][:, :], bb_=c0[0][:, :]:
                     e.activation(o, x, AF.Exp, bias=bb_),
                     reads=[r0[1], c0[1]], writes=[r1[1]])
            k.dve.op(lambda e, o=c1_[0][:, :], x=r1[0][:, :]:
                     e.reduce_max(o, x, AX.X), reads=[r1[1]], writes=[c1_[1]])
            k.dve.op(lambda e, o=r2[0][:, :], x=r1[0][:, :], s=c1_[0][:, :]:
                     e.tensor_scalar(o, x, s, None, ALU.is_ge),
                     reads=[r1[1], c1_[1]], writes=[r2[1]])
            k.dve.op(lambda e, o=r3[0][:, :], i0=r2[0][:, :], i1=r1[0][:, :]:
                     e.scalar_tensor_tensor(o, i0, -1e30, i1, ALU.mult, ALU.add),
                     reads=[r2[1], r1[1]], writes=[r3[1]])
            k.dve.op(lambda e, o=c2[0][:, :], x=r3[0][:, :]:
                     e.reduce_max(o, x, AX.X), reads=[r3[1]], writes=[c2[1]])
            k.dve.op(lambda e, o=r4[0][:, :], x=r1[0][:, :], s=c2[0][:, :]:
                     e.tensor_scalar(o, x, s, None, ALU.is_ge),
                     reads=[r1[1], c2[1]], writes=[r4[1]])
            k.dve.op(lambda e, o=r5[0][:, :], a=r1[0][:, :], b_=r4[0][:, :]:
                     e.tensor_tensor(o, a, b_, ALU.mult),
                     reads=[r1[1], r4[1]], writes=[r5[1]])
            k.dve.op(lambda e, o=c3[0][:, :], a=c1_[0][:, :], b_=c2[0][:, :]:
                     e.tensor_tensor(o, a, b_, ALU.add),
                     reads=[c1_[1], c2[1]], writes=[c3[1]])
            k.dve.op(lambda e, o=c0[0][:, :], x=c3[0][:, :]:
                     e.reciprocal(o, x), reads=[c3[1]], writes=[c0[1]])
            k.dve.op(lambda e, o=r0[0][:, :], x=r5[0][:, :], s=c0[0][:, :]:
                     e.tensor_scalar(o, x, s, 2.0, ALU.mult, ALU.mult),
                     reads=[r5[1], c0[1]], writes=[r0[1]])
            self.transpose(b7, b7[0][0:8, 0:128], r0[0][:, 0:8], [r0[1]])
            k.dve.op(lambda e, o=self.wTar[0][0:8, 128 * qs_:128 * (qs_ + 1)],
                     x=b7[0][0:8, 0:128]: e.tensor_copy(o, x),
                     reads=[b7[1]], writes=[self.wTar[1]])
        self.mm(b6, b6[0][:, 0:512], self.REPsb[0][:, :], self.wTar[0][0:8, 0:512],
                [self.REPsb[1], self.wTar[1]], start=True, stop=True)
        wt = self.wt16sb
        k.act.op(lambda e, o=wt[0][:, :], x=b6[0][:, 0:512]:
                 e.activation(o, x, AF.Copy), reads=[b6[1]], writes=[wt[1]])
        src = self.AallT[0].ap().rearrange("(c p) o -> p c o", p=128)
        bi = self.wsl[0]
        for j in range(4):
            k.qg.op(lambda e,
                    d=bi[0][:, :].rearrange("p (c o) -> p c o", c=DCH)
                    [:, 8 * j:8 * (j + 1), 0:128],
                    s=src[:, 8 * j:8 * (j + 1), :]:
                    e.dma_start(out=d, in_=s), reads=[self.AallT[1]], writes=[bi[1]],
                    slot=k.slot(bi[1]))
        for c in range(DCH):
            xa, xbuf = self.xn_chunk(g, c)
            self.mm(b7, b7[0][:, 0:512], bi[0][:, 256 * c:256 * c + 128],
                    xa, [bi[1], xbuf], start=(c == 0), stop=(c == DCH - 1),
                    inc=(c == DCH - 1))
        k.dve.op(lambda e, o=self.hwT[0][:, 512 * g:512 * (g + 1)],
                 a=b7[0][:, 0:512], b_=wt[0][:, :]:
                 e.tensor_tensor(o, a, b_, ALU.mult),
                 reads=[b7[1], wt[1]], writes=[self.hwT[1]])

    def big_proj(self, wt, resid, ydst, lora, stats, bias):
        k = self.k
        self.load_wslice(wt, 0, 256, [0], cast=True)
        for s in range(16):
            wb = self.wsl[s % 2]
            if s + 1 < 16:
                self.load_wslice(wt, 256 * (s + 1), 256, [(s + 1) % 2], cast=True)
            for oc in range(2):
                o2 = 2 * s + oc
                for g in range(2):
                    bk = self.B[[0, 1, 6, 7][2 * oc + g]]
                    ps = bk[0][:, 0:512]
                    for c in range(DCH):
                        last = (c == DCH - 1 and not lora)
                        xa, xbuf = self.xn_chunk(g, c)
                        self.mm(bk, ps,
                                wb[0][:, 256 * c + 128 * oc:256 * c + 128 * oc + 128],
                                xa, [wb[1], xbuf], start=(c == 0),
                                stop=last, inc=last)
                    if lora:
                        self.mm(bk, ps, self.Ballsb[0][:, 128 * o2:128 * (o2 + 1)],
                                self.hwT[0][:, 512 * g:512 * (g + 1)],
                                [self.Ballsb[1], self.hwT[1]], start=False, stop=True)
                    i = (2 * oc + g) % 2
                    rt = self.rtile[i]
                    k.qs.op(lambda e, d=rt[0][:, :],
                            s=resid[0].ap()[128 * o2:128 * (o2 + 1), 512 * g:512 * (g + 1)]:
                            e.dma_start(out=d, in_=s), reads=[resid[1]], writes=[rt[1]],
                            slot=k.slot(rt[1]))
                    yt = self.ytile[i]
                    bcol = self.bb[0][:, o2:o2 + 1] if bias else 0.0
                    breads = [bk[1], rt[1]] + ([self.bb[1]] if bias else [])
                    k.dve.op(lambda e, o=yt[0][:, :], i0=ps, s=bcol, i1=rt[0][:, :]:
                             e.scalar_tensor_tensor(o, i0, s, i1, ALU.add, ALU.add),
                             reads=breads, writes=[yt[1]])
                    k.qs.op(lambda e, d=ydst[0].ap()[128 * o2:128 * (o2 + 1),
                                                     512 * g:512 * (g + 1)],
                            s=yt[0][:, :]: e.dma_start(out=d, in_=s),
                            reads=[yt[1]], writes=[ydst[1]], slot=k.slot(yt[1]))
                    if stats:
                        yc = self.yc[i]
                        k.act.op(lambda e, o=yc[0][:, :], x=yt[0][:, :]:
                                 e.activation(o, x, AF.Copy), reads=[yt[1]], writes=[yc[1]])
                        sqc = self.sqc[i]
                        k.act.op(lambda e, o=sqc[0][:, :], x=yc[0][:, :]:
                                 e.activation(o, x, AF.Square), reads=[yc[1]], writes=[sqc[1]])
                        bS, bQ = self.B[2 + g], self.B[4 + g]
                        self.mm(bS, bS[0][0:1, 0:512], self.ones_b[0][:, :], yc[0][:, :],
                                [self.ones_b[1], yc[1]], start=(o2 == 0), stop=(o2 == 31))
                        self.mm(bQ, bQ[0][0:1, 0:512], self.ones_b[0][:, :], sqc[0][:, :],
                                [self.ones_b[1], sqc[1]], start=(o2 == 0), stop=(o2 == 31))

    def mm_fixup(self):
        pass

    def build(self):
        k = self.k
        self.load_consts()
        self.ln_stats_scan(self.xT)
        self.ln_rows_bcast()
        self.ln_norm_to_shard(self.xT, self.xnsh1)
        self.collective("AllGather", self.xnsh1, self.XN1)
        self.qkv_scan(self.XN1, [
            dict(wt=self.wqTs, ncols=NHL * HD, kind="rope", dst=self.qa,
                 rms_qscale=True, nrm_w=None, knorm=self.qna, wb=[0, 1]),
            dict(wt=self.wkTs, ncols=NHL * HD, kind="rope", dst=self.ka,
                 rms_qscale=False, nrm_w=None, knorm=self.kna,
                 wb=[self.rhs2[0], self.rhs2[1]]),
        ])
        self.sqrt_row(self.qna)
        self.sqrt_row(self.kna)
        self.qkv_scan(self.XN1, [], vstream=dict(
            wt=self.wvTs, ncols=NHL * HD, dst=self.va, wb=[0, 1]))
        self.flash("a")
        last = STAGES == 1
        y1dst = self.yout if last else self.y1
        self.load_OG(self.OG_a, 0)
        self.load_OG(self.OG_a, 1)
        self.big_proj(self.woT, self.xT, y1dst, lora=False,
                      stats=not last, bias=False)
        if last:
            return
        self.ln_rows_bcast()
        last = STAGES == 2
        y2dst = self.yout if last else self.y2
        self.ln_norm_to_half(y1dst, 0)
        self.ln_norm_to_half(y1dst, 1)
        self.router_and_h(0)
        self.router_and_h(1)
        self.big_proj(self.WbT, y1dst, y2dst, lora=True,
                      stats=not last, bias=True)
        if last:
            return
        self.ln_rows_bcast()
        self.ln_norm_to_shard(y2dst, self.xnsh3)
        self.collective("AllGather", self.xnsh3, self.XN3)
        self.qkv_scan(self.XN3, [
            dict(wt=self.bWqTs, ncols=NHL * HD, kind="rms", dst=self.qb,
                 rms_qscale=True, nrm_w=self.qnwsb, knorm=self.qna, wb=[0, 1]),
            dict(wt=self.bWkTs, ncols=HD, kind="rms", dst=self.kb,
                 rms_qscale=False, nrm_w=self.knwsb, knorm=self.kna,
                 wb=[self.rhs2[0]]),
        ], vstream=dict(wt=self.bWvTs, ncols=HD, dst=self.vb, wb=[self.rhs2[1]]))
        self.sqrt_row(self.qna)
        self.sqrt_row(self.kna, n=32)
        self.flash("b")
        last = STAGES == 3
        y3dst = self.yout if last else self.y3
        self.load_OG(self.OG_b, 0)
        self.load_OG(self.OG_b, 1)
        self.big_proj(self.bWoT, y2dst, y3dst, lora=False,
                      stats=not last, bias=False)
        if last:
            return
        self.ln_rows_bcast()
        self.ln_norm_to_half(y3dst, 0)
        self.ln_norm_to_half(y3dst, 1)
        self.router_and_h(0)
        self.router_and_h(1)
        self.big_proj(self.WbT, y3dst, self.yout, lora=True,
                      stats=False, bias=True)


def build_program():
    os.environ["TRN_ALLOW_ICI_ALL_TO_ALL"] = "1"
    k = BK()
    p = Prog(k)
    p.build()
    q = k.queues["qs"]
    for s in k.slotsems.values():
        if s.count:
            q.items.append(("w", s, s.count))
    for s in k.ccs:
        q.items.append(("w", s, s.count))
    nc = k.finish()
    return nc


_NC_CACHE = None


def kernel(**inputs):
    global _NC_CACHE
    os.environ["TRN_ALLOW_ICI_ALL_TO_ALL"] = "1"
    if _NC_CACHE is None:
        _NC_CACHE = build_program()
    nc = _NC_CACHE
    in_maps = [host_inputs(c, inputs) for c in range(NCORE)]
    REP = np.zeros((NE, 128), np.float32)
    for e in range(NE):
        REP[e, e * LR:(e + 1) * LR] = 1.0
    cosbc = np.broadcast_to(_CONSTS["cosrow"], (128, T)).copy()
    sinbc = np.broadcast_to(_CONSTS["sinrow"], (128, T)).copy()
    for m in in_maps:
        m["REP_in"] = REP
        m["cosbc"] = cosbc
        m["sinbc"] = sinbc
        m["idn_in"] = m.pop("idn")
        m["jT_in"] = m.pop("jT")
        m["AM_in"] = m.pop("AM")
        m["PM_in"] = m.pop("PM")
        m["NEGJ_in"] = m.pop("NEGJ")
        m["NEGJM_in"] = m.pop("NEGJM")
        m["SL8_in"] = m.pop("SL8")
        m["SLK8_in"] = m.pop("SLK8")
        m["onescol_in"] = m.pop("onescol")
        m.pop("cosrow"), m.pop("sinrow"), m.pop("colscale")
    res = run_bass_kernel_spmd(nc, in_maps, core_ids=list(range(NCORE)))
    return reassemble([r["yout"] for r in res.results])


# version 11 (final)
# speedup vs baseline: 1.1276x; 1.1276x over previous
import os
import numpy as np
import concourse.bass as bass
import concourse.mybir as mybir
from concourse.bass_utils import run_bass_kernel_spmd
from contextlib import ExitStack

F32 = mybir.dt.float32
BF16 = mybir.dt.bfloat16
AF = mybir.ActivationFunctionType
ALU = mybir.AluOpType
AX = mybir.AxisListType

NCORE = 8
T = 8192
D = 4096
TC = 1024
BLK = 512
NB = 16
NH = 32
HD = 128
NHL = 4
NE = 8
LR = 16
NEG = -1e30
EPS_LN = 1e-5
EPS_RMS = 1e-6
TT = 256
NTT = T // TT
DCH = 32
STAGES = int(os.environ.get("MK_STAGES", "4"))
TAPS = [t for t in os.environ.get("MK_TAPS", "").split(",") if t]


def blk_owner(b):
    return (b, 0) if b < 8 else (15 - b, 1)


def gcol(b):
    r, h = blk_owner(b)
    return 1024 * r + 512 * h



class Sem:
    def __init__(self, handle, name):
        self.h = handle
        self.name = name
        self.count = 0


class Buf:
    __slots__ = ("name", "writer", "readers")

    def __init__(self, name):
        self.name = name
        self.writer = None
        self.readers = []


class Qu:
    def __init__(self, name, sem, definc):
        self.name = name
        self.sem = sem
        self.definc = definc
        self.items = []
        self.waited = {}
        self.pending = []

    def _wait(self, sem, val):
        if val is None:
            raise AssertionError(f"{self.name}: unresolved pending event")
        if val <= 0:
            return
        if val > sem.count:
            raise AssertionError(
                f"{self.name}: wait on future count of {sem.name}")
        self._wait_emit(sem, val)

    def _wait_emit(self, sem, val):
        if self.waited.get(id(sem), -1) >= val:
            return
        self.waited[id(sem)] = val
        self.items.append(("w", sem, val))

    def _ev(self, ev):
        if ev is None:
            return
        sem, val = ev[0], ev[1]
        if isinstance(ev, list) and val is None:
            if sem is self.sem:
                return
            self.items.append(("wd", sem, ev))
            return
        self._wait(sem, val)

    def op(self, fn, reads=(), writes=(), inc=None, slot=None):
        for b in reads:
            if b is not None and b.writer is not None:
                self._ev(b.writer)
        for b in writes:
            if b is None:
                continue
            if b.writer is not None:
                self._ev(b.writer)
            for ev in b.readers:
                self._ev(ev)
        inc = self.definc if inc is None else inc
        if self.definc == 16:
            assert slot is not None, f"{self.name}: DMA op needs slot sem"
            sem = slot
        else:
            sem = self.sem
        if inc:
            sem.count += inc
            ev = (sem, sem.count)
            if sem is self.sem:
                for pb in self.pending:
                    pb[1] = self.sem.count
                self.pending = []
        else:
            ev = [self.sem, None]
            self.pending.append(ev)
        self.items.append(("o", fn, inc, sem))
        for b in writes:
            if b is not None:
                b.writer = ev
                b.readers = []
        for b in reads:
            if b is not None:
                b.readers.append(ev)
        return ev

    def flush(self, eng):
        for it in self.items:
            if it[0] == "w":
                _, sem, val = it
                eng.wait_ge(sem.h, val)
            elif it[0] == "wd":
                _, sem, ev = it
                assert ev[1] is not None, f"{self.name}: unresolved deferred wait"
                eng.wait_ge(sem.h, ev[1])
            else:
                _, fn, inc, sem = it
                inst = fn(eng)
                if inc:
                    inst.then_inc(sem.h, inc)


class BK:
    def __init__(self):
        nc = bass.Bass("TRN2", target_bir_lowering=False)
        self.nc = nc
        self.es = ExitStack()
        self.queues = {}
        for qn, definc in [("pe", 1), ("dve", 1), ("act", 1),
                           ("qs", 16), ("qg", 16)]:
            s = Sem(self.es.enter_context(nc.semaphore("s_" + qn)), qn)
            self.queues[qn] = Qu(qn, s, definc)
        self.pe = self.queues["pe"]
        self.dve = self.queues["dve"]
        self.act = self.queues["act"]
        self.qs = self.queues["qs"]
        self.qg = self.queues["qg"]
        self.ccs = []
        self.slotsems = {}
        self.np_inputs = {}

    def slot(self, buf):
        s = self.slotsems.get(id(buf))
        if s is None:
            s = Sem(self.es.enter_context(
                self.nc.semaphore(f"sl_{buf.name}")), f"sl_{buf.name}")
            self.slotsems[id(buf)] = s
        return s

    def sbuf(self, name, shape, dtype):
        h = self.es.enter_context(self.nc.sbuf_tensor(name, list(shape), dtype))
        return h, Buf(name)

    def dram_in(self, name, shape, dtype=F32):
        h = self.nc.dram_tensor(name, list(shape), dtype, kind="ExternalInput")
        return h, Buf(name)

    def dram_out(self, name, shape, dtype=F32):
        h = self.nc.dram_tensor(name, list(shape), dtype, kind="ExternalOutput")
        return h, Buf(name)

    def dram(self, name, shape, dtype):
        h = self.nc.dram_tensor(name, list(shape), dtype, kind="Internal")
        return h, Buf(name)

    def psum(self, name):
        h = self.es.enter_context(self.nc.psum_tensor(name, [128, 512], F32))
        return h, Buf(name)

    def ccsem(self):
        s = Sem(self.es.enter_context(
            self.nc.semaphore(f"cc{len(self.ccs)}")), f"cc{len(self.ccs)}")
        self.ccs.append(s)
        return s

    def finish(self):
        with self.nc.Block() as block:
            block.tensor(lambda e: self.queues["pe"].flush(e))
            block.vector(lambda e: self.queues["dve"].flush(e))
            block.scalar(lambda e: self.queues["act"].flush(e))
            block.sync(lambda e: self.queues["qs"].flush(e))
            block.gpsimd(lambda e: self.queues["qg"].flush(e))
        self.es.close()
        return self.nc



def host_consts():
    c = {}
    gpos = np.zeros(T, np.int64)
    for b in range(NB):
        gpos[gcol(b):gcol(b) + BLK] = np.arange(b * BLK, (b + 1) * BLK)
    pos = gpos.astype(np.float32)
    c["cosrow"] = np.cos(pos)[None, :]
    c["sinrow"] = np.sin(pos)[None, :]
    c["colones"] = np.ones((1, 128), np.float32)
    c["colscale"] = np.full((1, 128), float(HD) ** -0.5, np.float32)
    c["colneg"] = np.full((1, 128), -1.0, np.float32)
    c["iotacol"] = np.arange(128, dtype=np.float32)[None, :]
    c["idn"] = np.eye(128, dtype=np.float32)
    J = np.zeros((128, 128), np.float32)
    for i in range(64):
        J[2 * i, 2 * i + 1] = -1.0
        J[2 * i + 1, 2 * i] = 1.0
    c["jT"] = np.ascontiguousarray(J.T)
    i_ = np.arange(128)[:, None]
    j_ = np.arange(512)[None, :]
    jj = np.arange(128)[None, :]
    c["AM"] = np.where(jj <= i_, 0.0, NEG).astype(np.float32)
    c["PM"] = np.concatenate(
        [(128 * o + i_ <= j_).astype(np.float32) for o in range(4)],
        axis=1)
    c["NEGJ"] = np.broadcast_to(-j_.astype(np.float32), (128, 512)).copy()
    c["NEGJM"] = np.concatenate(
        [np.where(128 * o + i_ <= j_, -j_.astype(np.float32), NEG)
         for o in range(4)], axis=1).astype(np.float32)
    c["onescol"] = np.ones((128, 1), np.float32)
    return c


_CONSTS = host_consts()


def host_inputs(core, ins):
    hs = NHL * core
    d = {}
    x = np.asarray(ins["L0_attn_a_x"])[0]
    xT = np.ascontiguousarray(x.T)
    d["xT"] = np.ascontiguousarray(
        np.concatenate([xT[:, core * BLK:(core + 1) * BLK],
                        xT[:, (15 - core) * BLK:(16 - core) * BLK]], axis=1))
    tr = lambda a: np.ascontiguousarray(np.asarray(a).T.astype(np.float32))
    d["wqTs"] = tr(ins["L0_attn_a_wq"])[:, hs * HD:(hs + NHL) * HD]
    d["wkTs"] = tr(ins["L0_attn_a_wk"])[:, hs * HD:(hs + NHL) * HD]
    d["wvTs"] = tr(ins["L0_attn_a_wv"])[:, hs * HD:(hs + NHL) * HD]
    d["woT"] = tr(ins["L0_attn_a_wo"])
    d["bWqTs"] = tr(ins["L0_attn_b_Wq"])[:, hs * HD:(hs + NHL) * HD]
    d["bWkTs"] = tr(ins["L0_attn_b_Wk"])[:, core * HD:(core + 1) * HD]
    d["bWvTs"] = tr(ins["L0_attn_b_Wv"])[:, core * HD:(core + 1) * HD]
    d["bWoT"] = tr(ins["L0_attn_b_Wo"])
    d["WbT"] = tr(ins["L0_ffn_W_base"])
    d["WrT"] = tr(ins["L0_ffn_W_router"])
    d["AallT"] = np.ascontiguousarray(np.asarray(
        ins["L0_ffn_lora_A"]).transpose(2, 0, 1).reshape(D, NE * LR).astype(np.float32))
    d["Ball"] = np.ascontiguousarray(np.asarray(
        ins["L0_ffn_lora_B"]).transpose(0, 2, 1).reshape(NE * LR, D).astype(np.float32))
    d["bbase"] = np.ascontiguousarray(np.asarray(
        ins["L0_ffn_b_base"]).astype(np.float32).reshape(32, 128).T)
    d["qnw"] = np.asarray(ins["L0_attn_b_q_norm_w"]).astype(np.float32).reshape(128, 1)
    d["knw"] = np.asarray(ins["L0_attn_b_k_norm_w"]).astype(np.float32).reshape(128, 1)
    start = 2.0 ** (-2.0 ** (-(np.log2(NH) - 3)))
    slopes = np.array([start * start ** (h + 1 - 1) for h in range(NH)], np.float32)
    slopes = np.array([start * start ** i for i in range(NH)], np.float32)
    d["slopesrow"] = slopes[hs:hs + NHL][None, :].copy()
    sl8 = np.zeros((1, 16), np.float32)
    slk8 = np.zeros((1, 16), np.float32)
    for hp in range(2):
        for ko in range(4):
            for h in range(2):
                sl8[0, 8 * hp + 2 * ko + h] = slopes[hs + 2 * hp + h]
                slk8[0, 8 * hp + 2 * ko + h] = slopes[hs + 2 * hp + h] * 128.0 * ko
    d["SL8"] = sl8
    d["SLK8"] = slk8
    d["slopecols"] = np.broadcast_to(slopes[hs:hs + NHL][None, :],
                                     (128, NHL)).copy()
    d.update(_CONSTS)
    return {k: np.ascontiguousarray(v) for k, v in d.items()}


def reassemble(outs):
    full = np.empty((D, T), np.float32)
    for c in range(NCORE):
        y = outs[c]
        full[:, c * BLK:(c + 1) * BLK] = y[:, :BLK]
        full[:, (15 - c) * BLK:(16 - c) * BLK] = y[:, BLK:]
    return np.ascontiguousarray(full.T).reshape(1, T, D)



class Prog:
    def __init__(self, k: BK):
        self.k = k
        nc = k.nc
        di = k.dram_in
        self.xT = di("xT", [D, TC])
        self.wqTs = di("wqTs", [D, NHL * HD])
        self.wkTs = di("wkTs", [D, NHL * HD])
        self.wvTs = di("wvTs", [D, NHL * HD])
        self.woT = di("woT", [D, D])
        self.bWqTs = di("bWqTs", [D, NHL * HD])
        self.bWkTs = di("bWkTs", [D, HD])
        self.bWvTs = di("bWvTs", [D, HD])
        self.bWoT = di("bWoT", [D, D])
        self.WbT = di("WbT", [D, D])
        self.WrT = di("WrT", [D, NE])
        self.AallT = di("AallT", [D, NE * LR])
        self.Ball = di("Ball", [NE * LR, D])
        self.bbase = di("bbase", [128, 32])
        self.qnw = di("qnw", [128, 1])
        self.knw = di("knw", [128, 1])
        self.slopesrow = di("slopesrow", [1, NHL])
        self.SL8_in = di("SL8_in", [1, 16])
        self.SLK8_in = di("SLK8_in", [1, 16])
        self.slopecols = di("slopecols", [128, NHL])
        self.cosbc = di("cosbc", [128, T])
        self.sinbc = di("sinbc", [128, T])
        self.colones = di("colones", [1, 128])
        self.colneg = di("colneg", [1, 128])
        self.iotacol = di("iotacol", [1, 128])
        self.idn_in = di("idn_in", [128, 128])
        self.jT_in = di("jT_in", [128, 128])
        self.AM_in = di("AM_in", [128, 128])
        self.PM_in = di("PM_in", [128, 4 * 512])
        self.NEGJ_in = di("NEGJ_in", [128, 512])
        self.NEGJM_in = di("NEGJM_in", [128, 4 * 512])
        self.onescol_in = di("onescol_in", [128, 1])
        self.REP_in = di("REP_in", [NE, 128])
        self.yout = k.dram_out("yout", [D, TC], F32)
        dr = k.dram
        self.xnsh1 = [dr(f"xnsh1h{h}", [D, BLK], BF16) for h in range(2)]
        self.XN1 = [dr(f"XN1h{h}", [NCORE * D, BLK], BF16) for h in range(2)]
        self.xnsh3 = [dr(f"xnsh3h{h}", [D, BLK], BF16) for h in range(2)]
        self.XN3 = [dr(f"XN3h{h}", [NCORE * D, BLK], BF16) for h in range(2)]
        self.qa = dr("qa", [NHL * HD, T], BF16)
        self.ka = dr("ka", [NHL * HD, T], BF16)
        self.va = dr("va", [T, NHL * HD], BF16)
        self.osh_a = [dr(f"osh_a{h}", [NCORE * BLK, BLK], BF16) for h in range(2)]
        self.OG_a = [dr(f"OG_a{h}", [NCORE * BLK, BLK], BF16) for h in range(2)]
        self.qb = dr("qb", [NHL * HD, T], BF16)
        self.kb = dr("kb", [HD, T], BF16)
        self.vb = dr("vb", [T, HD], BF16)
        self.osh_b = [dr(f"osh_b{h}", [NCORE * BLK, BLK], BF16) for h in range(2)]
        self.OG_b = [dr(f"OG_b{h}", [NCORE * BLK, BLK], BF16) for h in range(2)]
        self.y1 = dr("y1", [D, TC], F32)
        self.y2 = dr("y2", [D, TC], F32)
        self.y3 = dr("y3", [D, TC], F32)
        sb = k.sbuf
        self.xng = [sb(f"xng{i}", [128, DCH * TT], BF16) for i in range(2)]
        self.wsl = [sb(f"wsl{i}", [128, DCH * 256], BF16) for i in range(2)]
        self.rhs2 = [sb(f"rhs2{i}", [128, DCH * 256], BF16) for i in range(2)]
        self.hwT = sb("hwT", [128, TC], BF16)
        self.wt16sb = sb("wt16sb", [128, 512], F32)
        self.Ballsb = sb("Ballsb", [128, D], BF16)
        self.wrssb = sb("wrssb", [128, DCH * NE], BF16)
        self.wTar = sb("wTar", [NE, 512], F32)
        self.lg = sb("lg", [NE, 512], F32)
        self.rsm = [sb(f"rsm{i}", [128, NE], F32) for i in range(6)]
        self.rcol = [sb(f"rcol{i}", [128, 1], F32) for i in range(4)]
        self.qt = [sb(f"qt{i}", [128, 2 * 512], BF16) for i in range(1)]
        self.kt = [sb(f"kt{i}", [128, 2 * 512], BF16) for i in range(2)]
        self.vt = [sb(f"vt{i}", [128, 4 * 128 * 2], BF16) for i in range(2)]
        self.pbuf = [sb(f"pbuf{i}", [128, 512], BF16) for i in range(4)]
        self.sscr = [sb(f"sscr{i}", [128, 512], F32) for i in range(3)]
        self.kna = sb("kna", [1, NHL * 32], F32)
        self.qna = sb("qna", [1, NHL * 32], F32)
        self.crow = sb("crow", [1, 2], F32)
        self.cbneg = sb("cbneg", [128, 2], F32)
        self.bmsb = sb("bmsb", [128, 2], F32)
        self.SL8 = sb("SL8", [1, 16], F32)
        self.SLK8 = sb("SLK8", [1, 16], F32)
        self.crow8 = sb("crow8", [1, 8], F32)
        self.bm8 = [sb(f"bm8{i}", [128, 8], F32) for i in range(2)]
        self.rowtmp = sb("rowtmp", [1, 8], F32)
        self.invlb = sb("invlb", [128, 512], F32)
        self.otst = [sb(f"otst{i}", [128, 512], BF16) for i in range(2)]
        self.cbt = [sb(f"cbt{i}", [128, TT], F32) for i in range(2)]
        self.sbt = [sb(f"sbt{i}", [128, TT], F32) for i in range(2)]
        self.ust = [sb(f"ust{i}", [128, TT], BF16) for i in range(1)]
        self.vscr = [sb(f"vscr{i}", [128, TT], F32) for i in range(2)]
        self.qkst = [sb(f"qkst{i}", [128, 512], BF16) for i in range(2)]
        self.sq256 = [sb(f"sq256{i}", [128, TT], F32) for i in range(1)]
        self.rmrow = sb("rmrow", [1, 2 * TT], F32)
        self.invbc = sb("invbc", [128, TT], F32)
        self.ybf = [sb(f"ybf{i}", [128, TC], BF16) for i in range(2)]
        self.sqt = [sb(f"sqt{i}", [128, TC], BF16) for i in range(1)]
        self.rows = sb("rows", [1, 3 * TC], F32)
        self.invb = sb("invb", [128, TC], BF16)
        self.minvb = sb("minvb", [128, TC], BF16)
        self.xnst = [sb(f"xnst{i}", [128, TC], BF16) for i in range(2)]
        self.rtile = [sb(f"rtile{i}", [128, 512], F32) for i in range(2)]
        self.ytile = [sb(f"ytile{i}", [128, 512], F32) for i in range(2)]
        self.yc = [sb(f"yc{i}", [128, 512], BF16) for i in range(2)]
        self.sqc = [sb(f"sqc{i}", [128, 512], BF16) for i in range(2)]
        self.idn = sb("idn", [128, 128], F32)
        self.jTb = sb("jTb", [128, 128], BF16)
        self.AMt = sb("AMt", [128, 128], F32)
        self.PMt = sb("PMt", [128, 4 * 512], BF16)
        self.NEGJt = sb("NEGJt", [128, 512], F32)
        self.NEGJMt = sb("NEGJMt", [128, 4 * 512], F32)
        self.slc = sb("slc", [128, NHL], F32)
        self.slr = sb("slr", [1, NHL], F32)
        self.c1 = sb("c1", [1, 128], F32)
        self.cn1 = sb("cn1", [1, 128], F32)
        self.iotar = sb("iotar", [1, 128], F32)
        self.ones_f = sb("ones_f", [128, 1], F32)
        self.ones_b = sb("ones_b", [128, 1], BF16)
        self.bb = sb("bb", [128, 32], F32)
        self.qnwsb = sb("qnwsb", [128, 1], F32)
        self.knwsb = sb("knwsb", [128, 1], F32)
        self.REPsb = sb("REPsb", [NE, 128], F32)
        self.B = [k.psum(f"bank{i}") for i in range(8)]

    def mm(self, bank, out_ap, lhsT, rhs, reads, start, stop, inc=True):
        self.k.pe.op(
            lambda e, o=out_ap, l=lhsT, r=rhs, s=start, t=stop: e.matmul(
                o, l, r, start=s, stop=t, skip_group_check=True),
            reads=reads, writes=[bank[1]], inc=1 if inc else 0)

    def transpose(self, bank, out_ap, in_ap, reads, p=128):
        self.k.pe.op(
            lambda e, o=out_ap, i=in_ap, p=p: e.transpose(o, i, self.idn[0][0:p, 0:p]),
            reads=list(reads) + [self.idn[1]], writes=[bank[1]], inc=1)

    def sqrt_row(self, t, n=128):
        self.k.act.op(lambda e, o=t[0][0:1, 0:n], x=t[0][0:1, 0:n]:
                      e.activation(o, x, AF.Sqrt), reads=[t[1]], writes=[t[1]])

    def load_consts(self):
        k = self.k
        qs, qg = k.qs, k.qg

        def ld(dst, dstbuf, src, srcbuf, cast=False):
            q = qg if cast else qs
            q.op(lambda e, d=dst, s=src: e.dma_start(out=d, in_=s),
                 reads=[srcbuf], writes=[dstbuf], slot=self.k.slot(dstbuf))

        ld(self.idn[0][:, :], self.idn[1], self.idn_in[0].ap(), self.idn_in[1])
        ld(self.jTb[0][:, :], self.jTb[1], self.jT_in[0].ap(), self.jT_in[1], cast=True)
        ld(self.AMt[0][:, :], self.AMt[1], self.AM_in[0].ap(), self.AM_in[1])
        ld(self.PMt[0][:, :], self.PMt[1], self.PM_in[0].ap(), self.PM_in[1], cast=True)
        ld(self.NEGJt[0][:, :], self.NEGJt[1], self.NEGJ_in[0].ap(), self.NEGJ_in[1])
        ld(self.NEGJMt[0][:, :], self.NEGJMt[1], self.NEGJM_in[0].ap(), self.NEGJM_in[1])
        ld(self.slc[0][:, :], self.slc[1], self.slopecols[0].ap(), self.slopecols[1])
        ld(self.slr[0][:, :], self.slr[1], self.slopesrow[0].ap(), self.slopesrow[1])
        ld(self.SL8[0][:, :], self.SL8[1], self.SL8_in[0].ap(), self.SL8_in[1])
        ld(self.SLK8[0][:, :], self.SLK8[1], self.SLK8_in[0].ap(), self.SLK8_in[1])
        ld(self.c1[0][:, :], self.c1[1], self.colones[0].ap(), self.colones[1])
        ld(self.cn1[0][:, :], self.cn1[1], self.colneg[0].ap(), self.colneg[1])
        ld(self.iotar[0][:, :], self.iotar[1], self.iotacol[0].ap(), self.iotacol[1])
        ld(self.ones_f[0][:, :], self.ones_f[1], self.onescol_in[0].ap(), self.onescol_in[1])
        ld(self.ones_b[0][:, :], self.ones_b[1], self.onescol_in[0].ap(), self.onescol_in[1], cast=True)
        ld(self.bb[0][:, :], self.bb[1], self.bbase[0].ap(), self.bbase[1])
        ld(self.qnwsb[0][:, :], self.qnwsb[1], self.qnw[0].ap(), self.qnw[1])
        ld(self.knwsb[0][:, :], self.knwsb[1], self.knw[0].ap(), self.knw[1])
        ld(self.REPsb[0][:, :], self.REPsb[1], self.REP_in[0].ap(), self.REP_in[1])
        ld(self.wrssb[0][:, :].rearrange("p (c e) -> p c e", c=DCH), self.wrssb[1],
           self.WrT[0].ap().rearrange("(c p) e -> p c e", p=128), self.WrT[1], cast=True)
        ld(self.Ballsb[0][:, :], self.Ballsb[1], self.Ball[0].ap(), self.Ball[1], cast=True)

    def ln_stats_scan(self, src):
        k = self.k

        def loads(c):
            yb = self.ybf[c % 2]
            k.qg.op(lambda e, d=yb[0][:, :], s=src[0].ap()[128 * c:128 * (c + 1), :]:
                    e.dma_start(out=d, in_=s), reads=[src[1]], writes=[yb[1]],
                    slot=k.slot(yb[1]))

        loads(0)
        for c in range(DCH):
            i = c % 2
            yb = self.ybf[i]
            if c + 1 < DCH:
                loads(c + 1)
            sq = self.sqt[0]
            k.act.op(lambda e, o=sq[0][:, :], x=yb[0][:, :]:
                     e.activation(o, x, AF.Square), reads=[yb[1]], writes=[sq[1]])
            for g in range(2):
                bS, bQ = self.B[2 + g], self.B[4 + g]
                self.mm(bS, bS[0][0:1, 0:512], self.ones_b[0][:, :],
                        yb[0][:, 512 * g:512 * (g + 1)],
                        [self.ones_b[1], yb[1]], start=(c == 0), stop=(c == DCH - 1))
                self.mm(bQ, bQ[0][0:1, 0:512], self.ones_b[0][:, :],
                        sq[0][:, 512 * g:512 * (g + 1)],
                        [self.ones_b[1], sq[1]], start=(c == 0), stop=(c == DCH - 1))

    def ln_rows_bcast(self, eps=EPS_LN):
        k = self.k
        R = self.rows[0]
        rb = self.rows[1]
        for g in range(2):
            M = R[0:1, 512 * g:512 * (g + 1)]
            A = R[0:1, TC + 512 * g: TC + 512 * (g + 1)]
            Bs = R[0:1, 2 * TC + 512 * g: 2 * TC + 512 * (g + 1)]
            bS, bQ = self.B[2 + g], self.B[4 + g]
            k.dve.op(lambda e, o=M, x=bS[0][0:1, 0:512]: e.tensor_scalar(
                o, x, 1.0 / D, None, ALU.mult), reads=[bS[1]], writes=[rb])
            k.dve.op(lambda e, o=A, x=bQ[0][0:1, 0:512]: e.tensor_scalar(
                o, x, 1.0 / D, None, ALU.mult), reads=[bQ[1]], writes=[rb])
            k.dve.op(lambda e, o=Bs, a=M, b=M: e.tensor_tensor(o, a, b, ALU.mult),
                     reads=[rb], writes=[rb])
            k.dve.op(lambda e, o=A, a=A, b=Bs: e.tensor_tensor(o, a, b, ALU.subtract),
                     reads=[rb], writes=[rb])
            k.dve.op(lambda e, o=Bs, x=A: e.tensor_scalar(
                o, x, float(eps), None, ALU.add), reads=[rb], writes=[rb])
            k.act.op(lambda e, o=A, x=Bs: e.activation(o, x, AF.Ln),
                     reads=[rb], writes=[rb])
            k.dve.op(lambda e, o=Bs, x=A: e.tensor_scalar(
                o, x, -0.5, None, ALU.mult), reads=[rb], writes=[rb])
            k.act.op(lambda e, o=A, x=Bs: e.activation(o, x, AF.Exp),
                     reads=[rb], writes=[rb])
            k.dve.op(lambda e, o=Bs, a=M, b=A: e.tensor_tensor(o, a, b, ALU.mult),
                     reads=[rb], writes=[rb])
            INV, MINV = A, Bs
            b6, b7 = self.B[6], self.B[7]
            self.mm(b6, b6[0][:, 0:512], self.c1[0][:, :], INV,
                    [self.c1[1], rb], start=True, stop=True)
            k.act.op(lambda e, o=self.invb[0][:, 512 * g:512 * (g + 1)],
                     x=b6[0][:, 0:512]: e.activation(o, x, AF.Copy),
                     reads=[b6[1]], writes=[self.invb[1]])
            self.mm(b7, b7[0][:, 0:512], self.c1[0][:, :], MINV,
                    [self.c1[1], rb], start=True, stop=True)
            k.act.op(lambda e, o=self.minvb[0][:, 512 * g:512 * (g + 1)],
                     x=b7[0][:, 0:512]: e.activation(o, x, AF.Copy),
                     reads=[b7[1]], writes=[self.minvb[1]])

    def ln_norm_to_shard(self, src, dsts, fulls):
        k = self.k
        items = [(h, c) for h in range(2) for c in range(DCH)]

        def loads(ix):
            h, c = items[ix]
            yb = self.ybf[ix % 2]
            k.qg.op(lambda e, d=yb[0][:, 0:512],
                    s=src[0].ap()[128 * c:128 * (c + 1), 512 * h:512 * (h + 1)]:
                    e.dma_start(out=d, in_=s), reads=[src[1]], writes=[yb[1]],
                    slot=k.slot(yb[1]))

        loads(0)
        for ix, (h, c) in enumerate(items):
            yb = self.ybf[ix % 2]
            if ix + 1 < len(items):
                loads(ix + 1)
            sq = self.sqt[0]
            k.dve.op(lambda e, o=sq[0][:, 0:512], a=yb[0][:, 0:512],
                     b=self.invb[0][:, 512 * h:512 * (h + 1)]:
                     e.tensor_tensor(o, a, b, ALU.mult),
                     reads=[yb[1], self.invb[1]], writes=[sq[1]])
            xn = self.xnst[ix % 2]
            k.dve.op(lambda e, o=xn[0][:, 0:512], a=sq[0][:, 0:512],
                     b=self.minvb[0][:, 512 * h:512 * (h + 1)]:
                     e.tensor_tensor(o, a, b, ALU.subtract),
                     reads=[sq[1], self.minvb[1]], writes=[xn[1]])
            k.qs.op(lambda e, d=dsts[h][0].ap()[128 * c:128 * (c + 1), :],
                    s=xn[0][:, 0:512]: e.dma_start(out=d, in_=s),
                    reads=[xn[1]], writes=[dsts[h][1]], slot=k.slot(xn[1]))
            if c == DCH - 1:
                for j in range(2):
                    sl = k.slot(self.xnst[j][1])
                    if sl.count:
                        k.qg._wait_emit(sl, sl.count)
                self.collective("AllGather", dsts[h], fulls[h])

    def ln_norm_to_half(self, src, g):
        k = self.k

        def loads(c):
            yb = self.yc[c % 2]
            k.qg.op(lambda e, d=yb[0][:, :],
                    s=src[0].ap()[128 * c:128 * (c + 1), 512 * g:512 * (g + 1)]:
                    e.dma_start(out=d, in_=s), reads=[src[1]], writes=[yb[1]],
                    slot=k.slot(yb[1]))

        loads(0)
        for c in range(DCH):
            i = c % 2
            yb = self.yc[i]
            if c + 1 < DCH:
                loads(c + 1)
            sq = self.sqc[i]
            k.dve.op(lambda e, o=sq[0][:, :], a=yb[0][:, :],
                     b=self.invb[0][:, 512 * g:512 * (g + 1)]:
                     e.tensor_tensor(o, a, b, ALU.mult),
                     reads=[yb[1], self.invb[1]], writes=[sq[1]])
            xa, xbuf = self.xn_chunk(g, c)
            k.dve.op(lambda e, o=xa,
                     a=sq[0][:, :], b=self.minvb[0][:, 512 * g:512 * (g + 1)]:
                     e.tensor_tensor(o, a, b, ALU.subtract),
                     reads=[sq[1], self.minvb[1]], writes=[xbuf])

    def collective(self, kind, shard, full, in_ap=None, out_ap=None):
        k = self.k
        cs = k.ccsem()
        q = k.qg
        if shard[1].writer is not None:
            q._ev(shard[1].writer)
        for ev in full[1].readers:
            q._ev(ev)
        if full[1].writer is not None:
            q._ev(full[1].writer)
        in_ap = shard[0].ap() if in_ap is None else in_ap
        out_ap = full[0].ap() if out_ap is None else out_ap

        def fn(e, kind=kind, si=in_ap, fo=out_ap, cs=cs):
            inst = e.collective_compute(
                kind, ALU.bypass, replica_groups=[list(range(NCORE))],
                ins=[si.opt()], outs=[fo.opt()])
            inst.then_inc(cs.h, 1)
            return inst
        q.items.append(("o", fn, 0, None))
        cs.count = 1
        full[1].writer = (cs, 1)
        full[1].readers = []
        shard[1].readers.append((cs, 1))
        return cs

    def load_wslice(self, wt, cols0, ncols, bufs, cast=True):
        k = self.k
        src = wt[0].ap()[:, cols0:cols0 + ncols].rearrange("(c p) o -> p c o", p=128)
        npieces = (ncols + 255) // 256
        q = k.qg if cast else k.qs
        nj = 4 if cast else 2
        for pc in range(npieces):
            w = min(256, ncols - 256 * pc)
            b_ = bufs[pc]
            bi = self.wsl[b_] if isinstance(b_, int) else b_
            for j in range(nj):
                sp = DCH // nj
                q.op(lambda e,
                     d=bi[0][:, :].rearrange("p (c o) -> p c o", c=DCH)[:, sp * j:sp * (j + 1), 0:w],
                     s=src[:, sp * j:sp * (j + 1), 256 * pc:256 * pc + w]:
                     e.dma_start(out=d, in_=s), reads=[wt[1]], writes=[bi[1]],
                     slot=k.slot(bi[1]))

    def qkv_scan(self, XN, streams, vstream=None):
        k = self.k
        for s_ in streams:
            self.load_wslice(s_["wt"], 0, s_["ncols"], s_["wb"])
        if vstream is not None:
            self.load_wslice(vstream["wt"], 0, vstream["ncols"], vstream["wb"])
        any_rope = any(s_["kind"] == "rope" for s_ in streams)

        def wbuf(s_, j):
            b_ = s_["wb"][j]
            return self.wsl[b_] if isinstance(b_, int) else b_

        def loads(t_):
            t0 = t_ * TT
            r = t0 // TC
            wc = t0 % TC
            i = t_ % 2
            xg = self.xng[i]
            XNh = XN[wc // 512]
            wcc = wc % 512
            for j in range(4):
                k.qs.op(lambda e,
                        d=xg[0][:, :].rearrange("p (c t) -> p c t", c=DCH)[:, 8 * j:8 * (j + 1), :],
                        s=XNh[0].ap()[D * r + 1024 * j:D * r + 1024 * (j + 1),
                                      wcc:wcc + TT].rearrange("(c p) t -> p c t", p=128):
                        e.dma_start(out=d, in_=s), reads=[XNh[1]], writes=[xg[1]],
                        slot=k.slot(xg[1]))
            if any_rope:
                cb, sb_ = self.cbt[i], self.sbt[i]
                k.qs.op(lambda e, d=cb[0][:, :], s=self.cosbc[0].ap()[:, t0:t0 + TT]:
                        e.dma_start(out=d, in_=s), reads=[self.cosbc[1]], writes=[cb[1]],
                        slot=k.slot(cb[1]))
                k.qs.op(lambda e, d=sb_[0][:, :], s=self.sinbc[0].ap()[:, t0:t0 + TT]:
                        e.dma_start(out=d, in_=s), reads=[self.sinbc[1]], writes=[sb_[1]],
                        slot=k.slot(sb_[1]))

        def compute(t_):
            t0 = t_ * TT
            i = t_ % 2
            xg = self.xng[i]
            cb, sb_ = self.cbt[i], self.sbt[i]
            mmctr = t_
            for s_ in streams:
                kind = s_["kind"]
                rms_qscale = s_["rms_qscale"]
                nrm_w = s_["nrm_w"]
                knorm = s_["knorm"]
                dst = s_["dst"]
                for oc in range(s_["ncols"] // 128):
                    bk = self.B[mmctr % 2]
                    mmctr += 1
                    ps = bk[0][:, 0:TT]
                    wb = wbuf(s_, oc // 2)
                    off = (oc % 2) * 128
                    for c in range(DCH):
                        self.mm(bk, ps, wb[0][:, 256 * c + off:256 * c + off + 128],
                                xg[0][:, TT * c:TT * (c + 1)], [wb[1], xg[1]],
                                start=(c == 0), stop=(c == DCH - 1),
                                inc=(c == DCH - 1))
                    if kind == "rope":
                        if rms_qscale:
                            pq = self.sq256[0]
                            k.dve.op(lambda e, o=pq[0][:, :], x=ps:
                                     e.tensor_scalar(o, x, float(HD) ** -0.5, None, ALU.mult),
                                     reads=[bk[1]], writes=[pq[1]])
                            qsrc_ap, qsrc_b = pq[0][:, :], pq[1]
                        else:
                            qsrc_ap, qsrc_b = ps, bk[1]
                        us = self.ust[0]
                        k.dve.op(lambda e, o=us[0][:, :], a=qsrc_ap, b=sb_[0][:, :]:
                                 e.tensor_tensor(o, a, b, ALU.mult),
                                 reads=[qsrc_b, sb_[1]], writes=[us[1]])
                        bj = self.B[6 + (oc % 2)]
                        self.mm(bj, bj[0][:, 0:TT], self.jTb[0][:, :], us[0][:, :],
                                [self.jTb[1], us[1]], start=True, stop=True)
                        vs = self.vscr[i]
                        k.dve.op(lambda e, o=vs[0][:, :], a=qsrc_ap, b=cb[0][:, :]:
                                 e.tensor_tensor(o, a, b, ALU.mult),
                                 reads=[qsrc_b, cb[1]], writes=[vs[1]])
                        st = self.qkst[i]
                        k.dve.op(lambda e, o=st[0][:, 0:TT], a=vs[0][:, :], b=bj[0][:, 0:TT]:
                                 e.tensor_tensor(o, a, b, ALU.add),
                                 reads=[vs[1], bj[1]], writes=[st[1]])
                    elif kind == "rms":
                        sq = self.sq256[0]
                        k.act.op(lambda e, o=sq[0][:, :], x=ps:
                                 e.activation(o, x, AF.Square), reads=[bk[1]], writes=[sq[1]])
                        b6 = self.B[6]
                        self.mm(b6, b6[0][0:1, 0:TT], self.ones_f[0][:, :], sq[0][:, :],
                                [self.ones_f[1], sq[1]], start=True, stop=True)
                        rm = self.rmrow
                        ra = rm[0][0:1, 0:TT]
                        rb2 = rm[0][0:1, TT:2 * TT]
                        k.dve.op(lambda e, o=ra, x=b6[0][0:1, 0:TT]:
                                 e.tensor_scalar(o, x, 1.0 / HD, float(EPS_RMS),
                                                 ALU.mult, ALU.add),
                                 reads=[b6[1]], writes=[rm[1]])
                        k.act.op(lambda e, o=rb2, x=ra: e.activation(o, x, AF.Ln),
                                 reads=[rm[1]], writes=[rm[1]])
                        qsc = float(np.log(HD ** -0.5)) if rms_qscale else 0.0
                        k.dve.op(lambda e, o=ra, x=rb2, q=qsc:
                                 e.tensor_scalar(o, x, -0.5, q, ALU.mult, ALU.add),
                                 reads=[rm[1]], writes=[rm[1]])
                        k.act.op(lambda e, o=rb2, x=ra: e.activation(o, x, AF.Exp),
                                 reads=[rm[1]], writes=[rm[1]])
                        b7 = self.B[7]
                        self.mm(b7, b7[0][:, 0:TT], self.c1[0][:, :], rb2,
                                [self.c1[1], rm[1]], start=True, stop=True)
                        ib = self.invbc
                        k.act.op(lambda e, o=ib[0][:, :], x=b7[0][:, 0:TT]:
                                 e.activation(o, x, AF.Copy), reads=[b7[1]], writes=[ib[1]])
                        vs = self.vscr[i]
                        k.dve.op(lambda e, o=vs[0][:, :], a=ps, b=ib[0][:, :]:
                                 e.tensor_tensor(o, a, b, ALU.mult),
                                 reads=[bk[1], ib[1]], writes=[vs[1]])
                        st = self.qkst[i]
                        k.dve.op(lambda e, o=st[0][:, 0:TT], x=vs[0][:, :], s=nrm_w[0][:, :]:
                                 e.tensor_scalar(o, x, s, None, ALU.mult),
                                 reads=[vs[1], nrm_w[1]], writes=[st[1]])
                    st = self.qkst[i]
                    k.qs.op(lambda e, d=dst[0].ap()[128 * oc:128 * (oc + 1), t0:t0 + TT],
                            s=st[0][:, 0:TT]: e.dma_start(out=d, in_=s),
                            reads=[st[1]], writes=[dst[1]], slot=k.slot(st[1]))
                    if knorm is not None:
                        r_ = t0 // TC
                        gh = (t0 % TC) // BLK
                        true_b = r_ if gh == 0 else 15 - r_
                        col = oc * 32 + true_b * 2 + (t0 % BLK) // TT
                        usq = self.ust[0]
                        k.act.op(lambda e, o=usq[0][:, :], x=st[0][:, 0:TT]:
                                 e.activation(o, x, AF.Square), reads=[st[1]], writes=[usq[1]])
                        b5 = self.B[5]
                        self.mm(b5, b5[0][0:1, 0:TT], self.ones_b[0][:, :], usq[0][:, :],
                                [self.ones_b[1], usq[1]], start=True, stop=True)
                        k.dve.op(lambda e, o=knorm[0][0:1, col:col + 1], x=b5[0][0:1, 0:TT]:
                                 e.reduce_max(o, x, AX.X), reads=[b5[1]], writes=[knorm[1]])
            if vstream is not None:
                ncols = vstream["ncols"]
                dst = vstream["dst"]
                w0 = wbuf(vstream, 0)
                w1 = wbuf(vstream, 1) if ncols > 256 else None
                for ts_ in range(2):
                    bk = self.B[mmctr % 2]
                    mmctr += 1
                    ps = bk[0][:, 0:ncols]
                    for c in range(DCH):
                        first = (c == 0)
                        if ncols > 256:
                            self.mm(bk, bk[0][:, 0:256],
                                    xg[0][:, TT * c + 128 * ts_:TT * c + 128 * ts_ + 128],
                                    w0[0][:, 256 * c:256 * (c + 1)],
                                    [xg[1], w0[1]], start=first, stop=False,
                                    inc=False)
                            self.mm(bk, bk[0][:, 256:512],
                                    xg[0][:, TT * c + 128 * ts_:TT * c + 128 * ts_ + 128],
                                    w1[0][:, 256 * c:256 * (c + 1)],
                                    [xg[1], w1[1]], start=False,
                                    stop=(c == DCH - 1), inc=(c == DCH - 1))
                        else:
                            self.mm(bk, ps,
                                    xg[0][:, TT * c + 128 * ts_:TT * c + 128 * ts_ + 128],
                                    w0[0][:, 256 * c:256 * c + ncols],
                                    [xg[1], w0[1]], start=first,
                                    stop=(c == DCH - 1), inc=(c == DCH - 1))
                    st = self.qkst[i]
                    k.act.op(lambda e, o=st[0][:, 0:ncols], x=ps:
                             e.activation(o, x, AF.Copy), reads=[bk[1]], writes=[st[1]])
                    k.qs.op(lambda e, d=dst[0].ap()[t0 + 128 * ts_:t0 + 128 * (ts_ + 1), 0:ncols],
                            s=st[0][:, 0:ncols]: e.dma_start(out=d, in_=s),
                            reads=[st[1]], writes=[dst[1]], slot=k.slot(st[1]))

        order = ([t_ for t_ in range(NTT) if t_ % 4 < 2]
                 + [t_ for t_ in range(NTT) if t_ % 4 >= 2])
        loads(order[0])
        for ix, t_ in enumerate(order):
            if ix + 1 < NTT:
                loads(order[ix + 1])
            compute(t_)

    def flash(self, which):
        k = self.k
        isa = which == "a"
        qten = self.qa if isa else self.qb
        kten = self.ka if isa else self.kb
        vten = self.va if isa else self.vb
        osh = self.osh_a if isa else self.osh_b
        og = self.OG_a if isa else self.OG_b
        ew = 256 if isa else 128
        for b in range(NB):
            q0 = gcol(b)
            dst_d, half_b = blk_owner(b)
            for hp in range(2):
                qt = self.qt[0]
                k.qs.op(lambda e, d=qt[0][:, :].rearrange("p (h q) -> p h q", h=2),
                        s=qten[0].ap()[256 * hp:256 * (hp + 1), q0:q0 + 512]
                        .rearrange("(h p) q -> p h q", p=128):
                        e.dma_start(out=d, in_=s), reads=[qten[1]], writes=[qt[1]],
                        slot=k.slot(qt[1]))
                b6, b7 = self.B[6], self.B[7]
                cr = self.crow
                rm = self.rmrow
                for h in range(2):
                    c0q = 32 * (2 * hp + h)
                    c0k = 32 * (2 * hp + h) if isa else 0
                    k.dve.op(lambda e, o=rm[0][0:1, h:h + 1],
                             x=self.qna[0][0:1, c0q + 2 * b:c0q + 2 * b + 2]:
                             e.reduce_max(o, x, AX.X), reads=[self.qna[1]], writes=[rm[1]])
                    k.dve.op(lambda e, o=rm[0][0:1, 2 + h:3 + h],
                             x=self.kna[0][0:1, c0k:c0k + 2 * b + 2]:
                             e.reduce_max(o, x, AX.X), reads=[self.kna[1]], writes=[rm[1]])
                    k.dve.op(lambda e, o=cr[0][0:1, h:h + 1], a=rm[0][0:1, h:h + 1],
                             b_=rm[0][0:1, 2 + h:3 + h]:
                             e.tensor_tensor(o, a, b_, ALU.mult),
                             reads=[rm[1]], writes=[cr[1]])
                if not isa:
                    for ko in range(4):
                        k.dve.op(lambda e, o=self.crow8[0][0:1, 2 * ko:2 * ko + 2],
                                 x=cr[0][0:1, 0:2]: e.tensor_copy(o, x),
                                 reads=[cr[1]], writes=[self.crow8[1]])
                if isa:
                    cbn = self.cbneg
                    for h in range(2):
                        self.mm(b6, b6[0][:, 0:1], self.cn1[0][:, :],
                                cr[0][0:1, h:h + 1], [self.cn1[1], cr[1]],
                                start=True, stop=True)
                        k.act.op(lambda e, o=cbn[0][:, h:h + 1], x=b6[0][:, 0:1]:
                                 e.activation(o, x, AF.Copy),
                                 reads=[b6[1]], writes=[cbn[1]])
                OT = (self.B[4], self.B[5])
                BL = (self.B[6], self.B[7])
                STB = ((self.B[2], self.B[3], self.B[1], self.B[0]) if isa
                       else (self.B[2], self.B[3], self.B[1]))
                nstb = len(STB)
                B0 = self.B[0]
                nks = 4 * (b + 1)
                steps = [(kb, ko, h) for kb in range(b + 1)
                         for ko in range(4) for h in range(2)]
                nstep = len(steps)

                def load_kb(kb):
                    kt = self.kt[kb % 2]
                    vt = self.vt[kb % 2]
                    kk0 = gcol(kb)
                    if isa:
                        k.qs.op(lambda e, d=kt[0][:, :].rearrange("p (h q) -> p h q", h=2),
                                s=kten[0].ap()[256 * hp:256 * (hp + 1), kk0:kk0 + 512]
                                .rearrange("(h p) q -> p h q", p=128):
                                e.dma_start(out=d, in_=s), reads=[kten[1]], writes=[kt[1]],
                                slot=k.slot(kt[1]))
                        k.qs.op(lambda e,
                                d=vt[0][:, 0:4 * ew].rearrange("p (c x) -> p c x", c=4),
                                s=vten[0].ap()[kk0:kk0 + 512, 256 * hp:256 * (hp + 1)]
                                .rearrange("(c p) x -> p c x", p=128):
                                e.dma_start(out=d, in_=s), reads=[vten[1]], writes=[vt[1]],
                                slot=k.slot(vt[1]))
                    else:
                        k.qs.op(lambda e, d=kt[0][:, 0:512],
                                s=kten[0].ap()[:, kk0:kk0 + 512]:
                                e.dma_start(out=d, in_=s), reads=[kten[1]], writes=[kt[1]],
                                slot=k.slot(kt[1]))
                        k.qs.op(lambda e,
                                d=vt[0][:, 0:4 * ew].rearrange("p (c x) -> p c x", c=4),
                                s=vten[0].ap()[kk0:kk0 + 512, :]
                                .rearrange("(c p) x -> p c x", p=128):
                                e.dma_start(out=d, in_=s), reads=[vten[1]], writes=[vt[1]],
                                slot=k.slot(vt[1]))

                def bias_kb(kb):
                    if isa:
                        return
                    rt = self.rowtmp
                    v0 = float(512 * (kb - b))
                    k.dve.op(lambda e, o=rt[0][0:1, 0:8],
                             x=self.SL8[0][0:1, 8 * hp:8 * hp + 8], v=v0:
                             e.tensor_scalar(o, x, v, None, ALU.mult),
                             reads=[self.SL8[1]], writes=[rt[1]])
                    k.dve.op(lambda e, o=rt[0][0:1, 0:8], a=rt[0][0:1, 0:8],
                             b_=self.SLK8[0][0:1, 8 * hp:8 * hp + 8]:
                             e.tensor_tensor(o, a, b_, ALU.add),
                             reads=[self.SLK8[1], rt[1]], writes=[rt[1]])
                    k.dve.op(lambda e, o=rt[0][0:1, 0:8], a=rt[0][0:1, 0:8],
                             b_=self.crow8[0][0:1, 0:8]:
                             e.tensor_tensor(o, a, b_, ALU.subtract),
                             reads=[self.crow8[1], rt[1]], writes=[rt[1]])
                    self.mm(B0, B0[0][:, 0:8], self.iotar[0][:, :],
                            self.SL8[0][0:1, 8 * hp:8 * hp + 8],
                            [self.iotar[1], self.SL8[1]], start=True, stop=False)
                    self.mm(B0, B0[0][:, 0:8], self.c1[0][:, :], rt[0][0:1, 0:8],
                            [self.c1[1], rt[1]], start=False, stop=True)
                    bm = self.bm8[kb % 2]
                    k.act.op(lambda e, o=bm[0][:, :], x=B0[0][:, 0:8]:
                             e.activation(o, x, AF.Copy), reads=[B0[1]], writes=[bm[1]])

                def emit_sT(i):
                    kb, ko, h = steps[i]
                    kt = self.kt[kb % 2]
                    bk = STB[i % nstb]
                    ps = bk[0][:, 0:512]
                    klh = (kt[0][:, 512 * h + 128 * ko:512 * h + 128 * ko + 128]
                           if isa else kt[0][:, 128 * ko:128 * ko + 128])
                    self.mm(bk, ps, klh, qt[0][:, 512 * h:512 * (h + 1)],
                            [kt[1], qt[1]], start=True, stop=True)
                    pb = self.pbuf[i % nstb]
                    if isa:
                        k.act.op(lambda e, o=pb[0][:, :], x=ps,
                                 bb_=self.cbneg[0][:, h:h + 1]:
                                 e.activation(o, x, AF.Exp, bias=bb_),
                                 reads=[bk[1], self.cbneg[1]], writes=[pb[1]])
                    else:
                        sc = self.sscr[i % 3]
                        nj, njb = ((self.NEGJMt[0][:, 512 * ko:512 * (ko + 1)],
                                    self.NEGJMt[1]) if kb == b else
                                   (self.NEGJt[0][:, :], self.NEGJt[1]))
                        k.dve.op(lambda e, o=sc[0][:, :], i0=nj,
                                 s=self.slc[0][:, 2 * hp + h:2 * hp + h + 1], i1=ps:
                                 e.scalar_tensor_tensor(o, i0, s, i1,
                                                        ALU.mult, ALU.add),
                                 reads=[njb, self.slc[1], bk[1]],
                                 writes=[sc[1]])
                        bm = self.bm8[kb % 2]
                        k.act.op(lambda e, o=pb[0][:, :], x=sc[0][:, :],
                                 bb_=bm[0][:, 2 * ko + h:2 * ko + h + 1]:
                                 e.activation(o, x, AF.Exp, bias=bb_),
                                 reads=[sc[1], bm[1]], writes=[pb[1]])

                def consume(i):
                    kb, ko, h = steps[i]
                    vt = self.vt[kb % 2]
                    pb = self.pbuf[i % nstb]
                    if kb == b and isa:
                        pb2 = self.qkst[0]
                        k.dve.op(lambda e, o=pb2[0][:, :], a=pb[0][:, :],
                                 b_=self.PMt[0][:, 512 * ko:512 * (ko + 1)]:
                                 e.tensor_tensor(o, a, b_, ALU.mult),
                                 reads=[pb[1], self.PMt[1]], writes=[pb2[1]])
                        pu = pb2
                    else:
                        pu = pb
                    first = (i == 0)
                    last = (i >= nstep - 2)
                    self.mm(BL[h], BL[h][0][0:1, 0:512],
                            self.ones_b[0][:, :], pu[0][:, :],
                            [self.ones_b[1], pu[1]], start=first or i == 1,
                            stop=last)
                    vlh = vt[0][:, ew * ko + 128 * h:ew * ko + 128 * h + 128] \
                        if isa else vt[0][:, 128 * ko:128 * ko + 128]
                    self.mm(OT[h], OT[h][0][:, 0:512], vlh, pu[0][:, :],
                            [vt[1], pu[1]], start=first or i == 1, stop=last)

                def prep(j):
                    kb = steps[j][0]
                    if j == 0 or steps[j - 1][0] != kb:
                        load_kb(kb)
                        bias_kb(kb)
                    emit_sT(j)

                la = nstb - 1 if isa else 2
                for j in range(min(la, nstep)):
                    prep(j)
                for i in range(nstep):
                    if i + la < nstep:
                        prep(i + la)
                    consume(i)
                for h in range(2):
                    lr_ap = self.rows[0][0:1, 0:512]
                    rb_ = self.rows[1]
                    k.dve.op(lambda e, o=lr_ap, x=BL[h][0][0:1, 0:512]:
                             e.tensor_scalar(o, x, 1e-38, None, ALU.max),
                             reads=[BL[h][1]], writes=[rb_])
                    il_ap = self.rows[0][0:1, 512:1024]
                    k.dve.op(lambda e, o=il_ap, x=lr_ap:
                             e.reciprocal(o, x), reads=[rb_], writes=[rb_])
                    b1 = self.B[1]
                    self.mm(b1, b1[0][:, 0:512], self.c1[0][:, :], il_ap,
                            [self.c1[1], rb_], start=True, stop=True)
                    ib = self.invlb
                    k.act.op(lambda e, o=ib[0][:, :], x=b1[0][:, 0:512]:
                             e.activation(o, x, AF.Copy), reads=[b1[1]], writes=[ib[1]])
                    ot = self.otst[h]
                    k.dve.op(lambda e, o=ot[0][:, :], a=OT[h][0][:, 0:512],
                             b_=ib[0][:, :]: e.tensor_tensor(o, a, b_, ALU.mult),
                             reads=[OT[h][1], ib[1]], writes=[ot[1]])
                    row0 = 512 * dst_d + 128 * (2 * hp + h)
                    oshh = osh[half_b]
                    k.qs.op(lambda e, d=oshh[0].ap()[row0:row0 + 128, :],
                            s=ot[0][:, :]: e.dma_start(out=d, in_=s),
                            reads=[ot[1]], writes=[oshh[1]], slot=k.slot(ot[1]))
            if b in (7, 15):
                hb = 0 if b == 7 else 1
                for h_ in range(2):
                    sl = k.slot(self.otst[h_][1])
                    if sl.count:
                        k.qg._wait_emit(sl, sl.count)
                self.collective("AllToAll", osh[hb], og[hb])

    def load_OG(self, OG, g):
        k = self.k
        for j in range(4):
            if g == 0:
                xb = self.rhs2[j // 2]
            else:
                xb = self.xng[j // 2]
            d_ap = (xb[0][:, :].rearrange("p (c t) -> p c t", c=16)
                    [:, 8 * (j % 2):8 * (j % 2) + 8, :])
            dbuf = xb[1]
            OGh = OG[g]
            k.qs.op(lambda e, d=d_ap,
                    s=OGh[0].ap()[1024 * j:1024 * (j + 1), :]
                    .rearrange("(c p) t -> p c t", p=128):
                    e.dma_start(out=d, in_=s), reads=[OGh[1]], writes=[dbuf],
                    slot=k.slot(dbuf))

    def xn_chunk(self, g, c):
        xb = (self.rhs2 if g == 0 else self.xng)[c // 16]
        cc = c % 16
        return xb[0][:, 512 * cc:512 * (cc + 1)], xb[1]

    def router_and_h(self, g):
        k = self.k
        b6, b7 = self.B[6], self.B[7]
        for c in range(DCH):
            xa, xbuf = self.xn_chunk(g, c)
            self.mm(b6, b6[0][0:8, 0:512], self.wrssb[0][:, 8 * c:8 * (c + 1)],
                    xa, [self.wrssb[1], xbuf],
                    start=(c == 0), stop=(c == DCH - 1), inc=(c == DCH - 1))
        lg = self.lg
        k.dve.op(lambda e, o=lg[0][:, :], x=b6[0][0:8, 0:512]:
                 e.tensor_copy(o, x), reads=[b6[1]], writes=[lg[1]])
        for qs_ in range(4):
            self.transpose(b7, b7[0][0:128, 0:8], lg[0][0:8, 128 * qs_:128 * (qs_ + 1)],
                           [lg[1]], p=8)
            r0, r1, r2, r3, r4, r5 = [self.rsm[j] for j in range(6)]
            c0, c1_, c2, c3 = [self.rcol[j] for j in range(4)]
            k.dve.op(lambda e, o=r0[0][:, :], x=b7[0][0:128, 0:8]:
                     e.tensor_copy(o, x), reads=[b7[1]], writes=[r0[1]])
            k.dve.op(lambda e, o=c0[0][:, :], x=r0[0][:, :]:
                     e.reduce_max(o, x, AX.X, negate=True), reads=[r0[1]], writes=[c0[1]])
            k.act.op(lambda e, o=r1[0][:, :], x=r0[0][:, :], bb_=c0[0][:, :]:
                     e.activation(o, x, AF.Exp, bias=bb_),
                     reads=[r0[1], c0[1]], writes=[r1[1]])
            k.dve.op(lambda e, o=c1_[0][:, :], x=r1[0][:, :]:
                     e.reduce_max(o, x, AX.X), reads=[r1[1]], writes=[c1_[1]])
            k.dve.op(lambda e, o=r2[0][:, :], x=r1[0][:, :], s=c1_[0][:, :]:
                     e.tensor_scalar(o, x, s, None, ALU.is_ge),
                     reads=[r1[1], c1_[1]], writes=[r2[1]])
            k.dve.op(lambda e, o=r3[0][:, :], i0=r2[0][:, :], i1=r1[0][:, :]:
                     e.scalar_tensor_tensor(o, i0, -1e30, i1, ALU.mult, ALU.add),
                     reads=[r2[1], r1[1]], writes=[r3[1]])
            k.dve.op(lambda e, o=c2[0][:, :], x=r3[0][:, :]:
                     e.reduce_max(o, x, AX.X), reads=[r3[1]], writes=[c2[1]])
            k.dve.op(lambda e, o=r4[0][:, :], x=r1[0][:, :], s=c2[0][:, :]:
                     e.tensor_scalar(o, x, s, None, ALU.is_ge),
                     reads=[r1[1], c2[1]], writes=[r4[1]])
            k.dve.op(lambda e, o=r5[0][:, :], a=r1[0][:, :], b_=r4[0][:, :]:
                     e.tensor_tensor(o, a, b_, ALU.mult),
                     reads=[r1[1], r4[1]], writes=[r5[1]])
            k.dve.op(lambda e, o=c3[0][:, :], a=c1_[0][:, :], b_=c2[0][:, :]:
                     e.tensor_tensor(o, a, b_, ALU.add),
                     reads=[c1_[1], c2[1]], writes=[c3[1]])
            k.dve.op(lambda e, o=c0[0][:, :], x=c3[0][:, :]:
                     e.reciprocal(o, x), reads=[c3[1]], writes=[c0[1]])
            k.dve.op(lambda e, o=r0[0][:, :], x=r5[0][:, :], s=c0[0][:, :]:
                     e.tensor_scalar(o, x, s, 2.0, ALU.mult, ALU.mult),
                     reads=[r5[1], c0[1]], writes=[r0[1]])
            self.transpose(b7, b7[0][0:8, 0:128], r0[0][:, 0:8], [r0[1]])
            k.dve.op(lambda e, o=self.wTar[0][0:8, 128 * qs_:128 * (qs_ + 1)],
                     x=b7[0][0:8, 0:128]: e.tensor_copy(o, x),
                     reads=[b7[1]], writes=[self.wTar[1]])
        self.mm(b6, b6[0][:, 0:512], self.REPsb[0][:, :], self.wTar[0][0:8, 0:512],
                [self.REPsb[1], self.wTar[1]], start=True, stop=True)
        wt = self.wt16sb
        k.act.op(lambda e, o=wt[0][:, :], x=b6[0][:, 0:512]:
                 e.activation(o, x, AF.Copy), reads=[b6[1]], writes=[wt[1]])
        src = self.AallT[0].ap().rearrange("(c p) o -> p c o", p=128)
        bi = self.wsl[0]
        for j in range(4):
            k.qg.op(lambda e,
                    d=bi[0][:, :].rearrange("p (c o) -> p c o", c=DCH)
                    [:, 8 * j:8 * (j + 1), 0:128],
                    s=src[:, 8 * j:8 * (j + 1), :]:
                    e.dma_start(out=d, in_=s), reads=[self.AallT[1]], writes=[bi[1]],
                    slot=k.slot(bi[1]))
        for c in range(DCH):
            xa, xbuf = self.xn_chunk(g, c)
            self.mm(b7, b7[0][:, 0:512], bi[0][:, 256 * c:256 * c + 128],
                    xa, [bi[1], xbuf], start=(c == 0), stop=(c == DCH - 1),
                    inc=(c == DCH - 1))
        k.dve.op(lambda e, o=self.hwT[0][:, 512 * g:512 * (g + 1)],
                 a=b7[0][:, 0:512], b_=wt[0][:, :]:
                 e.tensor_tensor(o, a, b_, ALU.mult),
                 reads=[b7[1], wt[1]], writes=[self.hwT[1]])

    def big_proj(self, wt, resid, ydst, lora, stats, bias):
        k = self.k
        self.load_wslice(wt, 0, 256, [0], cast=True)
        for s in range(16):
            wb = self.wsl[s % 2]
            if s + 1 < 16:
                self.load_wslice(wt, 256 * (s + 1), 256, [(s + 1) % 2], cast=True)
            for oc in range(2):
                o2 = 2 * s + oc
                for g in range(2):
                    bk = self.B[[0, 1, 6, 7][2 * oc + g]]
                    ps = bk[0][:, 0:512]
                    for c in range(DCH):
                        last = (c == DCH - 1 and not lora)
                        xa, xbuf = self.xn_chunk(g, c)
                        self.mm(bk, ps,
                                wb[0][:, 256 * c + 128 * oc:256 * c + 128 * oc + 128],
                                xa, [wb[1], xbuf], start=(c == 0),
                                stop=last, inc=last)
                    if lora:
                        self.mm(bk, ps, self.Ballsb[0][:, 128 * o2:128 * (o2 + 1)],
                                self.hwT[0][:, 512 * g:512 * (g + 1)],
                                [self.Ballsb[1], self.hwT[1]], start=False, stop=True)
                    i = (2 * oc + g) % 2
                    rt = self.rtile[i]
                    k.qs.op(lambda e, d=rt[0][:, :],
                            s=resid[0].ap()[128 * o2:128 * (o2 + 1), 512 * g:512 * (g + 1)]:
                            e.dma_start(out=d, in_=s), reads=[resid[1]], writes=[rt[1]],
                            slot=k.slot(rt[1]))
                    yt = self.ytile[i]
                    bcol = self.bb[0][:, o2:o2 + 1] if bias else 0.0
                    breads = [bk[1], rt[1]] + ([self.bb[1]] if bias else [])
                    k.dve.op(lambda e, o=yt[0][:, :], i0=ps, s=bcol, i1=rt[0][:, :]:
                             e.scalar_tensor_tensor(o, i0, s, i1, ALU.add, ALU.add),
                             reads=breads, writes=[yt[1]])
                    k.qs.op(lambda e, d=ydst[0].ap()[128 * o2:128 * (o2 + 1),
                                                     512 * g:512 * (g + 1)],
                            s=yt[0][:, :]: e.dma_start(out=d, in_=s),
                            reads=[yt[1]], writes=[ydst[1]], slot=k.slot(yt[1]))
                    if stats:
                        yc = self.yc[i]
                        k.act.op(lambda e, o=yc[0][:, :], x=yt[0][:, :]:
                                 e.activation(o, x, AF.Copy), reads=[yt[1]], writes=[yc[1]])
                        sqc = self.sqc[i]
                        k.act.op(lambda e, o=sqc[0][:, :], x=yc[0][:, :]:
                                 e.activation(o, x, AF.Square), reads=[yc[1]], writes=[sqc[1]])
                        bS, bQ = self.B[2 + g], self.B[4 + g]
                        self.mm(bS, bS[0][0:1, 0:512], self.ones_b[0][:, :], yc[0][:, :],
                                [self.ones_b[1], yc[1]], start=(o2 == 0), stop=(o2 == 31))
                        self.mm(bQ, bQ[0][0:1, 0:512], self.ones_b[0][:, :], sqc[0][:, :],
                                [self.ones_b[1], sqc[1]], start=(o2 == 0), stop=(o2 == 31))

    def mm_fixup(self):
        pass

    def build(self):
        k = self.k
        self.load_consts()
        self.ln_stats_scan(self.xT)
        self.ln_rows_bcast()
        self.ln_norm_to_shard(self.xT, self.xnsh1, self.XN1)
        self.qkv_scan(self.XN1, [
            dict(wt=self.wqTs, ncols=NHL * HD, kind="rope", dst=self.qa,
                 rms_qscale=True, nrm_w=None, knorm=self.qna, wb=[0, 1]),
            dict(wt=self.wkTs, ncols=NHL * HD, kind="rope", dst=self.ka,
                 rms_qscale=False, nrm_w=None, knorm=self.kna,
                 wb=[self.rhs2[0], self.rhs2[1]]),
        ])
        self.sqrt_row(self.qna)
        self.sqrt_row(self.kna)
        self.qkv_scan(self.XN1, [], vstream=dict(
            wt=self.wvTs, ncols=NHL * HD, dst=self.va, wb=[0, 1]))
        self.flash("a")
        last = STAGES == 1
        y1dst = self.yout if last else self.y1
        self.load_OG(self.OG_a, 0)
        self.load_OG(self.OG_a, 1)
        self.big_proj(self.woT, self.xT, y1dst, lora=False,
                      stats=not last, bias=False)
        if last:
            return
        self.ln_rows_bcast()
        last = STAGES == 2
        y2dst = self.yout if last else self.y2
        self.ln_norm_to_half(y1dst, 0)
        self.ln_norm_to_half(y1dst, 1)
        self.router_and_h(0)
        self.router_and_h(1)
        self.big_proj(self.WbT, y1dst, y2dst, lora=True,
                      stats=not last, bias=True)
        if last:
            return
        self.ln_rows_bcast()
        self.ln_norm_to_shard(y2dst, self.xnsh3, self.XN3)
        self.qkv_scan(self.XN3, [
            dict(wt=self.bWqTs, ncols=NHL * HD, kind="rms", dst=self.qb,
                 rms_qscale=True, nrm_w=self.qnwsb, knorm=self.qna, wb=[0, 1]),
            dict(wt=self.bWkTs, ncols=HD, kind="rms", dst=self.kb,
                 rms_qscale=False, nrm_w=self.knwsb, knorm=self.kna,
                 wb=[self.rhs2[0]]),
        ], vstream=dict(wt=self.bWvTs, ncols=HD, dst=self.vb, wb=[self.rhs2[1]]))
        self.sqrt_row(self.qna)
        self.sqrt_row(self.kna, n=32)
        self.flash("b")
        last = STAGES == 3
        y3dst = self.yout if last else self.y3
        self.load_OG(self.OG_b, 0)
        self.load_OG(self.OG_b, 1)
        self.big_proj(self.bWoT, y2dst, y3dst, lora=False,
                      stats=not last, bias=False)
        if last:
            return
        self.ln_rows_bcast()
        self.ln_norm_to_half(y3dst, 0)
        self.ln_norm_to_half(y3dst, 1)
        self.router_and_h(0)
        self.router_and_h(1)
        self.big_proj(self.WbT, y3dst, self.yout, lora=True,
                      stats=False, bias=True)


def build_program():
    os.environ["TRN_ALLOW_ICI_ALL_TO_ALL"] = "1"
    k = BK()
    p = Prog(k)
    p.build()
    q = k.queues["qs"]
    for s in k.slotsems.values():
        if s.count:
            q.items.append(("w", s, s.count))
    for s in k.ccs:
        q.items.append(("w", s, s.count))
    nc = k.finish()
    return nc


_NC_CACHE = None


def kernel(**inputs):
    global _NC_CACHE
    os.environ["TRN_ALLOW_ICI_ALL_TO_ALL"] = "1"
    if _NC_CACHE is None:
        _NC_CACHE = build_program()
    nc = _NC_CACHE
    in_maps = [host_inputs(c, inputs) for c in range(NCORE)]
    REP = np.zeros((NE, 128), np.float32)
    for e in range(NE):
        REP[e, e * LR:(e + 1) * LR] = 1.0
    cosbc = np.broadcast_to(_CONSTS["cosrow"], (128, T)).copy()
    sinbc = np.broadcast_to(_CONSTS["sinrow"], (128, T)).copy()
    for m in in_maps:
        m["REP_in"] = REP
        m["cosbc"] = cosbc
        m["sinbc"] = sinbc
        m["idn_in"] = m.pop("idn")
        m["jT_in"] = m.pop("jT")
        m["AM_in"] = m.pop("AM")
        m["PM_in"] = m.pop("PM")
        m["NEGJ_in"] = m.pop("NEGJ")
        m["NEGJM_in"] = m.pop("NEGJM")
        m["SL8_in"] = m.pop("SL8")
        m["SLK8_in"] = m.pop("SLK8")
        m["onescol_in"] = m.pop("onescol")
        m.pop("cosrow"), m.pop("sinrow"), m.pop("colscale")
    res = run_bass_kernel_spmd(nc, in_maps, core_ids=list(range(NCORE)))
    return reassemble([r["yout"] for r in res.results])


# version 12
# speedup vs baseline: 1.1508x; 1.0206x over previous
import os
import numpy as np
import concourse.bass as bass
import concourse.mybir as mybir
from concourse.bass_utils import run_bass_kernel_spmd
from contextlib import ExitStack

F32 = mybir.dt.float32
BF16 = mybir.dt.bfloat16
AF = mybir.ActivationFunctionType
ALU = mybir.AluOpType
AX = mybir.AxisListType

NCORE = 8
T = 8192
D = 4096
TC = 1024
BLK = 512
NB = 16
NH = 32
HD = 128
NHL = 4
NE = 8
LR = 16
NEG = -1e30
EPS_LN = 1e-5
EPS_RMS = 1e-6
TT = 256
NTT = T // TT
DCH = 32
STAGES = int(os.environ.get("MK_STAGES", "4"))
TAPS = [t for t in os.environ.get("MK_TAPS", "").split(",") if t]


def blk_owner(b):
    return (b, 0) if b < 8 else (15 - b, 1)


def gcol(b):
    r, h = blk_owner(b)
    return 1024 * r + 512 * h



class Sem:
    def __init__(self, handle, name):
        self.h = handle
        self.name = name
        self.count = 0


class Buf:
    __slots__ = ("name", "writer", "readers")

    def __init__(self, name):
        self.name = name
        self.writer = None
        self.readers = []


class Qu:
    def __init__(self, name, sem, definc):
        self.name = name
        self.sem = sem
        self.definc = definc
        self.items = []
        self.waited = {}
        self.pending = []

    def _wait(self, sem, val):
        if val is None:
            raise AssertionError(f"{self.name}: unresolved pending event")
        if val <= 0:
            return
        if val > sem.count:
            raise AssertionError(
                f"{self.name}: wait on future count of {sem.name}")
        self._wait_emit(sem, val)

    def _wait_emit(self, sem, val):
        if self.waited.get(id(sem), -1) >= val:
            return
        self.waited[id(sem)] = val
        self.items.append(("w", sem, val))

    def _ev(self, ev):
        if ev is None:
            return
        sem, val = ev[0], ev[1]
        if isinstance(ev, list) and val is None:
            if sem is self.sem:
                return
            self.items.append(("wd", sem, ev))
            return
        self._wait(sem, val)

    def op(self, fn, reads=(), writes=(), inc=None, slot=None):
        for b in reads:
            if b is not None and b.writer is not None:
                self._ev(b.writer)
        for b in writes:
            if b is None:
                continue
            if b.writer is not None:
                self._ev(b.writer)
            for ev in b.readers:
                self._ev(ev)
        inc = self.definc if inc is None else inc
        if self.definc == 16:
            assert slot is not None, f"{self.name}: DMA op needs slot sem"
            sem = slot
        else:
            sem = self.sem
        if inc:
            sem.count += inc
            ev = (sem, sem.count)
            if sem is self.sem:
                for pb in self.pending:
                    pb[1] = self.sem.count
                self.pending = []
        else:
            ev = [self.sem, None]
            self.pending.append(ev)
        self.items.append(("o", fn, inc, sem))
        for b in writes:
            if b is not None:
                b.writer = ev
                b.readers = []
        for b in reads:
            if b is not None:
                b.readers.append(ev)
        return ev

    def flush(self, eng):
        for it in self.items:
            if it[0] == "w":
                _, sem, val = it
                eng.wait_ge(sem.h, val)
            elif it[0] == "wd":
                _, sem, ev = it
                assert ev[1] is not None, f"{self.name}: unresolved deferred wait"
                eng.wait_ge(sem.h, ev[1])
            else:
                _, fn, inc, sem = it
                inst = fn(eng)
                if inc:
                    inst.then_inc(sem.h, inc)


class BK:
    def __init__(self):
        nc = bass.Bass("TRN2", target_bir_lowering=False)
        self.nc = nc
        self.es = ExitStack()
        self.queues = {}
        for qn, definc in [("pe", 1), ("dve", 1), ("act", 1),
                           ("qs", 16), ("qg", 16)]:
            s = Sem(self.es.enter_context(nc.semaphore("s_" + qn)), qn)
            self.queues[qn] = Qu(qn, s, definc)
        self.pe = self.queues["pe"]
        self.dve = self.queues["dve"]
        self.act = self.queues["act"]
        self.qs = self.queues["qs"]
        self.qg = self.queues["qg"]
        self.ccs = []
        self.slotsems = {}
        self.np_inputs = {}

    def slot(self, buf):
        s = self.slotsems.get(id(buf))
        if s is None:
            s = Sem(self.es.enter_context(
                self.nc.semaphore(f"sl_{buf.name}")), f"sl_{buf.name}")
            self.slotsems[id(buf)] = s
        return s

    def sbuf(self, name, shape, dtype):
        h = self.es.enter_context(self.nc.sbuf_tensor(name, list(shape), dtype))
        return h, Buf(name)

    def dram_in(self, name, shape, dtype=F32):
        h = self.nc.dram_tensor(name, list(shape), dtype, kind="ExternalInput")
        return h, Buf(name)

    def dram_out(self, name, shape, dtype=F32):
        h = self.nc.dram_tensor(name, list(shape), dtype, kind="ExternalOutput")
        return h, Buf(name)

    def dram(self, name, shape, dtype):
        h = self.nc.dram_tensor(name, list(shape), dtype, kind="Internal")
        return h, Buf(name)

    def psum(self, name):
        h = self.es.enter_context(self.nc.psum_tensor(name, [128, 512], F32))
        return h, Buf(name)

    def ccsem(self):
        s = Sem(self.es.enter_context(
            self.nc.semaphore(f"cc{len(self.ccs)}")), f"cc{len(self.ccs)}")
        self.ccs.append(s)
        return s

    def finish(self):
        with self.nc.Block() as block:
            block.tensor(lambda e: self.queues["pe"].flush(e))
            block.vector(lambda e: self.queues["dve"].flush(e))
            block.scalar(lambda e: self.queues["act"].flush(e))
            block.sync(lambda e: self.queues["qs"].flush(e))
            block.gpsimd(lambda e: self.queues["qg"].flush(e))
        self.es.close()
        return self.nc



def host_consts():
    c = {}
    gpos = np.zeros(T, np.int64)
    for b in range(NB):
        gpos[gcol(b):gcol(b) + BLK] = np.arange(b * BLK, (b + 1) * BLK)
    pos = gpos.astype(np.float32)
    c["cosrow"] = np.cos(pos)[None, :]
    c["sinrow"] = np.sin(pos)[None, :]
    c["colones"] = np.ones((1, 128), np.float32)
    c["colscale"] = np.full((1, 128), float(HD) ** -0.5, np.float32)
    c["colneg"] = np.full((1, 128), -1.0, np.float32)
    c["iotacol"] = np.arange(128, dtype=np.float32)[None, :]
    c["idn"] = np.eye(128, dtype=np.float32)
    J = np.zeros((128, 128), np.float32)
    for i in range(64):
        J[2 * i, 2 * i + 1] = -1.0
        J[2 * i + 1, 2 * i] = 1.0
    c["jT"] = np.ascontiguousarray(J.T)
    i_ = np.arange(128)[:, None]
    j_ = np.arange(512)[None, :]
    jj = np.arange(128)[None, :]
    c["AM"] = np.where(jj <= i_, 0.0, NEG).astype(np.float32)
    c["PM"] = np.concatenate(
        [(128 * o + i_ <= j_).astype(np.float32) for o in range(4)],
        axis=1)
    c["NEGJ"] = np.broadcast_to(-j_.astype(np.float32), (128, 512)).copy()
    c["NEGJM"] = np.concatenate(
        [np.where(128 * o + i_ <= j_, -j_.astype(np.float32), NEG)
         for o in range(4)], axis=1).astype(np.float32)
    c["onescol"] = np.ones((128, 1), np.float32)
    return c


_CONSTS = host_consts()


def host_inputs(core, ins):
    hs = NHL * core
    d = {}
    x = np.asarray(ins["L0_attn_a_x"])[0]
    xT = np.ascontiguousarray(x.T)
    d["xT"] = np.ascontiguousarray(
        np.concatenate([xT[:, core * BLK:(core + 1) * BLK],
                        xT[:, (15 - core) * BLK:(16 - core) * BLK]], axis=1))
    tr = lambda a: np.ascontiguousarray(np.asarray(a).T.astype(np.float32))
    d["wqTs"] = tr(ins["L0_attn_a_wq"])[:, hs * HD:(hs + NHL) * HD]
    d["wkTs"] = tr(ins["L0_attn_a_wk"])[:, hs * HD:(hs + NHL) * HD]
    d["wvTs"] = tr(ins["L0_attn_a_wv"])[:, hs * HD:(hs + NHL) * HD]
    d["woT"] = tr(ins["L0_attn_a_wo"])
    d["bWqTs"] = tr(ins["L0_attn_b_Wq"])[:, hs * HD:(hs + NHL) * HD]
    d["bWkTs"] = tr(ins["L0_attn_b_Wk"])[:, core * HD:(core + 1) * HD]
    d["bWvTs"] = tr(ins["L0_attn_b_Wv"])[:, core * HD:(core + 1) * HD]
    d["bWoT"] = tr(ins["L0_attn_b_Wo"])
    d["WbT"] = tr(ins["L0_ffn_W_base"])
    d["WrT"] = tr(ins["L0_ffn_W_router"])
    d["AallT"] = np.ascontiguousarray(np.asarray(
        ins["L0_ffn_lora_A"]).transpose(2, 0, 1).reshape(D, NE * LR).astype(np.float32))
    d["Ball"] = np.ascontiguousarray(np.asarray(
        ins["L0_ffn_lora_B"]).transpose(0, 2, 1).reshape(NE * LR, D).astype(np.float32))
    d["bbase"] = np.ascontiguousarray(np.asarray(
        ins["L0_ffn_b_base"]).astype(np.float32).reshape(32, 128).T)
    d["qnw"] = np.asarray(ins["L0_attn_b_q_norm_w"]).astype(np.float32).reshape(128, 1)
    d["knw"] = np.asarray(ins["L0_attn_b_k_norm_w"]).astype(np.float32).reshape(128, 1)
    start = 2.0 ** (-2.0 ** (-(np.log2(NH) - 3)))
    slopes = np.array([start * start ** (h + 1 - 1) for h in range(NH)], np.float32)
    slopes = np.array([start * start ** i for i in range(NH)], np.float32)
    d["slopesrow"] = slopes[hs:hs + NHL][None, :].copy()
    sl8 = np.zeros((1, 16), np.float32)
    slk8 = np.zeros((1, 16), np.float32)
    for hp in range(2):
        for ko in range(4):
            for h in range(2):
                sl8[0, 8 * hp + 2 * ko + h] = slopes[hs + 2 * hp + h]
                slk8[0, 8 * hp + 2 * ko + h] = slopes[hs + 2 * hp + h] * 128.0 * ko
    d["SL8"] = sl8
    d["SLK8"] = slk8
    d["slopecols"] = np.broadcast_to(slopes[hs:hs + NHL][None, :],
                                     (128, NHL)).copy()
    d.update(_CONSTS)
    return {k: np.ascontiguousarray(v) for k, v in d.items()}


def reassemble(outs):
    full = np.empty((D, T), np.float32)
    for c in range(NCORE):
        y = outs[c]
        full[:, c * BLK:(c + 1) * BLK] = y[:, :BLK]
        full[:, (15 - c) * BLK:(16 - c) * BLK] = y[:, BLK:]
    return np.ascontiguousarray(full.T).reshape(1, T, D)



class Prog:
    def __init__(self, k: BK):
        self.k = k
        nc = k.nc
        di = k.dram_in
        self.xT = di("xT", [D, TC])
        self.wqTs = di("wqTs", [D, NHL * HD])
        self.wkTs = di("wkTs", [D, NHL * HD])
        self.wvTs = di("wvTs", [D, NHL * HD])
        self.woT = di("woT", [D, D])
        self.bWqTs = di("bWqTs", [D, NHL * HD])
        self.bWkTs = di("bWkTs", [D, HD])
        self.bWvTs = di("bWvTs", [D, HD])
        self.bWoT = di("bWoT", [D, D])
        self.WbT = di("WbT", [D, D])
        self.WrT = di("WrT", [D, NE])
        self.AallT = di("AallT", [D, NE * LR])
        self.Ball = di("Ball", [NE * LR, D])
        self.bbase = di("bbase", [128, 32])
        self.qnw = di("qnw", [128, 1])
        self.knw = di("knw", [128, 1])
        self.slopesrow = di("slopesrow", [1, NHL])
        self.SL8_in = di("SL8_in", [1, 16])
        self.SLK8_in = di("SLK8_in", [1, 16])
        self.slopecols = di("slopecols", [128, NHL])
        self.cosbc = di("cosbc", [128, T])
        self.sinbc = di("sinbc", [128, T])
        self.colones = di("colones", [1, 128])
        self.colneg = di("colneg", [1, 128])
        self.iotacol = di("iotacol", [1, 128])
        self.idn_in = di("idn_in", [128, 128])
        self.jT_in = di("jT_in", [128, 128])
        self.AM_in = di("AM_in", [128, 128])
        self.PM_in = di("PM_in", [128, 4 * 512])
        self.NEGJ_in = di("NEGJ_in", [128, 512])
        self.NEGJM_in = di("NEGJM_in", [128, 4 * 512])
        self.onescol_in = di("onescol_in", [128, 1])
        self.REP_in = di("REP_in", [NE, 128])
        self.yout = k.dram_out("yout", [D, TC], F32)
        dr = k.dram
        self.xnsh1 = [dr(f"xnsh1h{h}", [D, BLK], BF16) for h in range(2)]
        self.XN1 = [dr(f"XN1h{h}", [NCORE * D, BLK], BF16) for h in range(2)]
        self.xnsh3 = [dr(f"xnsh3h{h}", [D, BLK], BF16) for h in range(2)]
        self.XN3 = [dr(f"XN3h{h}", [NCORE * D, BLK], BF16) for h in range(2)]
        self.qa = dr("qa", [NHL * HD, T], BF16)
        self.ka = dr("ka", [NHL * HD, T], BF16)
        self.va = dr("va", [T, NHL * HD], BF16)
        self.osh_a = [dr(f"osh_a{h}", [NCORE * BLK, BLK], BF16) for h in range(2)]
        self.OG_a = [dr(f"OG_a{h}", [NCORE * BLK, BLK], BF16) for h in range(2)]
        self.qb = dr("qb", [NHL * HD, T], BF16)
        self.kb = dr("kb", [HD, T], BF16)
        self.vb = dr("vb", [T, HD], BF16)
        self.osh_b = [dr(f"osh_b{h}", [NCORE * BLK, BLK], BF16) for h in range(2)]
        self.OG_b = [dr(f"OG_b{h}", [NCORE * BLK, BLK], BF16) for h in range(2)]
        self.y1 = dr("y1", [D, TC], F32)
        self.y2 = dr("y2", [D, TC], F32)
        self.y3 = dr("y3", [D, TC], F32)
        sb = k.sbuf
        self.xng = [sb(f"xng{i}", [128, DCH * TT], BF16) for i in range(2)]
        self.wsl = [sb(f"wsl{i}", [128, DCH * 256], BF16) for i in range(2)]
        self.rhs2 = [sb(f"rhs2{i}", [128, DCH * 256], BF16) for i in range(2)]
        self.hwT = sb("hwT", [128, TC], BF16)
        self.wt16sb = sb("wt16sb", [128, 512], F32)
        self.Ballsb = sb("Ballsb", [128, D], BF16)
        self.wrssb = sb("wrssb", [128, DCH * NE], BF16)
        self.wTar = sb("wTar", [NE, 512], F32)
        self.lg = sb("lg", [NE, 512], F32)
        self.rsm = [sb(f"rsm{i}", [128, NE], F32) for i in range(6)]
        self.rcol = [sb(f"rcol{i}", [128, 1], F32) for i in range(4)]
        self.qt = [sb(f"qt{i}", [128, 2 * 512], BF16) for i in range(1)]
        self.kt = [sb(f"kt{i}", [128, 2 * 512], BF16) for i in range(2)]
        self.vt = [sb(f"vt{i}", [128, 4 * 128 * 2], BF16) for i in range(2)]
        self.pbuf = [sb(f"pbuf{i}", [128, 512], BF16) for i in range(4)]
        self.sscr = [sb(f"sscr{i}", [128, 512], F32) for i in range(3)]
        self.kna = sb("kna", [1, NHL * 32], F32)
        self.qna = sb("qna", [1, NHL * 32], F32)
        self.crow = sb("crow", [1, 2], F32)
        self.cbneg = sb("cbneg", [128, 2], F32)
        self.bmsb = sb("bmsb", [128, 2], F32)
        self.SL8 = sb("SL8", [1, 16], F32)
        self.SLK8 = sb("SLK8", [1, 16], F32)
        self.crow8 = sb("crow8", [1, 8], F32)
        self.bm8 = [sb(f"bm8{i}", [128, 8], F32) for i in range(2)]
        self.rowtmp = sb("rowtmp", [1, 8], F32)
        self.invlb = sb("invlb", [128, 512], F32)
        self.otst = [sb(f"otst{i}", [128, 512], BF16) for i in range(2)]
        self.cbt = [sb(f"cbt{i}", [128, TT], F32) for i in range(2)]
        self.sbt = [sb(f"sbt{i}", [128, TT], F32) for i in range(2)]
        self.ust = [sb(f"ust{i}", [128, TT], BF16) for i in range(1)]
        self.vscr = [sb(f"vscr{i}", [128, TT], F32) for i in range(2)]
        self.qkst = [sb(f"qkst{i}", [128, 512], BF16) for i in range(2)]
        self.sq256 = [sb(f"sq256{i}", [128, TT], F32) for i in range(1)]
        self.rmrow = sb("rmrow", [1, 2 * TT], F32)
        self.invbc = sb("invbc", [128, TT], F32)
        self.ybf = [sb(f"ybf{i}", [128, TC], BF16) for i in range(2)]
        self.sqt = [sb(f"sqt{i}", [128, TC], BF16) for i in range(1)]
        self.rows = sb("rows", [1, 3 * TC], F32)
        self.invb = sb("invb", [128, TC], BF16)
        self.minvb = sb("minvb", [128, TC], BF16)
        self.xnst = [sb(f"xnst{i}", [128, TC], BF16) for i in range(2)]
        self.rtile = [sb(f"rtile{i}", [128, 512], F32) for i in range(2)]
        self.ytile = [sb(f"ytile{i}", [128, 512], F32) for i in range(2)]
        self.yc = [sb(f"yc{i}", [128, 512], BF16) for i in range(2)]
        self.sqc = [sb(f"sqc{i}", [128, 512], BF16) for i in range(2)]
        self.idn = sb("idn", [128, 128], F32)
        self.jTb = sb("jTb", [128, 128], BF16)
        self.AMt = sb("AMt", [128, 128], F32)
        self.PMt = sb("PMt", [128, 4 * 512], BF16)
        self.NEGJt = sb("NEGJt", [128, 512], F32)
        self.NEGJMt = sb("NEGJMt", [128, 4 * 512], F32)
        self.slc = sb("slc", [128, NHL], F32)
        self.slr = sb("slr", [1, NHL], F32)
        self.c1 = sb("c1", [1, 128], F32)
        self.cn1 = sb("cn1", [1, 128], F32)
        self.iotar = sb("iotar", [1, 128], F32)
        self.ones_f = sb("ones_f", [128, 1], F32)
        self.ones_b = sb("ones_b", [128, 1], BF16)
        self.bb = sb("bb", [128, 32], F32)
        self.qnwsb = sb("qnwsb", [128, 1], F32)
        self.knwsb = sb("knwsb", [128, 1], F32)
        self.REPsb = sb("REPsb", [NE, 128], F32)
        self.B = [k.psum(f"bank{i}") for i in range(8)]

    def mm(self, bank, out_ap, lhsT, rhs, reads, start, stop, inc=True):
        self.k.pe.op(
            lambda e, o=out_ap, l=lhsT, r=rhs, s=start, t=stop: e.matmul(
                o, l, r, start=s, stop=t, skip_group_check=True),
            reads=reads, writes=[bank[1]], inc=1 if inc else 0)

    def transpose(self, bank, out_ap, in_ap, reads, p=128):
        self.k.pe.op(
            lambda e, o=out_ap, i=in_ap, p=p: e.transpose(o, i, self.idn[0][0:p, 0:p]),
            reads=list(reads) + [self.idn[1]], writes=[bank[1]], inc=1)

    def sqrt_row(self, t, n=128):
        self.k.act.op(lambda e, o=t[0][0:1, 0:n], x=t[0][0:1, 0:n]:
                      e.activation(o, x, AF.Sqrt), reads=[t[1]], writes=[t[1]])

    def load_consts(self):
        k = self.k
        qs, qg = k.qs, k.qg

        def ld(dst, dstbuf, src, srcbuf, cast=False):
            q = qg if cast else qs
            q.op(lambda e, d=dst, s=src: e.dma_start(out=d, in_=s),
                 reads=[srcbuf], writes=[dstbuf], slot=self.k.slot(dstbuf))

        ld(self.idn[0][:, :], self.idn[1], self.idn_in[0].ap(), self.idn_in[1])
        ld(self.jTb[0][:, :], self.jTb[1], self.jT_in[0].ap(), self.jT_in[1], cast=True)
        ld(self.AMt[0][:, :], self.AMt[1], self.AM_in[0].ap(), self.AM_in[1])
        ld(self.PMt[0][:, :], self.PMt[1], self.PM_in[0].ap(), self.PM_in[1], cast=True)
        ld(self.NEGJt[0][:, :], self.NEGJt[1], self.NEGJ_in[0].ap(), self.NEGJ_in[1])
        ld(self.NEGJMt[0][:, :], self.NEGJMt[1], self.NEGJM_in[0].ap(), self.NEGJM_in[1])
        ld(self.slc[0][:, :], self.slc[1], self.slopecols[0].ap(), self.slopecols[1])
        ld(self.slr[0][:, :], self.slr[1], self.slopesrow[0].ap(), self.slopesrow[1])
        ld(self.SL8[0][:, :], self.SL8[1], self.SL8_in[0].ap(), self.SL8_in[1])
        ld(self.SLK8[0][:, :], self.SLK8[1], self.SLK8_in[0].ap(), self.SLK8_in[1])
        ld(self.c1[0][:, :], self.c1[1], self.colones[0].ap(), self.colones[1])
        ld(self.cn1[0][:, :], self.cn1[1], self.colneg[0].ap(), self.colneg[1])
        ld(self.iotar[0][:, :], self.iotar[1], self.iotacol[0].ap(), self.iotacol[1])
        ld(self.ones_f[0][:, :], self.ones_f[1], self.onescol_in[0].ap(), self.onescol_in[1])
        ld(self.ones_b[0][:, :], self.ones_b[1], self.onescol_in[0].ap(), self.onescol_in[1], cast=True)
        ld(self.bb[0][:, :], self.bb[1], self.bbase[0].ap(), self.bbase[1])
        ld(self.qnwsb[0][:, :], self.qnwsb[1], self.qnw[0].ap(), self.qnw[1])
        ld(self.knwsb[0][:, :], self.knwsb[1], self.knw[0].ap(), self.knw[1])
        ld(self.REPsb[0][:, :], self.REPsb[1], self.REP_in[0].ap(), self.REP_in[1])
        ld(self.wrssb[0][:, :].rearrange("p (c e) -> p c e", c=DCH), self.wrssb[1],
           self.WrT[0].ap().rearrange("(c p) e -> p c e", p=128), self.WrT[1], cast=True)
        ld(self.Ballsb[0][:, :], self.Ballsb[1], self.Ball[0].ap(), self.Ball[1], cast=True)

    def ln_stats_scan(self, src):
        k = self.k

        def loads(c):
            yb = self.ybf[c % 2]
            k.qg.op(lambda e, d=yb[0][:, :], s=src[0].ap()[128 * c:128 * (c + 1), :]:
                    e.dma_start(out=d, in_=s), reads=[src[1]], writes=[yb[1]],
                    slot=k.slot(yb[1]))

        loads(0)
        for c in range(DCH):
            i = c % 2
            yb = self.ybf[i]
            if c + 1 < DCH:
                loads(c + 1)
            sq = self.sqt[0]
            k.act.op(lambda e, o=sq[0][:, :], x=yb[0][:, :]:
                     e.activation(o, x, AF.Square), reads=[yb[1]], writes=[sq[1]])
            for g in range(2):
                bS, bQ = self.B[2 + g], self.B[4 + g]
                self.mm(bS, bS[0][0:1, 0:512], self.ones_b[0][:, :],
                        yb[0][:, 512 * g:512 * (g + 1)],
                        [self.ones_b[1], yb[1]], start=(c == 0), stop=(c == DCH - 1))
                self.mm(bQ, bQ[0][0:1, 0:512], self.ones_b[0][:, :],
                        sq[0][:, 512 * g:512 * (g + 1)],
                        [self.ones_b[1], sq[1]], start=(c == 0), stop=(c == DCH - 1))

    def ln_rows_bcast(self, eps=EPS_LN):
        k = self.k
        R = self.rows[0]
        rb = self.rows[1]
        for g in range(2):
            M = R[0:1, 512 * g:512 * (g + 1)]
            A = R[0:1, TC + 512 * g: TC + 512 * (g + 1)]
            Bs = R[0:1, 2 * TC + 512 * g: 2 * TC + 512 * (g + 1)]
            bS, bQ = self.B[2 + g], self.B[4 + g]
            k.dve.op(lambda e, o=M, x=bS[0][0:1, 0:512]: e.tensor_scalar(
                o, x, 1.0 / D, None, ALU.mult), reads=[bS[1]], writes=[rb])
            k.dve.op(lambda e, o=A, x=bQ[0][0:1, 0:512]: e.tensor_scalar(
                o, x, 1.0 / D, None, ALU.mult), reads=[bQ[1]], writes=[rb])
            k.dve.op(lambda e, o=Bs, a=M, b=M: e.tensor_tensor(o, a, b, ALU.mult),
                     reads=[rb], writes=[rb])
            k.dve.op(lambda e, o=A, a=A, b=Bs: e.tensor_tensor(o, a, b, ALU.subtract),
                     reads=[rb], writes=[rb])
            k.dve.op(lambda e, o=Bs, x=A: e.tensor_scalar(
                o, x, float(eps), None, ALU.add), reads=[rb], writes=[rb])
            k.act.op(lambda e, o=A, x=Bs: e.activation(o, x, AF.Ln),
                     reads=[rb], writes=[rb])
            k.dve.op(lambda e, o=Bs, x=A: e.tensor_scalar(
                o, x, -0.5, None, ALU.mult), reads=[rb], writes=[rb])
            k.act.op(lambda e, o=A, x=Bs: e.activation(o, x, AF.Exp),
                     reads=[rb], writes=[rb])
            k.dve.op(lambda e, o=Bs, a=M, b=A: e.tensor_tensor(o, a, b, ALU.mult),
                     reads=[rb], writes=[rb])
            INV, MINV = A, Bs
            b6, b7 = self.B[6], self.B[7]
            self.mm(b6, b6[0][:, 0:512], self.c1[0][:, :], INV,
                    [self.c1[1], rb], start=True, stop=True)
            k.act.op(lambda e, o=self.invb[0][:, 512 * g:512 * (g + 1)],
                     x=b6[0][:, 0:512]: e.activation(o, x, AF.Copy),
                     reads=[b6[1]], writes=[self.invb[1]])
            self.mm(b7, b7[0][:, 0:512], self.c1[0][:, :], MINV,
                    [self.c1[1], rb], start=True, stop=True)
            k.act.op(lambda e, o=self.minvb[0][:, 512 * g:512 * (g + 1)],
                     x=b7[0][:, 0:512]: e.activation(o, x, AF.Copy),
                     reads=[b7[1]], writes=[self.minvb[1]])

    def ln_norm_to_shard(self, src, dsts, fulls):
        k = self.k
        items = [(h, c) for h in range(2) for c in range(DCH)]

        def loads(ix):
            h, c = items[ix]
            yb = self.ybf[ix % 2]
            k.qg.op(lambda e, d=yb[0][:, 0:512],
                    s=src[0].ap()[128 * c:128 * (c + 1), 512 * h:512 * (h + 1)]:
                    e.dma_start(out=d, in_=s), reads=[src[1]], writes=[yb[1]],
                    slot=k.slot(yb[1]))

        loads(0)
        for ix, (h, c) in enumerate(items):
            yb = self.ybf[ix % 2]
            if ix + 1 < len(items):
                loads(ix + 1)
            sq = self.sqt[0]
            k.dve.op(lambda e, o=sq[0][:, 0:512], a=yb[0][:, 0:512],
                     b=self.invb[0][:, 512 * h:512 * (h + 1)]:
                     e.tensor_tensor(o, a, b, ALU.mult),
                     reads=[yb[1], self.invb[1]], writes=[sq[1]])
            xn = self.xnst[ix % 2]
            k.dve.op(lambda e, o=xn[0][:, 0:512], a=sq[0][:, 0:512],
                     b=self.minvb[0][:, 512 * h:512 * (h + 1)]:
                     e.tensor_tensor(o, a, b, ALU.subtract),
                     reads=[sq[1], self.minvb[1]], writes=[xn[1]])
            k.qs.op(lambda e, d=dsts[h][0].ap()[128 * c:128 * (c + 1), :],
                    s=xn[0][:, 0:512]: e.dma_start(out=d, in_=s),
                    reads=[xn[1]], writes=[dsts[h][1]], slot=k.slot(xn[1]))
            if c == DCH - 1:
                for j in range(2):
                    sl = k.slot(self.xnst[j][1])
                    if sl.count:
                        k.qg._wait_emit(sl, sl.count)
                self.collective("AllGather", dsts[h], fulls[h])

    def ln_norm_to_half(self, src, g):
        k = self.k

        def loads(c):
            yb = self.yc[c % 2]
            k.qg.op(lambda e, d=yb[0][:, :],
                    s=src[0].ap()[128 * c:128 * (c + 1), 512 * g:512 * (g + 1)]:
                    e.dma_start(out=d, in_=s), reads=[src[1]], writes=[yb[1]],
                    slot=k.slot(yb[1]))

        loads(0)
        for c in range(DCH):
            i = c % 2
            yb = self.yc[i]
            if c + 1 < DCH:
                loads(c + 1)
            sq = self.sqc[i]
            k.dve.op(lambda e, o=sq[0][:, :], a=yb[0][:, :],
                     b=self.invb[0][:, 512 * g:512 * (g + 1)]:
                     e.tensor_tensor(o, a, b, ALU.mult),
                     reads=[yb[1], self.invb[1]], writes=[sq[1]])
            xa, xbuf = self.xn_chunk(g, c)
            k.dve.op(lambda e, o=xa,
                     a=sq[0][:, :], b=self.minvb[0][:, 512 * g:512 * (g + 1)]:
                     e.tensor_tensor(o, a, b, ALU.subtract),
                     reads=[sq[1], self.minvb[1]], writes=[xbuf])

    def collective(self, kind, shard, full, in_ap=None, out_ap=None):
        k = self.k
        cs = k.ccsem()
        q = k.qg
        if shard[1].writer is not None:
            q._ev(shard[1].writer)
        for ev in full[1].readers:
            q._ev(ev)
        if full[1].writer is not None:
            q._ev(full[1].writer)
        in_ap = shard[0].ap() if in_ap is None else in_ap
        out_ap = full[0].ap() if out_ap is None else out_ap

        def fn(e, kind=kind, si=in_ap, fo=out_ap, cs=cs):
            inst = e.collective_compute(
                kind, ALU.bypass, replica_groups=[list(range(NCORE))],
                ins=[si.opt()], outs=[fo.opt()])
            inst.then_inc(cs.h, 1)
            return inst
        q.items.append(("o", fn, 0, None))
        cs.count = 1
        full[1].writer = (cs, 1)
        full[1].readers = []
        shard[1].readers.append((cs, 1))
        return cs

    def load_wslice(self, wt, cols0, ncols, bufs, cast=True):
        k = self.k
        src = wt[0].ap()[:, cols0:cols0 + ncols].rearrange("(c p) o -> p c o", p=128)
        npieces = (ncols + 255) // 256
        q = k.qg if cast else k.qs
        nj = 4 if cast else 2
        for pc in range(npieces):
            w = min(256, ncols - 256 * pc)
            b_ = bufs[pc]
            bi = self.wsl[b_] if isinstance(b_, int) else b_
            for j in range(nj):
                sp = DCH // nj
                q.op(lambda e,
                     d=bi[0][:, :].rearrange("p (c o) -> p c o", c=DCH)[:, sp * j:sp * (j + 1), 0:w],
                     s=src[:, sp * j:sp * (j + 1), 256 * pc:256 * pc + w]:
                     e.dma_start(out=d, in_=s), reads=[wt[1]], writes=[bi[1]],
                     slot=k.slot(bi[1]))

    def qkv_scan(self, XN, streams, vstream=None):
        k = self.k
        for s_ in streams:
            self.load_wslice(s_["wt"], 0, s_["ncols"], s_["wb"])
        if vstream is not None:
            self.load_wslice(vstream["wt"], 0, vstream["ncols"], vstream["wb"])
        any_rope = any(s_["kind"] == "rope" for s_ in streams)

        def wbuf(s_, j):
            b_ = s_["wb"][j]
            return self.wsl[b_] if isinstance(b_, int) else b_

        def loads(t_):
            t0 = t_ * TT
            r = t0 // TC
            wc = t0 % TC
            i = t_ % 2
            xg = self.xng[i]
            XNh = XN[wc // 512]
            wcc = wc % 512
            for j in range(4):
                k.qs.op(lambda e,
                        d=xg[0][:, :].rearrange("p (c t) -> p c t", c=DCH)[:, 8 * j:8 * (j + 1), :],
                        s=XNh[0].ap()[D * r + 1024 * j:D * r + 1024 * (j + 1),
                                      wcc:wcc + TT].rearrange("(c p) t -> p c t", p=128):
                        e.dma_start(out=d, in_=s), reads=[XNh[1]], writes=[xg[1]],
                        slot=k.slot(xg[1]))
            if any_rope:
                cb, sb_ = self.cbt[i], self.sbt[i]
                k.qs.op(lambda e, d=cb[0][:, :], s=self.cosbc[0].ap()[:, t0:t0 + TT]:
                        e.dma_start(out=d, in_=s), reads=[self.cosbc[1]], writes=[cb[1]],
                        slot=k.slot(cb[1]))
                k.qs.op(lambda e, d=sb_[0][:, :], s=self.sinbc[0].ap()[:, t0:t0 + TT]:
                        e.dma_start(out=d, in_=s), reads=[self.sinbc[1]], writes=[sb_[1]],
                        slot=k.slot(sb_[1]))

        def compute(t_):
            t0 = t_ * TT
            i = t_ % 2
            xg = self.xng[i]
            cb, sb_ = self.cbt[i], self.sbt[i]
            mmctr = t_
            for s_ in streams:
                kind = s_["kind"]
                rms_qscale = s_["rms_qscale"]
                nrm_w = s_["nrm_w"]
                knorm = s_["knorm"]
                dst = s_["dst"]
                for oc in range(s_["ncols"] // 128):
                    bk = self.B[mmctr % 2]
                    mmctr += 1
                    ps = bk[0][:, 0:TT]
                    wb = wbuf(s_, oc // 2)
                    off = (oc % 2) * 128
                    for c in range(DCH):
                        self.mm(bk, ps, wb[0][:, 256 * c + off:256 * c + off + 128],
                                xg[0][:, TT * c:TT * (c + 1)], [wb[1], xg[1]],
                                start=(c == 0), stop=(c == DCH - 1),
                                inc=(c == DCH - 1))
                    if kind == "rope":
                        if rms_qscale:
                            pq = self.sq256[0]
                            k.dve.op(lambda e, o=pq[0][:, :], x=ps:
                                     e.tensor_scalar(o, x, float(HD) ** -0.5, None, ALU.mult),
                                     reads=[bk[1]], writes=[pq[1]])
                            qsrc_ap, qsrc_b = pq[0][:, :], pq[1]
                        else:
                            qsrc_ap, qsrc_b = ps, bk[1]
                        us = self.ust[0]
                        k.dve.op(lambda e, o=us[0][:, :], a=qsrc_ap, b=sb_[0][:, :]:
                                 e.tensor_tensor(o, a, b, ALU.mult),
                                 reads=[qsrc_b, sb_[1]], writes=[us[1]])
                        bj = self.B[6 + (oc % 2)]
                        self.mm(bj, bj[0][:, 0:TT], self.jTb[0][:, :], us[0][:, :],
                                [self.jTb[1], us[1]], start=True, stop=True)
                        vs = self.vscr[i]
                        k.dve.op(lambda e, o=vs[0][:, :], a=qsrc_ap, b=cb[0][:, :]:
                                 e.tensor_tensor(o, a, b, ALU.mult),
                                 reads=[qsrc_b, cb[1]], writes=[vs[1]])
                        st = self.qkst[i]
                        k.dve.op(lambda e, o=st[0][:, 0:TT], a=vs[0][:, :], b=bj[0][:, 0:TT]:
                                 e.tensor_tensor(o, a, b, ALU.add),
                                 reads=[vs[1], bj[1]], writes=[st[1]])
                    elif kind == "rms":
                        sq = self.sq256[0]
                        k.act.op(lambda e, o=sq[0][:, :], x=ps:
                                 e.activation(o, x, AF.Square), reads=[bk[1]], writes=[sq[1]])
                        b6 = self.B[6]
                        self.mm(b6, b6[0][0:1, 0:TT], self.ones_f[0][:, :], sq[0][:, :],
                                [self.ones_f[1], sq[1]], start=True, stop=True)
                        rm = self.rmrow
                        ra = rm[0][0:1, 0:TT]
                        rb2 = rm[0][0:1, TT:2 * TT]
                        k.dve.op(lambda e, o=ra, x=b6[0][0:1, 0:TT]:
                                 e.tensor_scalar(o, x, 1.0 / HD, float(EPS_RMS),
                                                 ALU.mult, ALU.add),
                                 reads=[b6[1]], writes=[rm[1]])
                        k.act.op(lambda e, o=rb2, x=ra: e.activation(o, x, AF.Ln),
                                 reads=[rm[1]], writes=[rm[1]])
                        qsc = float(np.log(HD ** -0.5)) if rms_qscale else 0.0
                        k.dve.op(lambda e, o=ra, x=rb2, q=qsc:
                                 e.tensor_scalar(o, x, -0.5, q, ALU.mult, ALU.add),
                                 reads=[rm[1]], writes=[rm[1]])
                        k.act.op(lambda e, o=rb2, x=ra: e.activation(o, x, AF.Exp),
                                 reads=[rm[1]], writes=[rm[1]])
                        b7 = self.B[7]
                        self.mm(b7, b7[0][:, 0:TT], self.c1[0][:, :], rb2,
                                [self.c1[1], rm[1]], start=True, stop=True)
                        ib = self.invbc
                        k.act.op(lambda e, o=ib[0][:, :], x=b7[0][:, 0:TT]:
                                 e.activation(o, x, AF.Copy), reads=[b7[1]], writes=[ib[1]])
                        vs = self.vscr[i]
                        k.dve.op(lambda e, o=vs[0][:, :], a=ps, b=ib[0][:, :]:
                                 e.tensor_tensor(o, a, b, ALU.mult),
                                 reads=[bk[1], ib[1]], writes=[vs[1]])
                        st = self.qkst[i]
                        k.dve.op(lambda e, o=st[0][:, 0:TT], x=vs[0][:, :], s=nrm_w[0][:, :]:
                                 e.tensor_scalar(o, x, s, None, ALU.mult),
                                 reads=[vs[1], nrm_w[1]], writes=[st[1]])
                    st = self.qkst[i]
                    k.qs.op(lambda e, d=dst[0].ap()[128 * oc:128 * (oc + 1), t0:t0 + TT],
                            s=st[0][:, 0:TT]: e.dma_start(out=d, in_=s),
                            reads=[st[1]], writes=[dst[1]], slot=k.slot(st[1]))
                    if knorm is not None:
                        r_ = t0 // TC
                        gh = (t0 % TC) // BLK
                        true_b = r_ if gh == 0 else 15 - r_
                        col = oc * 32 + true_b * 2 + (t0 % BLK) // TT
                        usq = self.ust[0]
                        k.act.op(lambda e, o=usq[0][:, :], x=st[0][:, 0:TT]:
                                 e.activation(o, x, AF.Square), reads=[st[1]], writes=[usq[1]])
                        b5 = self.B[5]
                        self.mm(b5, b5[0][0:1, 0:TT], self.ones_b[0][:, :], usq[0][:, :],
                                [self.ones_b[1], usq[1]], start=True, stop=True)
                        k.dve.op(lambda e, o=knorm[0][0:1, col:col + 1], x=b5[0][0:1, 0:TT]:
                                 e.reduce_max(o, x, AX.X), reads=[b5[1]], writes=[knorm[1]])
            if vstream is not None:
                ncols = vstream["ncols"]
                dst = vstream["dst"]
                w0 = wbuf(vstream, 0)
                w1 = wbuf(vstream, 1) if ncols > 256 else None
                for ts_ in range(2):
                    bk = self.B[mmctr % 2]
                    mmctr += 1
                    ps = bk[0][:, 0:ncols]
                    for c in range(DCH):
                        first = (c == 0)
                        if ncols > 256:
                            self.mm(bk, bk[0][:, 0:256],
                                    xg[0][:, TT * c + 128 * ts_:TT * c + 128 * ts_ + 128],
                                    w0[0][:, 256 * c:256 * (c + 1)],
                                    [xg[1], w0[1]], start=first, stop=False,
                                    inc=False)
                            self.mm(bk, bk[0][:, 256:512],
                                    xg[0][:, TT * c + 128 * ts_:TT * c + 128 * ts_ + 128],
                                    w1[0][:, 256 * c:256 * (c + 1)],
                                    [xg[1], w1[1]], start=False,
                                    stop=(c == DCH - 1), inc=(c == DCH - 1))
                        else:
                            self.mm(bk, ps,
                                    xg[0][:, TT * c + 128 * ts_:TT * c + 128 * ts_ + 128],
                                    w0[0][:, 256 * c:256 * c + ncols],
                                    [xg[1], w0[1]], start=first,
                                    stop=(c == DCH - 1), inc=(c == DCH - 1))
                    st = self.qkst[i]
                    k.act.op(lambda e, o=st[0][:, 0:ncols], x=ps:
                             e.activation(o, x, AF.Copy), reads=[bk[1]], writes=[st[1]])
                    k.qs.op(lambda e, d=dst[0].ap()[t0 + 128 * ts_:t0 + 128 * (ts_ + 1), 0:ncols],
                            s=st[0][:, 0:ncols]: e.dma_start(out=d, in_=s),
                            reads=[st[1]], writes=[dst[1]], slot=k.slot(st[1]))

        order = ([t_ for t_ in range(NTT) if t_ % 4 < 2]
                 + [t_ for t_ in range(NTT) if t_ % 4 >= 2])
        loads(order[0])
        for ix, t_ in enumerate(order):
            if ix + 1 < NTT:
                loads(order[ix + 1])
            compute(t_)

    def flash(self, which):
        k = self.k
        isa = which == "a"
        qten = self.qa if isa else self.qb
        kten = self.ka if isa else self.kb
        vten = self.va if isa else self.vb
        osh = self.osh_a if isa else self.osh_b
        og = self.OG_a if isa else self.OG_b
        ew = 256 if isa else 128
        for b in range(NB):
            q0 = gcol(b)
            dst_d, half_b = blk_owner(b)
            for hp in range(2):
                qt = self.qt[0]
                k.qs.op(lambda e, d=qt[0][:, :].rearrange("p (h q) -> p h q", h=2),
                        s=qten[0].ap()[256 * hp:256 * (hp + 1), q0:q0 + 512]
                        .rearrange("(h p) q -> p h q", p=128):
                        e.dma_start(out=d, in_=s), reads=[qten[1]], writes=[qt[1]],
                        slot=k.slot(qt[1]))
                b6, b7 = self.B[6], self.B[7]
                cr = self.crow
                rm = self.rmrow
                for h in range(2):
                    c0q = 32 * (2 * hp + h)
                    c0k = 32 * (2 * hp + h) if isa else 0
                    k.dve.op(lambda e, o=rm[0][0:1, h:h + 1],
                             x=self.qna[0][0:1, c0q + 2 * b:c0q + 2 * b + 2]:
                             e.reduce_max(o, x, AX.X), reads=[self.qna[1]], writes=[rm[1]])
                    k.dve.op(lambda e, o=rm[0][0:1, 2 + h:3 + h],
                             x=self.kna[0][0:1, c0k:c0k + 2 * b + 2]:
                             e.reduce_max(o, x, AX.X), reads=[self.kna[1]], writes=[rm[1]])
                    k.dve.op(lambda e, o=cr[0][0:1, h:h + 1], a=rm[0][0:1, h:h + 1],
                             b_=rm[0][0:1, 2 + h:3 + h]:
                             e.tensor_tensor(o, a, b_, ALU.mult),
                             reads=[rm[1]], writes=[cr[1]])
                if not isa:
                    for ko in range(4):
                        k.dve.op(lambda e, o=self.crow8[0][0:1, 2 * ko:2 * ko + 2],
                                 x=cr[0][0:1, 0:2]: e.tensor_copy(o, x),
                                 reads=[cr[1]], writes=[self.crow8[1]])
                if isa:
                    cbn = self.cbneg
                    for h in range(2):
                        self.mm(b6, b6[0][:, 0:1], self.cn1[0][:, :],
                                cr[0][0:1, h:h + 1], [self.cn1[1], cr[1]],
                                start=True, stop=True)
                        k.act.op(lambda e, o=cbn[0][:, h:h + 1], x=b6[0][:, 0:1]:
                                 e.activation(o, x, AF.Copy),
                                 reads=[b6[1]], writes=[cbn[1]])
                OT = (self.B[4], self.B[5])
                BL = (self.B[6], self.B[7])
                STB = ((self.B[2], self.B[3], self.B[1], self.B[0]) if isa
                       else (self.B[2], self.B[3], self.B[1]))
                nstb = len(STB)
                B0 = self.B[0]
                nks = 4 * (b + 1)
                steps = [(kb, ko, h) for kb in range(b + 1)
                         for ko in range(4) for h in range(2)]
                nstep = len(steps)

                def load_kb(kb):
                    kt = self.kt[kb % 2]
                    vt = self.vt[kb % 2]
                    kk0 = gcol(kb)
                    if isa:
                        k.qs.op(lambda e, d=kt[0][:, :].rearrange("p (h q) -> p h q", h=2),
                                s=kten[0].ap()[256 * hp:256 * (hp + 1), kk0:kk0 + 512]
                                .rearrange("(h p) q -> p h q", p=128):
                                e.dma_start(out=d, in_=s), reads=[kten[1]], writes=[kt[1]],
                                slot=k.slot(kt[1]))
                        k.qs.op(lambda e,
                                d=vt[0][:, 0:4 * ew].rearrange("p (c x) -> p c x", c=4),
                                s=vten[0].ap()[kk0:kk0 + 512, 256 * hp:256 * (hp + 1)]
                                .rearrange("(c p) x -> p c x", p=128):
                                e.dma_start(out=d, in_=s), reads=[vten[1]], writes=[vt[1]],
                                slot=k.slot(vt[1]))
                    else:
                        k.qs.op(lambda e, d=kt[0][:, 0:512],
                                s=kten[0].ap()[:, kk0:kk0 + 512]:
                                e.dma_start(out=d, in_=s), reads=[kten[1]], writes=[kt[1]],
                                slot=k.slot(kt[1]))
                        k.qs.op(lambda e,
                                d=vt[0][:, 0:4 * ew].rearrange("p (c x) -> p c x", c=4),
                                s=vten[0].ap()[kk0:kk0 + 512, :]
                                .rearrange("(c p) x -> p c x", p=128):
                                e.dma_start(out=d, in_=s), reads=[vten[1]], writes=[vt[1]],
                                slot=k.slot(vt[1]))

                def bias_kb(kb):
                    if isa:
                        return
                    rt = self.rowtmp
                    v0 = float(512 * (kb - b))
                    k.dve.op(lambda e, o=rt[0][0:1, 0:8],
                             x=self.SL8[0][0:1, 8 * hp:8 * hp + 8], v=v0:
                             e.tensor_scalar(o, x, v, None, ALU.mult),
                             reads=[self.SL8[1]], writes=[rt[1]])
                    k.dve.op(lambda e, o=rt[0][0:1, 0:8], a=rt[0][0:1, 0:8],
                             b_=self.SLK8[0][0:1, 8 * hp:8 * hp + 8]:
                             e.tensor_tensor(o, a, b_, ALU.add),
                             reads=[self.SLK8[1], rt[1]], writes=[rt[1]])
                    k.dve.op(lambda e, o=rt[0][0:1, 0:8], a=rt[0][0:1, 0:8],
                             b_=self.crow8[0][0:1, 0:8]:
                             e.tensor_tensor(o, a, b_, ALU.subtract),
                             reads=[self.crow8[1], rt[1]], writes=[rt[1]])
                    self.mm(B0, B0[0][:, 0:8], self.iotar[0][:, :],
                            self.SL8[0][0:1, 8 * hp:8 * hp + 8],
                            [self.iotar[1], self.SL8[1]], start=True, stop=False)
                    self.mm(B0, B0[0][:, 0:8], self.c1[0][:, :], rt[0][0:1, 0:8],
                            [self.c1[1], rt[1]], start=False, stop=True)
                    bm = self.bm8[kb % 2]
                    k.act.op(lambda e, o=bm[0][:, :], x=B0[0][:, 0:8]:
                             e.activation(o, x, AF.Copy), reads=[B0[1]], writes=[bm[1]])

                def emit_sT(i):
                    kb, ko, h = steps[i]
                    kt = self.kt[kb % 2]
                    bk = STB[i % nstb]
                    ps = bk[0][:, 0:512]
                    klh = (kt[0][:, 512 * h + 128 * ko:512 * h + 128 * ko + 128]
                           if isa else kt[0][:, 128 * ko:128 * ko + 128])
                    self.mm(bk, ps, klh, qt[0][:, 512 * h:512 * (h + 1)],
                            [kt[1], qt[1]], start=True, stop=True)
                    pb = self.pbuf[i % nstb]
                    if isa:
                        k.act.op(lambda e, o=pb[0][:, :], x=ps,
                                 bb_=self.cbneg[0][:, h:h + 1]:
                                 e.activation(o, x, AF.Exp, bias=bb_),
                                 reads=[bk[1], self.cbneg[1]], writes=[pb[1]])
                    else:
                        sc = self.sscr[i % 3]
                        nj, njb = ((self.NEGJMt[0][:, 512 * ko:512 * (ko + 1)],
                                    self.NEGJMt[1]) if kb == b else
                                   (self.NEGJt[0][:, :], self.NEGJt[1]))
                        k.dve.op(lambda e, o=sc[0][:, :], i0=nj,
                                 s=self.slc[0][:, 2 * hp + h:2 * hp + h + 1], i1=ps:
                                 e.scalar_tensor_tensor(o, i0, s, i1,
                                                        ALU.mult, ALU.add),
                                 reads=[njb, self.slc[1], bk[1]],
                                 writes=[sc[1]])
                        bm = self.bm8[kb % 2]
                        k.act.op(lambda e, o=pb[0][:, :], x=sc[0][:, :],
                                 bb_=bm[0][:, 2 * ko + h:2 * ko + h + 1]:
                                 e.activation(o, x, AF.Exp, bias=bb_),
                                 reads=[sc[1], bm[1]], writes=[pb[1]])

                def consume(i):
                    kb, ko, h = steps[i]
                    vt = self.vt[kb % 2]
                    pb = self.pbuf[i % nstb]
                    if kb == b and isa:
                        pb2 = self.qkst[0]
                        k.dve.op(lambda e, o=pb2[0][:, :], a=pb[0][:, :],
                                 b_=self.PMt[0][:, 512 * ko:512 * (ko + 1)]:
                                 e.tensor_tensor(o, a, b_, ALU.mult),
                                 reads=[pb[1], self.PMt[1]], writes=[pb2[1]])
                        pu = pb2
                    else:
                        pu = pb
                    first = (i == 0)
                    last = (i >= nstep - 2)
                    self.mm(BL[h], BL[h][0][0:1, 0:512],
                            self.ones_b[0][:, :], pu[0][:, :],
                            [self.ones_b[1], pu[1]], start=first or i == 1,
                            stop=last)
                    vlh = vt[0][:, ew * ko + 128 * h:ew * ko + 128 * h + 128] \
                        if isa else vt[0][:, 128 * ko:128 * ko + 128]
                    self.mm(OT[h], OT[h][0][:, 0:512], vlh, pu[0][:, :],
                            [vt[1], pu[1]], start=first or i == 1, stop=last)

                def prep(j):
                    kb = steps[j][0]
                    if j == 0 or steps[j - 1][0] != kb:
                        load_kb(kb)
                        bias_kb(kb)
                    emit_sT(j)

                la = nstb - 1 if isa else 2
                for j in range(min(la, nstep)):
                    prep(j)
                for i in range(nstep):
                    if i + la < nstep:
                        prep(i + la)
                    consume(i)
                for h in range(2):
                    lr_ap = self.rows[0][0:1, 0:512]
                    rb_ = self.rows[1]
                    k.dve.op(lambda e, o=lr_ap, x=BL[h][0][0:1, 0:512]:
                             e.tensor_scalar(o, x, 1e-38, None, ALU.max),
                             reads=[BL[h][1]], writes=[rb_])
                    il_ap = self.rows[0][0:1, 512:1024]
                    ll_ap = self.rows[0][0:1, 1024:1536]
                    k.act.op(lambda e, o=ll_ap, x=lr_ap:
                             e.activation(o, x, AF.Ln), reads=[rb_], writes=[rb_])
                    k.act.op(lambda e, o=il_ap, x=ll_ap:
                             e.activation(o, x, AF.Exp, scale=-1.0),
                             reads=[rb_], writes=[rb_])
                    b1 = self.B[1]
                    self.mm(b1, b1[0][:, 0:512], self.c1[0][:, :], il_ap,
                            [self.c1[1], rb_], start=True, stop=True)
                    ib = self.invlb
                    k.act.op(lambda e, o=ib[0][:, :], x=b1[0][:, 0:512]:
                             e.activation(o, x, AF.Copy), reads=[b1[1]], writes=[ib[1]])
                    ot = self.otst[h]
                    k.dve.op(lambda e, o=ot[0][:, :], a=OT[h][0][:, 0:512],
                             b_=ib[0][:, :]: e.tensor_tensor(o, a, b_, ALU.mult),
                             reads=[OT[h][1], ib[1]], writes=[ot[1]])
                    row0 = 512 * dst_d + 128 * (2 * hp + h)
                    oshh = osh[half_b]
                    k.qs.op(lambda e, d=oshh[0].ap()[row0:row0 + 128, :],
                            s=ot[0][:, :]: e.dma_start(out=d, in_=s),
                            reads=[ot[1]], writes=[oshh[1]], slot=k.slot(ot[1]))
            if b in (7, 15):
                hb = 0 if b == 7 else 1
                for h_ in range(2):
                    sl = k.slot(self.otst[h_][1])
                    if sl.count:
                        k.qg._wait_emit(sl, sl.count)
                self.collective("AllToAll", osh[hb], og[hb])

    def load_OG(self, OG, g):
        k = self.k
        for j in range(4):
            if g == 0:
                xb = self.rhs2[j // 2]
            else:
                xb = self.xng[j // 2]
            d_ap = (xb[0][:, :].rearrange("p (c t) -> p c t", c=16)
                    [:, 8 * (j % 2):8 * (j % 2) + 8, :])
            dbuf = xb[1]
            OGh = OG[g]
            k.qs.op(lambda e, d=d_ap,
                    s=OGh[0].ap()[1024 * j:1024 * (j + 1), :]
                    .rearrange("(c p) t -> p c t", p=128):
                    e.dma_start(out=d, in_=s), reads=[OGh[1]], writes=[dbuf],
                    slot=k.slot(dbuf))

    def xn_chunk(self, g, c):
        xb = (self.rhs2 if g == 0 else self.xng)[c // 16]
        cc = c % 16
        return xb[0][:, 512 * cc:512 * (cc + 1)], xb[1]

    def router_and_h(self, g):
        k = self.k
        b6, b7 = self.B[6], self.B[7]
        for c in range(DCH):
            xa, xbuf = self.xn_chunk(g, c)
            self.mm(b6, b6[0][0:8, 0:512], self.wrssb[0][:, 8 * c:8 * (c + 1)],
                    xa, [self.wrssb[1], xbuf],
                    start=(c == 0), stop=(c == DCH - 1), inc=(c == DCH - 1))
        lg = self.lg
        k.dve.op(lambda e, o=lg[0][:, :], x=b6[0][0:8, 0:512]:
                 e.tensor_copy(o, x), reads=[b6[1]], writes=[lg[1]])
        for qs_ in range(4):
            self.transpose(b7, b7[0][0:128, 0:8], lg[0][0:8, 128 * qs_:128 * (qs_ + 1)],
                           [lg[1]], p=8)
            r0, r1, r2, r3, r4, r5 = [self.rsm[j] for j in range(6)]
            c0, c1_, c2, c3 = [self.rcol[j] for j in range(4)]
            k.dve.op(lambda e, o=r0[0][:, :], x=b7[0][0:128, 0:8]:
                     e.tensor_copy(o, x), reads=[b7[1]], writes=[r0[1]])
            k.dve.op(lambda e, o=c0[0][:, :], x=r0[0][:, :]:
                     e.reduce_max(o, x, AX.X, negate=True), reads=[r0[1]], writes=[c0[1]])
            k.act.op(lambda e, o=r1[0][:, :], x=r0[0][:, :], bb_=c0[0][:, :]:
                     e.activation(o, x, AF.Exp, bias=bb_),
                     reads=[r0[1], c0[1]], writes=[r1[1]])
            k.dve.op(lambda e, o=c1_[0][:, :], x=r1[0][:, :]:
                     e.reduce_max(o, x, AX.X), reads=[r1[1]], writes=[c1_[1]])
            k.dve.op(lambda e, o=r2[0][:, :], x=r1[0][:, :], s=c1_[0][:, :]:
                     e.tensor_scalar(o, x, s, None, ALU.is_ge),
                     reads=[r1[1], c1_[1]], writes=[r2[1]])
            k.dve.op(lambda e, o=r3[0][:, :], i0=r2[0][:, :], i1=r1[0][:, :]:
                     e.scalar_tensor_tensor(o, i0, -1e30, i1, ALU.mult, ALU.add),
                     reads=[r2[1], r1[1]], writes=[r3[1]])
            k.dve.op(lambda e, o=c2[0][:, :], x=r3[0][:, :]:
                     e.reduce_max(o, x, AX.X), reads=[r3[1]], writes=[c2[1]])
            k.dve.op(lambda e, o=r4[0][:, :], x=r1[0][:, :], s=c2[0][:, :]:
                     e.tensor_scalar(o, x, s, None, ALU.is_ge),
                     reads=[r1[1], c2[1]], writes=[r4[1]])
            k.dve.op(lambda e, o=r5[0][:, :], a=r1[0][:, :], b_=r4[0][:, :]:
                     e.tensor_tensor(o, a, b_, ALU.mult),
                     reads=[r1[1], r4[1]], writes=[r5[1]])
            k.dve.op(lambda e, o=c3[0][:, :], a=c1_[0][:, :], b_=c2[0][:, :]:
                     e.tensor_tensor(o, a, b_, ALU.add),
                     reads=[c1_[1], c2[1]], writes=[c3[1]])
            k.dve.op(lambda e, o=c0[0][:, :], x=c3[0][:, :]:
                     e.reciprocal(o, x), reads=[c3[1]], writes=[c0[1]])
            k.dve.op(lambda e, o=r0[0][:, :], x=r5[0][:, :], s=c0[0][:, :]:
                     e.tensor_scalar(o, x, s, 2.0, ALU.mult, ALU.mult),
                     reads=[r5[1], c0[1]], writes=[r0[1]])
            self.transpose(b7, b7[0][0:8, 0:128], r0[0][:, 0:8], [r0[1]])
            k.dve.op(lambda e, o=self.wTar[0][0:8, 128 * qs_:128 * (qs_ + 1)],
                     x=b7[0][0:8, 0:128]: e.tensor_copy(o, x),
                     reads=[b7[1]], writes=[self.wTar[1]])
        self.mm(b6, b6[0][:, 0:512], self.REPsb[0][:, :], self.wTar[0][0:8, 0:512],
                [self.REPsb[1], self.wTar[1]], start=True, stop=True)
        wt = self.wt16sb
        k.act.op(lambda e, o=wt[0][:, :], x=b6[0][:, 0:512]:
                 e.activation(o, x, AF.Copy), reads=[b6[1]], writes=[wt[1]])
        src = self.AallT[0].ap().rearrange("(c p) o -> p c o", p=128)
        bi = self.wsl[0]
        for j in range(4):
            k.qg.op(lambda e,
                    d=bi[0][:, :].rearrange("p (c o) -> p c o", c=DCH)
                    [:, 8 * j:8 * (j + 1), 0:128],
                    s=src[:, 8 * j:8 * (j + 1), :]:
                    e.dma_start(out=d, in_=s), reads=[self.AallT[1]], writes=[bi[1]],
                    slot=k.slot(bi[1]))
        for c in range(DCH):
            xa, xbuf = self.xn_chunk(g, c)
            self.mm(b7, b7[0][:, 0:512], bi[0][:, 256 * c:256 * c + 128],
                    xa, [bi[1], xbuf], start=(c == 0), stop=(c == DCH - 1),
                    inc=(c == DCH - 1))
        k.dve.op(lambda e, o=self.hwT[0][:, 512 * g:512 * (g + 1)],
                 a=b7[0][:, 0:512], b_=wt[0][:, :]:
                 e.tensor_tensor(o, a, b_, ALU.mult),
                 reads=[b7[1], wt[1]], writes=[self.hwT[1]])

    def big_proj(self, wt, resid, ydst, lora, stats, bias):
        k = self.k
        self.load_wslice(wt, 0, 256, [0], cast=True)
        for s in range(16):
            wb = self.wsl[s % 2]
            if s + 1 < 16:
                self.load_wslice(wt, 256 * (s + 1), 256, [(s + 1) % 2], cast=True)
            for oc in range(2):
                o2 = 2 * s + oc
                for g in range(2):
                    bk = self.B[[0, 1, 6, 7][2 * oc + g]]
                    ps = bk[0][:, 0:512]
                    for c in range(DCH):
                        last = (c == DCH - 1 and not lora)
                        xa, xbuf = self.xn_chunk(g, c)
                        self.mm(bk, ps,
                                wb[0][:, 256 * c + 128 * oc:256 * c + 128 * oc + 128],
                                xa, [wb[1], xbuf], start=(c == 0),
                                stop=last, inc=last)
                    if lora:
                        self.mm(bk, ps, self.Ballsb[0][:, 128 * o2:128 * (o2 + 1)],
                                self.hwT[0][:, 512 * g:512 * (g + 1)],
                                [self.Ballsb[1], self.hwT[1]], start=False, stop=True)
                    i = (2 * oc + g) % 2
                    rt = self.rtile[i]
                    k.qs.op(lambda e, d=rt[0][:, :],
                            s=resid[0].ap()[128 * o2:128 * (o2 + 1), 512 * g:512 * (g + 1)]:
                            e.dma_start(out=d, in_=s), reads=[resid[1]], writes=[rt[1]],
                            slot=k.slot(rt[1]))
                    yt = self.ytile[i]
                    bcol = self.bb[0][:, o2:o2 + 1] if bias else 0.0
                    breads = [bk[1], rt[1]] + ([self.bb[1]] if bias else [])
                    k.dve.op(lambda e, o=yt[0][:, :], i0=ps, s=bcol, i1=rt[0][:, :]:
                             e.scalar_tensor_tensor(o, i0, s, i1, ALU.add, ALU.add),
                             reads=breads, writes=[yt[1]])
                    k.qs.op(lambda e, d=ydst[0].ap()[128 * o2:128 * (o2 + 1),
                                                     512 * g:512 * (g + 1)],
                            s=yt[0][:, :]: e.dma_start(out=d, in_=s),
                            reads=[yt[1]], writes=[ydst[1]], slot=k.slot(yt[1]))
                    if stats:
                        yc = self.yc[i]
                        k.act.op(lambda e, o=yc[0][:, :], x=yt[0][:, :]:
                                 e.activation(o, x, AF.Copy), reads=[yt[1]], writes=[yc[1]])
                        sqc = self.sqc[i]
                        k.act.op(lambda e, o=sqc[0][:, :], x=yc[0][:, :]:
                                 e.activation(o, x, AF.Square), reads=[yc[1]], writes=[sqc[1]])
                        bS, bQ = self.B[2 + g], self.B[4 + g]
                        self.mm(bS, bS[0][0:1, 0:512], self.ones_b[0][:, :], yc[0][:, :],
                                [self.ones_b[1], yc[1]], start=(o2 == 0), stop=(o2 == 31))
                        self.mm(bQ, bQ[0][0:1, 0:512], self.ones_b[0][:, :], sqc[0][:, :],
                                [self.ones_b[1], sqc[1]], start=(o2 == 0), stop=(o2 == 31))

    def mm_fixup(self):
        pass

    def build(self):
        k = self.k
        self.load_consts()
        self.ln_stats_scan(self.xT)
        self.ln_rows_bcast()
        self.ln_norm_to_shard(self.xT, self.xnsh1, self.XN1)
        self.qkv_scan(self.XN1, [
            dict(wt=self.wqTs, ncols=NHL * HD, kind="rope", dst=self.qa,
                 rms_qscale=True, nrm_w=None, knorm=self.qna, wb=[0, 1]),
            dict(wt=self.wkTs, ncols=NHL * HD, kind="rope", dst=self.ka,
                 rms_qscale=False, nrm_w=None, knorm=self.kna,
                 wb=[self.rhs2[0], self.rhs2[1]]),
        ])
        self.sqrt_row(self.qna)
        self.sqrt_row(self.kna)
        self.qkv_scan(self.XN1, [], vstream=dict(
            wt=self.wvTs, ncols=NHL * HD, dst=self.va, wb=[0, 1]))
        self.flash("a")
        last = STAGES == 1
        y1dst = self.yout if last else self.y1
        self.load_OG(self.OG_a, 0)
        self.load_OG(self.OG_a, 1)
        self.big_proj(self.woT, self.xT, y1dst, lora=False,
                      stats=not last, bias=False)
        if last:
            return
        self.ln_rows_bcast()
        last = STAGES == 2
        y2dst = self.yout if last else self.y2
        self.ln_norm_to_half(y1dst, 0)
        self.ln_norm_to_half(y1dst, 1)
        self.router_and_h(0)
        self.router_and_h(1)
        self.big_proj(self.WbT, y1dst, y2dst, lora=True,
                      stats=not last, bias=True)
        if last:
            return
        self.ln_rows_bcast()
        self.ln_norm_to_shard(y2dst, self.xnsh3, self.XN3)
        self.qkv_scan(self.XN3, [
            dict(wt=self.bWqTs, ncols=NHL * HD, kind="rms", dst=self.qb,
                 rms_qscale=True, nrm_w=self.qnwsb, knorm=self.qna, wb=[0, 1]),
            dict(wt=self.bWkTs, ncols=HD, kind="rms", dst=self.kb,
                 rms_qscale=False, nrm_w=self.knwsb, knorm=self.kna,
                 wb=[self.rhs2[0]]),
        ], vstream=dict(wt=self.bWvTs, ncols=HD, dst=self.vb, wb=[self.rhs2[1]]))
        self.sqrt_row(self.qna)
        self.sqrt_row(self.kna, n=32)
        self.flash("b")
        last = STAGES == 3
        y3dst = self.yout if last else self.y3
        self.load_OG(self.OG_b, 0)
        self.load_OG(self.OG_b, 1)
        self.big_proj(self.bWoT, y2dst, y3dst, lora=False,
                      stats=not last, bias=False)
        if last:
            return
        self.ln_rows_bcast()
        self.ln_norm_to_half(y3dst, 0)
        self.ln_norm_to_half(y3dst, 1)
        self.router_and_h(0)
        self.router_and_h(1)
        self.big_proj(self.WbT, y3dst, self.yout, lora=True,
                      stats=False, bias=True)


def build_program():
    os.environ["TRN_ALLOW_ICI_ALL_TO_ALL"] = "1"
    k = BK()
    p = Prog(k)
    p.build()
    q = k.queues["qs"]
    for s in k.slotsems.values():
        if s.count:
            q.items.append(("w", s, s.count))
    for s in k.ccs:
        q.items.append(("w", s, s.count))
    nc = k.finish()
    return nc


_NC_CACHE = None


def kernel(**inputs):
    global _NC_CACHE
    os.environ["TRN_ALLOW_ICI_ALL_TO_ALL"] = "1"
    if _NC_CACHE is None:
        _NC_CACHE = build_program()
    nc = _NC_CACHE
    in_maps = [host_inputs(c, inputs) for c in range(NCORE)]
    REP = np.zeros((NE, 128), np.float32)
    for e in range(NE):
        REP[e, e * LR:(e + 1) * LR] = 1.0
    cosbc = np.broadcast_to(_CONSTS["cosrow"], (128, T)).copy()
    sinbc = np.broadcast_to(_CONSTS["sinrow"], (128, T)).copy()
    for m in in_maps:
        m["REP_in"] = REP
        m["cosbc"] = cosbc
        m["sinbc"] = sinbc
        m["idn_in"] = m.pop("idn")
        m["jT_in"] = m.pop("jT")
        m["AM_in"] = m.pop("AM")
        m["PM_in"] = m.pop("PM")
        m["NEGJ_in"] = m.pop("NEGJ")
        m["NEGJM_in"] = m.pop("NEGJM")
        m["SL8_in"] = m.pop("SL8")
        m["SLK8_in"] = m.pop("SLK8")
        m["onescol_in"] = m.pop("onescol")
        m.pop("cosrow"), m.pop("sinrow"), m.pop("colscale")
    res = run_bass_kernel_spmd(nc, in_maps, core_ids=list(range(NCORE)))
    return reassemble([r["yout"] for r in res.results])
